# version 03
# baseline (speedup 1.0000x reference)
import math
import numpy as np

import concourse.bass as bass
import concourse.mybir as mybir
from concourse.bass_utils import run_bass_kernel_spmd

F32 = mybir.dt.float32
BF16 = mybir.dt.bfloat16
I32 = mybir.dt.int32
AF = mybir.ActivationFunctionType
ALU = mybir.AluOpType
AX = mybir.AxisListType

D = 1024
T = 2048
DT = D // 128
TT = T // 128
NCORES = 8
LN_EPS = 1e-5

ENGINES = ("tensor", "vector", "scalar", "gpsimd", "sync")



def _wb_table():
    t = []
    def add(key, shape):
        t.append((key, tuple(shape)))
    for e in range(7):
        add(("L1wg", e), (32, 8, 128, 128))
        add(("L1wu", e), (32, 8, 128, 128))
        add(("L1wd", e), (4096, D))
    add(("L1swg", 0), (32, 8, 128, 128))
    add(("L1swu", 0), (32, 8, 128, 128))
    add(("L1swd", 0), (4096, D))
    add(("L3w1a", 0), (8, 512, 2048))
    add(("L3w1b", 0), (8, 512, 2048))
    add(("L3w2a", 0), (8, 1024, D))
    add(("L3w2b", 0), (8, 1024, D))
    add(("L3wo", 0), (D, D))
    add(("L3wp1", 0), (D, 128))
    add(("L3wp2", 0), (128, 8))
    add(("L0wrvkT", 0), (D, 3 * D))
    add(("L0woutT", 0), (D, D))
    add(("L0qprojT", 0), (D, 128))
    add(("L0routerT", 0), (128, 48))
    add(("L0auxkT", 0), (128, 4096))
    add(("L0auxv", 0), (4096, D))
    add(("L0uncw1T", 0), (D, 256))
    add(("L0maskrow", 0), (1, 4096))
    add(("L0ones", 0), (1, T))
    add(("L0uncw2T", 0), (256, 1))
    add(("L1wqkvT", 0), (D, 3 * D))
    add(("L1woutT", 0), (D, D))
    add(("L1routerT", 0), (D, 7))
    add(("L2wqkvT", 0), (D, 3 * D))
    add(("L2woutT", 0), (D, D))
    add(("L2wgate", 0), (D, 8))
    add(("L2wd", 0), (D, 512))
    add(("L2wu", 0), (512, D))
    add(("L2bu", 0), (8, D))
    add(("L3wq", 0), (D, D))
    add(("L3wk", 0), (D, D))
    add(("L3wv", 0), (D, D))
    nbins = 2 * NCORES
    fills = [0] * nbins
    binof = {}
    for key, shape in sorted(t, key=lambda kv: -int(np.prod(kv[1]))):
        n = (int(np.prod(shape)) + 31) // 32 * 32
        j = min(range(nbins), key=lambda i: (fills[i], i))
        binof[key] = (j, fills[j])
        fills[j] += n
    hsh = (max(fills) + 511) // 512 * 512
    offsets = {}
    for key, shape in t:
        j, off = binof[key]
        offsets[key] = (j % 2, (j // 2) * hsh + off)
    return dict(t), offsets, hsh

WB_SHAPES, WB_OFFSETS, WB_HSH = _wb_table()
WB_SHARD = 2 * WB_HSH
WB_HALF_TOTAL = NCORES * WB_HSH


def host_pack_wb(inp):
    img = np.zeros((2, WB_HALF_TOTAL), dtype=np.float32)
    def put(key, arr):
        a = np.ascontiguousarray(arr, dtype=np.float32)
        assert a.shape == WB_SHAPES[key], (key, a.shape, WB_SHAPES[key])
        h, o = WB_OFFSETS[key]
        img[h, o:o + a.size] = a.reshape(-1)
    def tile_gu(w):
        return w.reshape(8, 128, 32, 128).transpose(2, 0, 1, 3)
    for e in range(7):
        put(("L1wg", e), tile_gu(inp["L1_ffn_w_gate"][e]))
        put(("L1wu", e), tile_gu(inp["L1_ffn_w_up"][e]))
        put(("L1wd", e), inp["L1_ffn_w_down"][e])
    put(("L1swg", 0), tile_gu(inp["L1_ffn_sw_gate"][0]))
    put(("L1swu", 0), tile_gu(inp["L1_ffn_sw_up"][0]))
    put(("L1swd", 0), inp["L1_ffn_sw_down"][0])
    put(("L3w1a", 0), inp["L3_ffn_W1"][:, 0:512, :])
    put(("L3w1b", 0), inp["L3_ffn_W1"][:, 512:1024, :])
    put(("L3w2a", 0), inp["L3_ffn_W2"][:, 0:1024, :])
    put(("L3w2b", 0), inp["L3_ffn_W2"][:, 1024:2048, :])
    put(("L3wo", 0), inp["L3_ffn_Wo"])
    put(("L3wp1", 0), inp["L3_ffn_Wp1"])
    put(("L3wp2", 0), inp["L3_ffn_Wp2"])
    wrvkT = np.ascontiguousarray(inp["L0_attn_W_rvk"], dtype=np.float32).T
    g = np.empty_like(wrvkT)
    for ct in range(8):
        g[:, ct * 384 + 0:ct * 384 + 128] = wrvkT[:, 0 * 1024 + ct * 128:0 * 1024 + (ct + 1) * 128]
        g[:, ct * 384 + 128:ct * 384 + 256] = wrvkT[:, 1 * 1024 + ct * 128:1 * 1024 + (ct + 1) * 128]
        g[:, ct * 384 + 256:ct * 384 + 384] = wrvkT[:, 2 * 1024 + ct * 128:2 * 1024 + (ct + 1) * 128]
    put(("L0wrvkT", 0), g)
    put(("L0woutT", 0), inp["L0_attn_W_out"].T)
    put(("L0qprojT", 0), inp["L0_ffn_q_proj_w"].T)
    put(("L0routerT", 0), inp["L0_ffn_router_w"].T)
    put(("L0auxkT", 0), inp["L0_ffn_aux_keys"].T)
    put(("L0auxv", 0), inp["L0_ffn_aux_values"])
    put(("L0uncw1T", 0), inp["L0_ffn_unc_w1"].T)
    put(("L0maskrow", 0), inp["L0_ffn_reliability_mask"].reshape(1, 4096))
    put(("L0ones", 0), np.ones((1, T), np.float32))
    put(("L0uncw2T", 0), inp["L0_ffn_unc_w2"].T)
    wqt = np.ascontiguousarray(inp["L1_attn_w_qkv"], np.float32).T
    halfA = np.concatenate([wqt[:, 0:512], wqt[:, 1024:1536], wqt[:, 2048:2560]], 1)
    halfB = np.concatenate([wqt[:, 512:1024], wqt[:, 1536:2048], wqt[:, 2560:3072]], 1)
    put(("L1wqkvT", 0), np.concatenate([halfA, halfB], 1))
    put(("L1woutT", 0), inp["L1_attn_w_out"].T)
    put(("L1routerT", 0), inp["L1_ffn_router_w"].T)
    w2t = np.ascontiguousarray(inp["L2_attn_w_qkv"], np.float32).T
    w2v = w2t.reshape(1024, 16, 3, 64)
    w2p = np.concatenate([w2v[:, :, 0], w2v[:, :, 1], w2v[:, :, 2]],
                         axis=1).reshape(1024, 3072)
    put(("L2wqkvT", 0), w2p)
    put(("L2woutT", 0), inp["L2_attn_w_out"].T)
    put(("L2wgate", 0), inp["L2_ffn_w_gate"])
    put(("L2wd", 0), inp["L2_ffn_Wd"].transpose(1, 0, 2).reshape(D, 512))
    put(("L2wu", 0), inp["L2_ffn_Wu"].reshape(512, D))
    put(("L2bu", 0), inp["L2_ffn_bu"])
    put(("L3wq", 0), inp["L3_attn_WQ"])
    put(("L3wk", 0), inp["L3_attn_WK"])
    put(("L3wv", 0), inp["L3_attn_WV"])
    return img


FV_COLS = 66

def host_pack_fvec(inp):
    f = np.zeros((128, FV_COLS), dtype=np.float32)
    def colmaj(v):
        return np.asarray(v, np.float32).reshape(-1, 128).T
    f[:, 0:8] = colmaj(inp["L0_attn_time_decay"])
    f[:, 8:16] = colmaj(inp["L0_attn_time_first"])
    f[:, 16:18] = colmaj(inp["L0_ffn_unc_b1"])
    f[:, 18:26] = colmaj(inp["L1_attn_b_out"])
    f[:, 26:34] = colmaj(inp["L2_attn_ln_w"])
    f[:, 34:42] = colmaj(inp["L2_attn_ln_b"])
    f[:, 42:46] = colmaj(inp["L2_ffn_bd"].reshape(-1))
    f[:, 46:47] = colmaj(inp["L3_ffn_bp1"])
    f[:, 47:55] = colmaj(inp["L3_ffn_bo"])
    f[0, 55] = np.float32(inp["L0_ffn_unc_b2"].reshape(-1)[0])
    f[0, 56] = np.float32(np.asarray(inp["L0_ffn_gate_w1"]).reshape(-1)[0])
    f[0, 57] = np.float32(np.asarray(inp["L0_ffn_gate_bias"]).reshape(-1)[0])
    f[0, 58:66] = np.asarray(inp["L3_ffn_bp2"], np.float32).reshape(-1)
    return f


class Prog:

    def __init__(self, nc):
        self.nc = nc
        self.ops = {e: [] for e in ENGINES}
        self.sem_handles = {}
        self.counts = {}
        self.dma_ix = {}

    def add_sem(self, name, handle):
        self.sem_handles[name] = handle
        self.counts[name] = 0

    AUTO_SERIAL = ("vector", "scalar", "gpsimd")

    def on(self, engine, fn, inc=None, inc_by=1, serial=None):
        if inc is None:
            inc = self.ESEM[engine]
        sem = self.sem_handles[inc]
        if serial is None:
            serial = engine in self.AUTO_SERIAL
        pre = None
        if serial:
            own = self.ESEM[engine]
            ov = self.counts[own]
            if ov > 0:
                pre = (self.sem_handles[own], ov)
        def wrapped(eng, fn=fn, sem=sem, n=inc_by, pre=pre):
            if pre is not None:
                eng.wait_ge(pre[0], pre[1])
            fn(eng).then_inc(sem, n)
        self.ops[engine].append(wrapped)
        self.counts[inc] += inc_by
        return self.counts[inc]

    def wait(self, engine, sem_name, val=None):
        if val is None:
            val = self.counts[sem_name]
        assert val <= self.counts[sem_name], (sem_name, val, self.counts[sem_name])
        if val <= 0:
            return
        sem = self.sem_handles[sem_name]
        self.ops[engine].append(lambda eng, s=sem, v=val: eng.wait_ge(s, v))

    ESEM = {"tensor": "T", "vector": "V", "scalar": "S", "gpsimd": "G", "sync": "Y"}

    def softsync(self, engines=("tensor", "vector", "scalar")):
        self.flush("tensor")
        cur = {e: self.counts[self.ESEM[e]] for e in engines}
        for e in engines:
            for e2 in engines:
                if e2 != e and cur[e2] > 0:
                    self.wait(e, self.ESEM[e2], cur[e2])

    def flush(self, engine):
        s = self.ESEM[engine]
        self.wait(engine, s, self.counts[s])

    def mark(self, engine):
        s = self.ESEM[engine]
        v = self.on(engine, lambda eng: eng.nop(), inc=s)
        return (s, v)

    def barrier(self, engines=("tensor", "vector", "scalar", "gpsimd", "sync")):
        marks = [(e, self.mark(e)) for e in engines]
        for e in engines:
            for e2, (s, v) in marks:
                if e2 != e:
                    self.wait(e, s, v)

    DMA_R = 4

    def dma(self, queue, out, in_, transpose=False, slow_ok=False):
        if queue not in self.dma_ix:
            self.dma_ix[queue] = 0
        k = self.dma_ix[queue]
        self.dma_ix[queue] = k + 1
        qc = {"sync": "y", "scalar": "c", "gpsimd": "g", "vector": "v"}[queue]
        s = f"d{qc}{k % self.DMA_R}"
        sem = self.sem_handles[s]
        prev = (k // self.DMA_R) * 16
        def fn(eng, out=out, in_=in_, sem=sem, prev=prev, transpose=transpose,
               slow_ok=slow_ok):
            if prev > 0:
                eng.wait_ge(sem, prev)
            if transpose:
                eng.dma_start(out=out, in_=in_, transpose=True).then_inc(sem, 16)
            elif slow_ok:
                eng.dma_start(out=out, in_=in_,
                              allow_slow_non_contiguous=True).then_inc(sem, 16)
            else:
                eng.dma_start(out=out, in_=in_).then_inc(sem, 16)
        self.ops[queue].append(fn)
        self.counts[s] += 16
        return (s, prev + 16)

    def dma2(self, queue, mkfn):
        if queue not in self.dma_ix:
            self.dma_ix[queue] = 0
        k = self.dma_ix[queue]
        self.dma_ix[queue] = k + 1
        qc = {"sync": "y", "scalar": "c", "gpsimd": "g", "vector": "v"}[queue]
        s = f"d{qc}{k % self.DMA_R}"
        sem = self.sem_handles[s]
        prev = (k // self.DMA_R) * 16
        def fn(eng, mkfn=mkfn, sem=sem, prev=prev):
            if prev > 0:
                eng.wait_ge(sem, prev)
            try:
                mkfn(eng).then_inc(sem, 16)
            except Exception:
                import traceback as _tb
                _tb.print_exc()
                raise
        self.ops[queue].append(fn)
        self.counts[s] += 16
        return (s, prev + 16)

    def wait_dma(self, engine, tokens):
        if isinstance(tokens, tuple):
            tokens = [tokens]
        best = {}
        for s, v in tokens:
            best[s] = max(best.get(s, 0), v)
        for s, v in best.items():
            self.wait(engine, s, v)



def emit_ln(p, c, src_tiles, dst_tiles, out_f32_tiles=None, want_var_row=False,
            sq_in_dst=False):
    ps_a, ps_b, ps_big = c["ps_a"], c["ps_b"], c["ps_big"]
    af, row = c["af"], c["row"]
    hftile = [af(dt) for dt in range(DT)]
    onescol = c["onesf"]
    onesrow = c["onesrow"]

    p.softsync()
    for ch in range(4):
        if ch >= 2:
            p.wait("tensor", "V", p.counts["V"] - (1 if ch == 2 else 0))
        for dt in range(DT):
            p.on("tensor", lambda t, ch=ch, dt=dt: t.matmul(
                ps_a[0:1, 0:512] if ch % 2 == 0 else ps_b[0:1, 0:512],
                onescol, src_tiles[dt][:, bass.ts(ch, 512)],
                start=(dt == 0), stop=(dt == DT - 1)), inc="T")
        tv = p.counts["T"]
        p.wait("vector", "T", tv)
        p.on("vector", lambda v, ch=ch: v.tensor_copy(
            row(0)[:, bass.ts(ch, 512)],
            ps_a[0:1, 0:512] if ch % 2 == 0 else ps_b[0:1, 0:512]), inc="V")
    p.flush("tensor")
    tsum1 = p.counts["T"]
    sqm = []
    sqt = dst_tiles if sq_in_dst else hftile
    for dt in range(DT):
        if dt == 0:
            p.wait("scalar", "T", tsum1)
        p.on("scalar", lambda s, dt=dt: s.activation(
            sqt[dt], src_tiles[dt], AF.Square), inc="S")
        sqm.append(p.counts["S"])
    p.on("vector", lambda v: v.tensor_scalar(
        row(0), row(0), -1.0 / D, None, ALU.mult), inc="V")
    p.on("vector", lambda v: v.tensor_tensor(
        row(1), row(0), row(0), ALU.mult), inc="V")
    vneg = p.counts["V"]
    for ch in range(4):
        for dt in range(DT):
            if ch == 0:
                p.wait("tensor", "S", sqm[dt])
            if dt == 0:
                p.wait("tensor", "V", vneg if ch < 2 else p.counts["V"])
            p.on("tensor", lambda t, ch=ch, dt=dt: t.matmul(
                ps_a[0:1, 0:512] if ch % 2 == 0 else ps_b[0:1, 0:512],
                c["onesb"] if sq_in_dst else onescol,
                sqt[dt][:, bass.ts(ch, 512)],
                start=(dt == 0), stop=(dt == DT - 1)), inc="T")
        tv = p.counts["T"]
        p.wait("vector", "T", tv)
        p.on("vector", lambda v, ch=ch: v.scalar_tensor_tensor(
            row(1)[:, bass.ts(ch, 512)],
            ps_a[0:1, 0:512] if ch % 2 == 0 else ps_b[0:1, 0:512],
            1.0 / D, row(1)[:, bass.ts(ch, 512)], ALU.mult, ALU.subtract),
            inc="V")
    if want_var_row:
        p.on("vector", lambda v: v.tensor_copy(c["var_row_dst"], row(1)), inc="V")
    vv = p.counts["V"]
    p.wait("scalar", "V", vv)
    p.on("scalar", lambda s: s.activation(
        row(1), row(1), AF.Sqrt, bias=c["epscol"][0:1, :]), inc="S")
    p.wait("vector", "S")
    p.on("vector", lambda v: v.reciprocal(row(1), row(1)), inc="V")
    vstat = p.counts["V"]

    p.wait("tensor", "V", vstat)
    for ch in range(4):
        p.on("tensor", lambda t, ch=ch: t.matmul(
            ps_big[:, bass.ts(ch, 512)], onesrow, row(0)[:, bass.ts(ch, 512)],
            start=True, stop=True), inc="T")
    trep = p.counts["T"]
    p.wait("vector", "T", trep)
    for dt in range(DT):
        p.on("vector", lambda v, dt=dt: v.tensor_tensor(
            hftile[dt], src_tiles[dt], ps_big[:, 0:T], ALU.add), inc="V")
    v1 = p.counts["V"]
    p.wait("tensor", "V", v1)
    for ch in range(4):
        p.on("tensor", lambda t, ch=ch: t.matmul(
            ps_big[:, bass.ts(ch, 512)], onesrow, row(1)[:, bass.ts(ch, 512)],
            start=True, stop=True), inc="T")
    trep2 = p.counts["T"]
    p.wait("vector", "T", trep2)
    for dt in range(DT):
        p.on("vector", lambda v, dt=dt: v.tensor_tensor(
            dst_tiles[dt], hftile[dt], ps_big[:, 0:T], ALU.mult), inc="V")
        if out_f32_tiles is not None:
            p.on("vector", lambda v, dt=dt: v.tensor_tensor(
                out_f32_tiles[dt], hftile[dt], ps_big[:, 0:T], ALU.mult),
                inc="V")
    p.mark("vector")


def build_nc(phases=("l0", "l1", "l2", "l3"), n_cores=NCORES):
    import contextlib
    nc = bass.Bass("TRN2", target_bir_lowering=False, debug=False)
    est = contextlib.ExitStack()

    io = {}
    io["x_in"] = nc.dram_tensor("x_in", [D, T], F32, kind="ExternalInput").ap()
    conv_n = WB_SHARD if n_cores > 1 else 2 * WB_HALF_TOTAL
    io["conv_src"] = nc.dram_tensor("conv_src", [conv_n], F32,
                                    kind="ExternalInput").ap()
    io["fvec"] = nc.dram_tensor("fvec", [128, FV_COLS], F32,
                                kind="ExternalInput").ap()
    io["cid"] = nc.dram_tensor("cid", [1], I32, kind="ExternalInput").ap()
    io["gidx"] = nc.dram_tensor("gidx", [2, 128], mybir.dt.uint32,
                                kind="ExternalInput").ap()
    io["b1"] = nc.dram_tensor("b1", [8, 128, 16], F32, kind="ExternalInput").ap()
    io["b2"] = nc.dram_tensor("b2", [8, 128, 8], F32, kind="ExternalInput").ap()
    io["out"] = nc.dram_tensor("out", [D, T], F32, kind="ExternalOutput").ap()

    wb_shard = nc.dram_tensor("wb_shard", [WB_SHARD], BF16).ap()
    wb_allA = nc.dram_tensor("wb_allA", [WB_HALF_TOTAL], BF16).ap()
    wb_allB = nc.dram_tensor("wb_allB", [WB_HALF_TOTAL], BF16).ap()
    ar_in = nc.dram_tensor("ar_in", [4160], F32).ap()
    grow = nc.dram_tensor("grow", [8, T], F32).ap()

    ar_out = nc.dram_tensor("ar_out", [4160], F32).ap()
    xg_in = nc.dram_tensor("xg_in", [D * T], F32).ap()
    xg_out = nc.dram_tensor("xg_out", [NCORES * D * T], F32).ap()

    def wb(key, e=0):
        k = key if isinstance(key, tuple) else (key, e)
        shape = WB_SHAPES[k]
        h, off = WB_OFFSETS[k]
        n = int(np.prod(shape))
        flat = (wb_allA if h == 0 else wb_allB)[off:off + n]
        names = " ".join(f"a{i}" for i in range(len(shape)))
        return flat.rearrange(f"({names}) -> {names}",
                              **{f"a{i}": s for i, s in enumerate(shape)})

    def sb(name, shape, dtype):
        return est.enter_context(nc.sbuf_tensor(name, shape, dtype))

    xs = sb("xs", [128, DT, T], F32)
    hs = sb("hs", [128, DT, T], BF16)
    arena = sb("arena", [128, 40960], BF16)
    wbuf = sb("wbuf", [128, 8192], BF16)
    stat = sb("stat", [128, 64], F32)
    cst = sb("cst", [128, 8], F32)
    cidt = sb("cidt", [1, 1], I32)
    idxt = sb("idxt", [1, 8], mybir.dt.uint32)
    gix = sb("gix", [128, 8], mybir.dt.uint32)
    cstb = sb("cstb", [128, 272], BF16)
    smal = sb("smal", [128, 1024], F32)
    idf = sb("idf", [128, 128], F32)
    arena_f32 = arena[:].bitcast(F32)

    def af(slot, n=1):
        return arena_f32[:, slot * T:(slot + n) * T]

    def ab(slot, n=1):
        return arena[:, slot * T:(slot + n) * T]

    def row(i):
        return arena_f32[0:1, (8 + i) * T:(9 + i) * T]
    hf = None

    ps_big = est.enter_context(nc.psum_tensor("ps_big", [128, 2048], F32))
    ps_a = est.enter_context(nc.psum_tensor("ps_a", [128, 512], F32))
    ps_b = est.enter_context(nc.psum_tensor("ps_b", [128, 512], F32))
    ps_c = est.enter_context(nc.psum_tensor("ps_c", [128, 512], F32))
    ps_d = est.enter_context(nc.psum_tensor("ps_d", [128, 512], F32))

    p = Prog(nc)
    semnames = ["T", "V", "S", "G", "Y", "cc"]
    for q in "gcvy":
        semnames += [f"d{q}{i}" for i in range(Prog.DMA_R)]
    for s in semnames:
        p.add_sem(s, est.enter_context(nc.semaphore(s)))

    def wbrows(key, width):
        h, off = WB_OFFSETS[key if isinstance(key, tuple) else (key, 0)]
        half = wb_allA if h == 0 else wb_allB
        n = (WB_HALF_TOTAL // width) * width
        return half[0:n].rearrange("(r w) -> r w", w=width), off

    c = dict(nc=nc, io=io, wb=wb, wbrows=wbrows,
             xs=xs, hs=hs, af=af, ab=ab, row=row,
             arena=arena, cidt=cidt, idxt=idxt, gix=gix,
             wbuf=wbuf, stat=stat, cst=cst, cstb=cstb, smal=smal,
             idf=idf, ps_big=ps_big, ps_a=ps_a, ps_b=ps_b, ps_c=ps_c, ps_d=ps_d,
             ar_in=ar_in, ar_out=ar_out, xg_in=xg_in, xg_out=xg_out, grow=grow,
             wb_shard=wb_shard, wb_allA=wb_allA, wb_allB=wb_allB,
             n_cores=n_cores)

    emit_prologue(p, c)

    xtiles = [xs[:, dt, :] for dt in range(DT)]
    c["xtiles"] = xtiles
    c["hstiles"] = [hs[:, dt, :] for dt in range(DT)]

    if "l0a" in phases:
        emit_l0(p, c, parts=("a",))
    elif "l0f" in phases:
        emit_l0(p, c, parts=("f1",))
    elif "l0" in phases:
        emit_l0(p, c)
    if "l1a" in phases:
        emit_l1(p, c, parts=("a",))
    elif "l1f" in phases:
        emit_l1(p, c, parts=("f1",))
    elif "l1" in phases:
        emit_l1(p, c)
    if "xag" in phases:
        emit_xallgather(p, c, to_seq=True)
        emit_xallgather(p, c, to_seq=False)
    elif "l2f" in phases:
        emit_l2(p, c, parts=("f1",))
    elif "l2at" in phases:
        emit_l2(p, c, parts=("a",))
    elif "l2" in phases:
        emit_l2(p, c)
    if "l3a" in phases:
        emit_l3(p, c, parts=("a",))
    elif "l3f" in phases:
        emit_l3(p, c, parts=("f1",))
    elif "l3" in phases:
        emit_l3(p, c)

    emit_epilogue(p, c)

    with nc.Block() as block:
        for ename in ENGINES:
            deco = getattr(block, ename)
            def make(ops):
                def _(eng):
                    for fn in ops:
                        fn(eng)
                return _
            deco(make(p.ops[ename]))
    est.close()
    return nc


def rowslice(c):
    return c["rows"][15:16, 0:128]


def emit_prologue(p, c):
    nc, io = c["nc"], c["io"]
    xs, cst, cstb, idf, smal = c["xs"], c["cst"], c["cstb"], c["idf"], c["smal"]
    af = c["af"]

    tk_x = p.dma("sync", xs[:], io["x_in"].rearrange("(a p) t -> p a t", p=128))
    tk_f = p.dma("sync", smal[:, 0:FV_COLS], io["fvec"])
    tk_c = p.dma("sync", c["cidt"][0:1, 0:1], io["cid"][None, :])
    tk_g = p.dma("gpsimd", c["gix"][:, 0:2], io["gidx"].rearrange("a p -> p a"),
                 slow_ok=True)
    t0, t1 = af(0), af(1)
    p.on("gpsimd", lambda g: g.iota(t0[:, 0:128], pattern=[[1, 128]],
                                    base=0, channel_multiplier=0,
                                    allow_small_or_imprecise_dtypes=True), inc="G")
    p.on("gpsimd", lambda g: g.iota(t1[:, 0:128], pattern=[[0, 128]],
                                    base=0, channel_multiplier=1,
                                    allow_small_or_imprecise_dtypes=True), inc="G")
    p.wait("vector", "G")
    p.on("vector", lambda v: v.tensor_tensor(idf[:], t0[:, 0:128],
                                             t1[:, 0:128], ALU.is_equal),
         inc="V")
    p.on("vector", lambda v: v.tensor_copy(cstb[:, 0:128], idf[:]), inc="V")
    p.on("vector", lambda v: v.memset(cst[:, 0:1], 1.0), inc="V")
    p.on("vector", lambda v: v.memset(cst[:, 1:2], LN_EPS), inc="V")
    p.on("vector", lambda v: v.memset(cst[:, 2:3], -1.0), inc="V")
    p.on("vector", lambda v: v.memset(cstb[:, 128:129], 1.0), inc="V")
    p.on("vector", lambda v: v.memset(cstb[0:1, 144:272], 1.0), inc="V")
    p.on("vector", lambda v: v.memset(c["smal"][0:1, 96:224], 1.0), inc="V")
    c["idb"] = cstb[:, 0:128]
    c["onesf"] = cst[:, 0:1]
    c["epscol"] = cst[:, 1:2]
    c["negone"] = cst[:, 2:3]
    c["onesb"] = cstb[:, 128:129]
    c["onesrowb"] = cstb[0:1, 144:272]
    c["onesrow"] = c["smal"][0:1, 96:224]
    c["idf"] = idf[:]

    CH = 1 << 20
    conv_tk = []
    if c["n_cores"] == 1:
        for h, dst in ((0, c["wb_allA"]), (1, c["wb_allB"])):
            n = WB_HALF_TOTAL
            for i in range((n + CH - 1) // CH):
                a, b = i * CH, min((i + 1) * CH, n)
                conv_tk.append(p.dma("gpsimd", dst[a:b],
                                     io["conv_src"][h * n + a:h * n + b]))
        p.wait_dma("gpsimd", conv_tk)
        m = p.mark("gpsimd")
        c["wb_ready_g"] = m
        _skip = True
    else:
        _skip = False
        nch = (WB_SHARD + CH - 1) // CH
        for i in range(nch):
            a, b = i * CH, min((i + 1) * CH, WB_SHARD)
            conv_tk.append(p.dma("gpsimd", c["wb_shard"][a:b],
                                 io["conv_src"][a:b]))
        p.wait_dma("gpsimd", conv_tk)
    if not _skip and c["n_cores"] > 1:
        p.on("gpsimd", lambda g: g.collective_compute(
            "AllGather", ALU.bypass,
            replica_groups=[list(range(c["n_cores"]))],
            ins=[c["wb_shard"][0:WB_HSH].opt()], outs=[c["wb_allA"].opt()]),
            inc="cc")
        p.on("gpsimd", lambda g: g.collective_compute(
            "AllGather", ALU.bypass,
            replica_groups=[list(range(c["n_cores"]))],
            ins=[c["wb_shard"][WB_HSH:].opt()], outs=[c["wb_allB"].opt()]),
            inc="cc")
        p.wait("gpsimd", "cc")
        c["wb_ready"] = ("cc", p.counts["cc"])
    if not _skip:
        m = p.mark("gpsimd")
        c["wb_ready_g"] = m
    p.wait_dma("sync", [tk_x, tk_f, tk_c, tk_g])
    sy, syv = p.mark("sync")
    vm = p.counts["V"]
    for e in ("tensor", "vector", "scalar"):
        p.wait(e, sy, syv)
        if e != "vector":
            p.wait(e, "V", vm)


def emit_epilogue(p, c):
    p.barrier()
    tk = p.dma("sync", c["io"]["out"].rearrange("(a p) t -> p a t", p=128),
               c["xs"][:])
    p.wait_dma("sync", tk)
    qeng = {"y": "sync", "c": "scalar", "g": "gpsimd", "v": "vector"}
    for s, v in sorted(p.counts.items()):
        if len(s) == 3 and s[0] == "d" and v > 0:
            p.wait(qeng[s[1]], s, v)


def emit_l0_attn(p, c):
    nc, wb, xs, hs = c["nc"], c["wb"], c["xs"], c["hs"]
    af, ab, row = c["af"], c["ab"], c["row"]
    wbuf, smal, stat = c["wbuf"], c["smal"], c["stat"]
    ps_a, ps_b = c["ps_a"], c["ps_b"]
    xt, ht = c["xtiles"], c["hstiles"]

    emit_ln(p, c, xt, ht)

    p.on("scalar", lambda s: s.activation(stat[:, 0:8], smal[:, 0:8], AF.Exp))
    p.on("scalar", lambda s: s.activation(stat[:, 0:8], stat[:, 0:8], AF.Exp,
                                          scale=c["negone"]))
    sdec = p.counts["S"]
    p.wait("vector", "S", sdec)

    wrvk = wb("L0wrvkT")
    wvb = wbuf[:, 0:6144].rearrange("p (b k f) -> p b k f", b=2, k=8)
    rsig = wbuf[:, 6144:8192]

    rwkv = [ab(i) for i in range(8)]
    fK, fV, fEK, fEKV, fEUK, fA, fB = 4, 5, 6, 7, 8, 9, 9

    p.wait("sync", c["wb_ready_g"][0], c["wb_ready_g"][1])
    p.flush("tensor")
    p.wait("sync", "T", p.counts["T"])
    last_mm_using = {}
    for ct in range(8):
        buf = ct % 2
        if ct >= 2:
            p.wait("sync", "T", last_mm_using[buf])
        tks = [p.dma("sync", wvb[:, buf, kt, :],
                     wrvk[bass.ts(kt, 128), bass.ts(ct, 384)])
               for kt in range(8)]
        p.wait_dma("tensor", tks)
        if ct == 0:
            p.wait("tensor", "V")
        for part, cols in (("v", 128), ("k", 256), ("r", 0)):
            for ch in range(4):
                pst = ps_a if ch % 2 == 0 else ps_b
                if (ct, part, ch) != (0, "v", 0):
                    p.wait("tensor", "V", p.counts["V"])
                    p.wait("tensor", "S", p.counts["S"])
                for kt in range(8):
                    p.on("tensor", lambda t, buf=buf, kt=kt, cols=cols, ch=ch,
                         pst=pst: t.matmul(
                             pst[:, 0:512],
                             wvb[:, buf, kt, cols:cols + 128],
                             ht[kt][:, bass.ts(ch, 512)],
                             start=(kt == 0), stop=(kt == 7)), inc="T")
                tv = p.counts["T"]
                if part == "v":
                    p.wait("vector", "T", tv)
                    p.on("vector", lambda v, ch=ch, pst=pst: v.tensor_copy(
                        af(5)[:, bass.ts(ch, 512)], pst[:, 0:512]))
                elif part == "k":
                    p.wait("vector", "T", tv)
                    p.on("vector", lambda v, ch=ch, pst=pst: v.tensor_copy(
                        af(4)[:, bass.ts(ch, 512)], pst[:, 0:512]))
                else:
                    p.wait("scalar", "T", tv)
                    p.on("scalar", lambda s, ch=ch, pst=pst: s.activation(
                        rsig[:, bass.ts(ch, 512)], pst[:, 0:512], AF.Sigmoid))
        last_mm_using[buf] = p.counts["T"]
        p.wait("scalar", "V")
        p.on("scalar", lambda s: s.activation(af(6), af(4), AF.Exp))
        p.on("scalar", lambda s, ct=ct: s.activation(
            af(8), af(4), AF.Exp, bias=smal[:, 8 + ct:9 + ct]))
        sk = p.counts["S"]
        p.wait("vector", "S", sk)
        p.on("vector", lambda v: v.tensor_tensor(af(7), af(6), af(5), ALU.mult))
        dbc = stat[:, ct:ct + 1].broadcast_to([128, T])
        p.on("vector", lambda v, dbc=dbc: v.tensor_tensor_scan(
            af(9), dbc, af(7), 0.0, ALU.mult, ALU.add))
        p.on("vector", lambda v, dbc=dbc: v.tensor_tensor_scan(
            af(4), dbc, af(6), 0.0, ALU.mult, ALU.add))
        p.on("vector", lambda v: v.tensor_tensor(af(5), af(8), af(5), ALU.mult))
        p.on("vector", lambda v: v.tensor_tensor(
            af(5)[:, 1:T], af(5)[:, 1:T], af(9)[:, 0:T - 1], ALU.add))
        p.on("vector", lambda v: v.tensor_copy(af(6)[:, 0:1], af(8)[:, 0:1]))
        p.on("vector", lambda v: v.tensor_tensor(
            af(6)[:, 1:T], af(8)[:, 1:T], af(4)[:, 0:T - 1], ALU.add))
        p.on("vector", lambda v: v.reciprocal(af(8), af(6)))
        p.on("vector", lambda v: v.tensor_tensor(af(5), af(5), af(8), ALU.mult))
        p.wait("vector", "S")
        p.on("vector", lambda v, ct=ct: v.tensor_tensor(
            rwkv[ct], af(5), rsig[:, 0:T], ALU.mult))
    vwkv = p.counts["V"]

    wout = wb("L0woutT")
    wob = wbuf[:].rearrange("p (b k f) -> p b k f", b=2, k=4)
    p.wait("sync", "T")
    p.wait("sync", "V")
    tks = [p.dma("sync", wob[:, half, kt, :],
                 wout[bass.ts(half * 4 + kt, 128), :])
           for half in range(2) for kt in range(4)]
    p.wait_dma("tensor", tks)
    p.wait("tensor", "V", vwkv)
    for mt in range(8):
        for ch in range(4):
            pst = ps_a if (mt * 4 + ch) % 2 == 0 else ps_b
            if (mt, ch) != (0, 0):
                p.wait("tensor", "V", p.counts["V"])
            for kt in range(8):
                p.on("tensor", lambda t, mt=mt, ch=ch, kt=kt, pst=pst: t.matmul(
                    pst[:, 0:512],
                    wob[:, kt // 4, kt % 4, bass.ts(mt, 128)],
                    ab(kt)[:, bass.ts(ch, 512)],
                    start=(kt == 0), stop=(kt == 7)), inc="T")
            tv = p.counts["T"]
            p.wait("vector", "T", tv)
            p.on("vector", lambda v, mt=mt, ch=ch, pst=pst: v.tensor_tensor(
                xt[mt][:, bass.ts(ch, 512)], xt[mt][:, bass.ts(ch, 512)],
                pst[:, 0:512], ALU.add))
    p.flush("tensor")
    p.mark("vector")


def emit_l0(p, c, parts=("f1", "a", "f2")):
    if "f1" in parts:
        emit_l0_ffn(p, c, 0)
    if "a" in parts:
        emit_l0_attn(p, c)
    if "f2" in parts:
        emit_l0_ffn(p, c, 1)


def emit_l0_ffn(p, c, slot):
    nc, wb, xs, hs = c["nc"], c["wb"], c["xs"], c["hs"]
    af, ab, row = c["af"], c["ab"], c["row"]
    wbuf, smal, stat, cst = c["wbuf"], c["smal"], c["stat"], c["cst"]
    ps_a, ps_b, ps_c, ps_d, ps_big = (c["ps_a"], c["ps_b"], c["ps_c"],
                                      c["ps_d"], c["ps_big"])
    xt, ht = c["xtiles"], c["hstiles"]
    onesb, onesf, onesrow = c["onesb"], c["onesf"], c["onesrow"]

    emit_ln(p, c, xt, ht)
    vln = p.counts["V"]

    for dt in range(DT):
        p.on("vector", lambda v, dt=dt: v.scalar_tensor_tensor(
            xt[dt], ht[dt], 0.5, xt[dt], ALU.mult, ALU.add))

    p.wait("scalar", "V", vln)
    for dt in range(DT):
        p.on("scalar", lambda s, dt=dt: s.activation(af(dt), ht[dt], AF.Square))
    ssq = p.counts["S"]
    p.wait("tensor", "V", vln)
    for tt in range(TT):
        for dt in range(DT):
            p.on("tensor", lambda t, tt=tt, dt=dt: t.matmul(
                ps_c[:, tt:tt + 1], ht[dt][:, bass.ts(tt, 128)], onesb,
                start=(tt == 0 and dt == 0), stop=(tt == TT - 1 and dt == DT - 1)),
                inc="T")
    p.wait("tensor", "S", ssq)
    for tt in range(TT):
        for dt in range(DT):
            p.on("tensor", lambda t, tt=tt, dt=dt: t.matmul(
                ps_d[:, tt:tt + 1], af(dt)[:, bass.ts(tt, 128)], onesf,
                start=(tt == 0 and dt == 0), stop=(tt == TT - 1 and dt == DT - 1)),
                inc="T")
    p.flush("tensor")
    tsum = p.counts["T"]
    p.wait("vector", "T", tsum)
    p.on("vector", lambda v: v.tensor_scalar(
        stat[:, 16:32], ps_c[:, 0:16], 1.0 / D, None, ALU.mult))
    p.on("vector", lambda v: v.tensor_tensor(
        stat[:, 48:64], stat[:, 16:32], stat[:, 16:32], ALU.mult))
    p.on("vector", lambda v: v.scalar_tensor_tensor(
        stat[:, 32:48], ps_d[:, 0:16], 1.0 / D, stat[:, 48:64], ALU.mult,
        ALU.subtract))
    vvar = p.counts["V"]
    p.wait("scalar", "V", vvar)
    p.on("scalar", lambda s: s.activation(
        stat[:, 48:64], stat[:, 32:48], AF.Ln, bias=onesf))
    slv = p.counts["S"]
    p.wait("vector", "S", slv)
    p.on("vector", lambda v: v.tensor_reduce(
        stat[:, 16:17], stat[:, 48:64], axis=AX.X, op=ALU.add))
    vred = p.counts["V"]
    p.wait("tensor", "V", vred)
    p.on("tensor", lambda t: t.matmul(ps_c[0:1, 32:33], stat[:, 16:17], onesf,
                                      start=True, stop=True), inc="T")
    p.flush("tensor")
    tps = p.counts["T"]
    p.wait("vector", "T", tps)
    p.on("vector", lambda v: v.tensor_copy(stat[0:1, 17:18], ps_c[0:1, 32:33]))
    vpart = p.counts["V"]
    p.wait("gpsimd", "V", vpart)
    ar_off = 4096 + slot
    tk = p.dma("gpsimd", c["ar_in"][ar_off:ar_off + 1], stat[0:1, 17:18])
    p.wait_dma("gpsimd", tk)
    if c["n_cores"] > 1:
        p.on("gpsimd", lambda g: g.collective_compute(
            "AllReduce", ALU.add, replica_groups=[list(range(c["n_cores"]))],
            ins=[c["ar_in"][ar_off:ar_off + 1].opt()],
            outs=[c["ar_out"][ar_off:ar_off + 1].opt()]), inc="cc")
        p.wait("gpsimd", "cc")
    else:
        tk = p.dma("gpsimd", c["ar_out"][ar_off:ar_off + 1],
                   c["ar_in"][ar_off:ar_off + 1])
        p.wait_dma("gpsimd", tk)
    tk_ar = p.dma("gpsimd", stat[0:1, 18:19], c["ar_out"][ar_off:ar_off + 1])
    garv = p.mark("gpsimd")

    p.wait("sync", c["wb_ready_g"][0], c["wb_ready_g"][1])
    p.flush("tensor")
    p.wait("sync", "T", p.counts["T"])
    w1t = wb("L0uncw1T")
    tk_w1 = [p.dma("sync", wbuf[:, bass.ts(kt, 256)], w1t[bass.ts(kt, 128), :])
             for kt in range(8)]
    w2t = wb("L0uncw2T")
    tk_w2 = [p.dma("sync", wbuf[:, 2048 + kt:2049 + kt],
                   w2t[bass.ts(kt, 128), :]) for kt in range(2)]
    qpt = wb("L0qprojT")
    tk_qp = [p.dma("sync", wbuf[:, 2304 + kt * 128:2304 + (kt + 1) * 128],
                   qpt[bass.ts(kt, 128), :]) for kt in range(8)]
    tk_rt = p.dma("sync", wbuf[:, 3328:3376], wb("L0routerT"))
    tk_ak = p.dma("sync", wbuf[:, 4096:8192], wb("L0auxkT"))
    p.wait("sync", "T", tsum)
    tk_on = p.dma("sync", ab(0)[48:49, :], wb("L0ones"))
    tk_mk = [p.dma("sync", ab(1 + i)[48:49, :],
                   wb("L0maskrow")[:, bass.ts(i, 2048)]) for i in range(2)]

    p.wait_dma("tensor", tk_w1)
    p.wait("tensor", "V", vvar)
    nch = 0
    for mt in range(2):
        for ch in range(4):
            pst = ps_a if nch % 2 == 0 else ps_b
            if nch >= 2:
                p.wait("tensor", "S", p.counts["S"])
            for kt in range(8):
                p.on("tensor", lambda t, mt=mt, ch=ch, kt=kt, pst=pst: t.matmul(
                    pst[:, 0:512],
                    wbuf[:, kt * 256 + mt * 128:kt * 256 + (mt + 1) * 128],
                    ht[kt][:, bass.ts(ch, 512)],
                    start=(kt == 0), stop=(kt == 7)), inc="T")
            p.flush("tensor")
            tv = p.counts["T"]
            p.wait("scalar", "T", tv)
            p.on("scalar", lambda s, mt=mt, ch=ch, pst=pst: s.activation(
                ab(4 + mt)[:, bass.ts(ch, 512)], pst[:, 0:512], AF.Gelu,
                bias=smal[:, 16 + mt:17 + mt]))
            nch += 1
    sh1 = p.counts["S"]
    p.wait("tensor", "S", sh1)
    p.wait_dma("tensor", tk_w2)
    for tt in range(TT):
        for kt in range(2):
            p.on("tensor", lambda t, tt=tt, kt=kt: t.matmul(
                ps_c[:, 16 + tt:17 + tt], ab(4 + kt)[:, bass.ts(tt, 128)],
                wbuf[:, 2048 + kt:2049 + kt],
                start=(tt == 0 and kt == 0), stop=(tt == TT - 1 and kt == 1)),
                inc="T")
    p.on("tensor", lambda t: t.matmul(ps_d[:, 0:1], onesrow,
                                      smal[0:1, 55:56], start=True, stop=True),
         inc="T")
    p.on("tensor", lambda t: t.matmul(ps_d[:, 1:2], onesrow,
                                      smal[0:1, 56:57], start=True, stop=True),
         inc="T")
    p.on("tensor", lambda t: t.matmul(ps_d[:, 2:3], onesrow,
                                      smal[0:1, 57:58], start=True, stop=True),
         inc="T")
    p.flush("tensor")
    tlearn = p.counts["T"]
    p.wait("vector", "T", tlearn)
    p.on("vector", lambda v: v.tensor_copy(stat[:, 44:47], ps_d[:, 0:3]))
    vb2 = p.counts["V"]
    p.wait("scalar", "V", vb2)
    p.on("scalar", lambda s: s.activation(
        stat[:, 0:16], ps_c[:, 16:32], AF.Sigmoid, bias=stat[:, 44:45]))
    ssig = p.counts["S"]

    p.wait_dma("tensor", tk_qp)
    p.wait("tensor", "S", ssig)
    for ch in range(4):
        pst = ps_a if ch % 2 == 0 else ps_b
        if ch >= 2:
            p.wait("tensor", "V", p.counts["V"])
        for kt in range(8):
            p.on("tensor", lambda t, ch=ch, kt=kt, pst=pst: t.matmul(
                pst[:, 0:512], wbuf[:, 2304 + kt * 128:2304 + (kt + 1) * 128],
                ht[kt][:, bass.ts(ch, 512)],
                start=(kt == 0), stop=(kt == 7)), inc="T")
        p.flush("tensor")
        tv = p.counts["T"]
        p.wait("vector", "T", tv)
        p.on("vector", lambda v, ch=ch, pst=pst: v.tensor_copy(
            ab(3)[:, bass.ts(ch, 512)], pst[:, 0:512]))
    vq = p.counts["V"]
    p.wait_dma("tensor", tk_rt)
    p.wait("tensor", "V", vq)
    for ch in range(4):
        pst = ps_a if ch % 2 == 0 else ps_b
        if ch >= 2:
            p.wait("tensor", "V", p.counts["V"])
        p.on("tensor", lambda t, ch=ch, pst=pst: t.matmul(
            pst[0:48, 0:512], wbuf[:, 3328:3376], ab(3)[:, bass.ts(ch, 512)],
            start=True, stop=True), inc="T")
        p.flush("tensor")
        tv = p.counts["T"]
        p.wait("vector", "T", tv)
        p.on("vector", lambda v, ch=ch, pst=pst: v.tensor_scalar(
            ab(0)[0:48, bass.ts(ch, 512)], pst[0:48, 0:512],
            0.14433756729740643, None, ALU.mult))
    p.wait_dma("tensor", tk_ak)
    for kb in range(8):
        pst = ps_a if kb % 2 == 0 else ps_b
        p.wait("tensor", "V", p.counts["V"])
        p.on("tensor", lambda t, kb=kb, pst=pst: t.matmul(
            pst[0:48, 0:512], wbuf[:, 3328:3376],
            wbuf[:, 4096 + kb * 512:4096 + (kb + 1) * 512],
            start=True, stop=True), inc="T")
        p.flush("tensor")
        tv = p.counts["T"]
        p.wait("vector", "T", tv)
        p.on("vector", lambda v, kb=kb, pst=pst: v.tensor_copy(
            ab(1)[0:48, bass.ts(kb, 512)] if kb < 4 else
            ab(2)[0:48, bass.ts(kb - 4, 512)], pst[0:48, 0:512]))
    vrk = p.counts["V"]
    t_wbuf_free = p.counts["T"]

    p.wait("vector", "G", garv[1])
    p.wait_dma("vector", tk_ar)
    p.on("vector", lambda v: v.tensor_scalar(
        stat[0:1, 18:19], stat[0:1, 18:19], 1.0 / 16384.0, 1e-6, ALU.mult,
        ALU.add))
    p.on("vector", lambda v: v.reciprocal(stat[0:1, 18:19], stat[0:1, 18:19]))
    vrg = p.counts["V"]
    p.wait("tensor", "V", vrg)
    p.on("tensor", lambda t: t.matmul(ps_d[:, 3:4], onesrow,
                                      stat[0:1, 18:19], start=True, stop=True),
         inc="T")
    p.flush("tensor")
    trg = p.counts["T"]
    p.wait("vector", "T", trg)
    p.on("vector", lambda v: v.tensor_copy(stat[:, 47:48], ps_d[:, 3:4]))
    p.on("vector", lambda v: v.scalar_tensor_tensor(
        stat[:, 16:32], stat[:, 48:64], 0.5,
        stat[:, 47:48].broadcast_to([128, 1, 16]).squeeze(1)
        if False else stat[:, 47:48].broadcast_to([128, 16]),
        ALU.mult, ALU.mult))
    p.on("vector", lambda v: v.scalar_tensor_tensor(
        stat[:, 16:32], stat[:, 0:16], 2.5, stat[:, 16:32], ALU.mult, ALU.add))
    p.on("vector", lambda v: v.tensor_scalar(
        stat[:, 16:32], stat[:, 16:32], 0.0, 5.0, ALU.max, ALU.min))
    vunc = p.counts["V"]
    p.wait("scalar", "V", vunc)
    p.on("scalar", lambda s: s.activation(
        stat[:, 0:16], stat[:, 16:32], AF.Sigmoid, bias=stat[:, 46:47],
        scale=stat[:, 45:46]))
    sgate = p.counts["S"]
    p.wait("vector", "S", sgate)

    p.wait_dma("tensor", [tk_on] + tk_mk)
    auxv = wb("L0auxv")
    wT = ab(12, 8).rearrange("p (k t) -> p k t", k=32)
    for g4 in range(4):
        for i4 in range(4):
            tt = g4 * 4 + i4
            for half in range(2):
                p.wait("tensor", "V", p.counts["V"])
                for kb4 in range(4):
                    kb = half * 4 + kb4
                    p.on("tensor", lambda t, tt=tt, kb=kb, kb4=kb4: t.matmul(
                        ps_big[:, bass.ts(kb4, 512)],
                        ab(0)[0:49, bass.ts(tt, 128)],
                        (ab(1) if kb < 4 else ab(2))[0:49, bass.ts(kb % 4, 512)],
                        start=True, stop=True), inc="T")
                p.flush("tensor")
                tv = p.counts["T"]
                p.wait("vector", "T", tv)
                p.on("vector", lambda v, half=half: v.tensor_copy(
                    af(6 + half), ps_big[:, 0:T]))
            p.on("vector", lambda v: v.max(stat[:, 32:40], af(6, 2)))
            p.on("vector", lambda v: v.tensor_scalar(
                stat[:, 40:41], stat[:, 32:33], -1.0, None, ALU.mult))
            ve = p.counts["V"]
            p.wait("scalar", "V", ve)
            p.on("scalar", lambda s: s.activation(af(8), af(6), AF.Exp,
                                                  bias=stat[:, 40:41]))
            p.on("scalar", lambda s: s.activation(af(9), af(7), AF.Exp,
                                                  bias=stat[:, 40:41]))
            se = p.counts["S"]
            p.wait("vector", "S", se)
            p.on("vector", lambda v: v.tensor_scalar(
                af(6), af(6), stat[:, 39:40], None, ALU.is_ge))
            p.on("vector", lambda v: v.tensor_scalar(
                af(7), af(7), stat[:, 39:40], None, ALU.is_ge))
            p.on("vector", lambda v: v.tensor_tensor(af(8), af(8), af(6),
                                                     ALU.mult))
            p.on("vector", lambda v: v.tensor_tensor(af(9), af(9), af(7),
                                                     ALU.mult))
            p.on("vector", lambda v: v.tensor_reduce(
                stat[:, 41:42], af(8, 2), axis=AX.X, op=ALU.add))
            p.on("vector", lambda v: v.reciprocal(stat[:, 41:42],
                                                  stat[:, 41:42]))
            p.on("vector", lambda v, tt=tt: v.scalar_tensor_tensor(
                stat[:, 42:43], stat[:, tt:tt + 1], 0.5, stat[:, 41:42],
                ALU.mult, ALU.mult))
            p.on("vector", lambda v, i4=i4: v.tensor_scalar(
                ab(4 + 2 * i4), af(8), stat[:, 42:43], None, ALU.mult))
            p.on("vector", lambda v, i4=i4: v.tensor_scalar(
                ab(5 + 2 * i4), af(9), stat[:, 42:43], None, ALU.mult))
        vw = p.counts["V"]
        p.wait("tensor", "V", vw)
        for i4 in range(4):
            for kt in range(32):
                pst = (ps_a if kt % 2 == 0 else ps_b)[:].bitcast(BF16)
                if not (i4 == 0 and kt < 2):
                    p.wait("tensor", "V", p.counts["V"] - 1)
                p.on("tensor", lambda t, i4=i4, kt=kt, pst=pst: t.transpose(
                    pst[:, 0:128], ab(4 + 2 * i4 + kt // 16)[:, bass.ts(kt % 16, 128)],
                    c["idb"]), inc="T")
                p.flush("tensor")
                tv = p.counts["T"]
                p.wait("vector", "T", tv)
                p.on("vector", lambda v, i4=i4, kt=kt, pst=pst: v.tensor_copy(
                    wT[:, kt, bass.ts(i4, 128)], pst[:, 0:128]))
        vtr = p.counts["V"]
        p.wait("tensor", "V", vtr)
        psum_of = [ps_big[:, 0:512], ps_big[:, 512:1024], ps_big[:, 1024:1536],
                   ps_big[:, 1536:2048], ps_a[:, 0:512], ps_b[:, 0:512],
                   ps_c[:, 0:512], ps_d[:, 0:512]]
        for ktg in range(4):
            if g4 == 0 and ktg == 0:
                p.wait("sync", "T", t_wbuf_free)
            else:
                p.flush("tensor")
                p.wait("sync", "T", p.counts["T"])
            tks = [p.dma("sync", wbuf[:, bass.ts(kt, 1024)],
                         auxv[bass.ts(ktg * 8 + kt, 128), :])
                   for kt in range(8)]
            p.wait_dma("tensor", tks)
            for dt in range(DT):
                for kt in range(8):
                    p.on("tensor", lambda t, dt=dt, kt=kt, ktg=ktg: t.matmul(
                        psum_of[dt],
                        wbuf[:, kt * 1024 + dt * 128:kt * 1024 + (dt + 1) * 128],
                        wT[:, ktg * 8 + kt, :],
                        start=(ktg == 0 and kt == 0 and (dt % 4 == 0 or True)),
                        stop=(ktg == 3 and kt == 7)), inc="T")
        p.flush("tensor")
        tagg = p.counts["T"]
        p.wait("vector", "T", tagg)
        for dt in range(DT):
            p.on("vector", lambda v, dt=dt, g4=g4: v.tensor_tensor(
                xt[dt][:, bass.ts(g4, 512)], xt[dt][:, bass.ts(g4, 512)],
                psum_of[dt], ALU.add))
        if g4 < 3:
            p.wait("tensor", "V", p.counts["V"])
    p.flush("tensor")
    p.mark("vector")


def emit_l1_ffn(p, c, slot):
    nc, wb, xs, hs = c["nc"], c["wb"], c["xs"], c["hs"]
    af, ab, row = c["af"], c["ab"], c["row"]
    wbuf, smal, stat, cst = c["wbuf"], c["smal"], c["stat"], c["cst"]
    ps_a, ps_b, ps_c, ps_d, ps_big = (c["ps_a"], c["ps_b"], c["ps_c"],
                                      c["ps_d"], c["ps_big"])
    xt, ht = c["xtiles"], c["hstiles"]
    onesb, onesf, onesrow = c["onesb"], c["onesf"], c["onesrow"]

    emit_ln(p, c, xt, ht)
    vln = p.counts["V"]

    p.wait("sync", c["wb_ready_g"][0], c["wb_ready_g"][1])
    p.flush("tensor")
    p.wait("sync", "T", p.counts["T"])
    tk_rt = p.dma("sync", wbuf[:, 0:56].rearrange("p (k f) -> p k f", k=8),
                  wb("L1routerT").rearrange("(k p) f -> p k f", p=128))
    p.on("vector", lambda v: v.memset(smal[:, 224:352], -1.0e38))
    vpad = p.counts["V"]
    p.wait_dma("tensor", tk_rt)
    p.wait("tensor", "V", vln)
    for tt in range(TT):
        for kt in range(8):
            p.on("tensor", lambda t, tt=tt, kt=kt: t.matmul(
                ps_c[:, tt * 7:(tt + 1) * 7], ht[kt][:, bass.ts(tt, 128)],
                wbuf[:, kt * 7:(kt + 1) * 7],
                start=(tt == 0 and kt == 0), stop=(tt == TT - 1 and kt == 7)),
                inc="T")
    p.flush("tensor")
    tlg = p.counts["T"]
    p.wait("vector", "T", tlg)
    p.wait("vector", "V", vpad)
    for tt in range(TT):
        p.on("vector", lambda v, tt=tt: v.tensor_copy(
            smal[:, 224 + tt * 8:224 + tt * 8 + 7],
            ps_c[:, tt * 7:(tt + 1) * 7]))
    for tt in range(TT):
        lg = smal[:, 224 + tt * 8:224 + tt * 8 + 8]
        p.on("vector", lambda v, lg=lg: v.max(stat[:, 0:8], lg))
        p.on("vector", lambda v: v.tensor_scalar(
            stat[:, 8:9], stat[:, 0:1], -1.0, None, ALU.mult))
        p.on("vector", lambda v, lg=lg: v.tensor_scalar(
            stat[:, 16:24], lg, stat[:, 1:2], None, ALU.is_ge))
        vge = p.counts["V"]
        p.wait("scalar", "V", vge)
        p.on("scalar", lambda s: s.activation(
            stat[:, 9:10], stat[:, 1:2], AF.Exp, bias=stat[:, 8:9]))
        p.on("scalar", lambda s, lg=lg: s.activation(
            stat[:, 24:32], lg, AF.Exp, bias=stat[:, 8:9]))
        sexp = p.counts["S"]
        p.wait("vector", "S", sexp)
        p.on("vector", lambda v: v.tensor_scalar(
            stat[:, 9:10], stat[:, 9:10], 1.0, None, ALU.add))
        p.on("vector", lambda v: v.reciprocal(stat[:, 9:10], stat[:, 9:10]))
        p.on("vector", lambda v: v.tensor_tensor(
            stat[:, 24:32], stat[:, 24:32], stat[:, 16:24], ALU.mult))
        p.on("vector", lambda v, lg=lg: v.tensor_scalar(
            lg, stat[:, 24:32], stat[:, 9:10], None, ALU.mult))
    vgates = p.counts["V"]
    p.wait("tensor", "V", vgates)
    prev_tk = None
    for e in range(7):
        gcol = smal[:, 224 + e:224 + e + 121:8].rearrange("p f -> p f")
        pst = ps_a if e % 2 == 0 else ps_b
        if e >= 2:
            p.wait("tensor", "V", p.counts["V"])
        p.on("tensor", lambda t, gcol=gcol, pst=pst: t.transpose(
            pst[0:16, 0:128], gcol, c["idf"]), inc="T")
        p.flush("tensor")
        tv = p.counts["T"]
        p.wait("vector", "T", tv)
        if prev_tk is not None:
            p.wait_dma("vector", prev_tk)
        p.on("vector", lambda v, pst=pst: v.tensor_copy(
            smal[0:16, 352:480], pst[0:16, 0:128]))
        vv = p.counts["V"]
        p.wait("gpsimd", "V", vv)
        prev_tk = p.dma("gpsimd", c["grow"][e:e + 1, :].rearrange(
            "a (b f) -> (a b) f", b=16), smal[0:16, 352:480])
    p.wait_dma("gpsimd", prev_tk)
    gmark = p.mark("gpsimd")

    wv = wbuf[:, 0:5120].rearrange("p (b f) -> p b f", b=2)
    hslot = lambda hq: c["arena"][:, hq * 512:(hq + 1) * 512]
    crep = af(8)[:, 0:512]
    gsus = ab(17)
    hshared = af(9)[:, 0:512]
    psum_mix = [ps_big[:, bass.ts(i, 512)] for i in range(4)]

    p.wait("sync", "G", gmark[1])
    for e in range(8):
        routed = e < 7
        wgt = wb("L1wg", e) if routed else wb("L1swg")
        wut = wb("L1wu", e) if routed else wb("L1swu")
        wdt = wb("L1wd", e) if routed else wb("L1swd")
        for c4 in range(4):
            if routed:
                p.flush("tensor")
                p.wait("sync", "T", p.counts["T"])
                tk = p.dma("sync", smal[0:1, 512:1024],
                           c["grow"][e:e + 1, bass.ts(c4, 512)])
                p.wait_dma("tensor", tk)
                p.wait("tensor", "V", p.counts["V"])
                p.on("tensor", lambda t: t.matmul(
                    ps_d[:, 0:512], onesrow, smal[0:1, 512:1024],
                    start=True, stop=True), inc="T")
                p.flush("tensor")
                tv = p.counts["T"]
                p.wait("vector", "T", tv)
                p.on("vector", lambda v: v.tensor_copy(crep, ps_d[:, 0:512]))
                vcrep = p.counts["V"]
                p.wait("tensor", "V", vcrep)
            for hq in range(32):
                buf = hq % 2
                p.flush("tensor")
                p.wait("sync", "T", p.counts["T"])
                tks = [p.dma("sync",
                             wv[:, buf, 0:1024].rearrange("p (a c) -> p a c", a=8),
                             wgt[hq].rearrange("a b c -> b a c")),
                       p.dma("sync",
                             wv[:, buf, 1024:2048].rearrange("p (a c) -> p a c", a=8),
                             wut[hq].rearrange("a b c -> b a c")),
                       p.dma("sync", wv[:, buf, 2048:2560],
                             wdt[bass.ts(hq, 128), 0:512])]
                p.wait_dma("tensor", tks)
                p.wait("tensor", "S", p.counts["S"])
                p.wait("tensor", "V", p.counts["V"])
                for kt in range(8):
                    p.on("tensor", lambda t, buf=buf, kt=kt, c4=c4: t.matmul(
                        ps_c[:, 0:512], wv[:, buf, kt * 128:(kt + 1) * 128],
                        ht[kt][:, bass.ts(c4, 512)],
                        start=(kt == 0), stop=(kt == 7)), inc="T")
                p.flush("tensor")
                tg = p.counts["T"]
                p.wait("scalar", "T", tg)
                p.on("scalar", lambda s: s.activation(
                    gsus[:, 0:512], ps_c[:, 0:512], AF.Silu))
                sgs = p.counts["S"]
                for kt in range(8):
                    p.on("tensor", lambda t, buf=buf, kt=kt, c4=c4: t.matmul(
                        ps_d[:, 0:512],
                        wv[:, buf, 1024 + kt * 128:1024 + (kt + 1) * 128],
                        ht[kt][:, bass.ts(c4, 512)],
                        start=(kt == 0), stop=(kt == 7)), inc="T")
                p.flush("tensor")
                tu = p.counts["T"]
                p.wait("vector", "T", tu)
                p.wait("vector", "S", sgs)
                if routed:
                    p.on("vector", lambda v: v.tensor_tensor(
                        gsus[:, 512:1024], ps_d[:, 0:512], crep, ALU.mult))
                else:
                    p.on("vector", lambda v: v.tensor_copy(
                        gsus[:, 512:1024], ps_d[:, 0:512]))
                p.on("vector", lambda v, hq=hq: v.tensor_tensor(
                    hslot(hq), gsus[:, 0:512], gsus[:, 512:1024], ALU.mult))
                vh = p.counts["V"]
                p.wait("tensor", "V", vh)
                for dt in range(4):
                    p.on("tensor", lambda t, buf=buf, dt=dt, hq=hq: t.matmul(
                        psum_mix[dt],
                        wv[:, buf, 2048 + dt * 128:2048 + (dt + 1) * 128],
                        hslot(hq), start=(hq == 0), stop=(hq == 31)),
                        inc="T")
            if True:
                p.flush("tensor")
                tmix = p.counts["T"]
                p.wait("vector", "T", tmix)
                for dt in range(4):
                    p.on("vector", lambda v, dt=dt, c4=c4: v.scalar_tensor_tensor(
                        xt[dt][:, bass.ts(c4, 512)], psum_mix[dt], 0.5,
                        xt[dt][:, bass.ts(c4, 512)], ALU.mult, ALU.add))
                vmix = p.counts["V"]
                p.wait("tensor", "V", vmix)
                for hq in range(32):
                    buf = hq % 2
                    p.flush("tensor")
                    p.wait("sync", "T", p.counts["T"])
                    tk = p.dma("sync", wv[:, buf, 2048:2560],
                               wdt[bass.ts(hq, 128), 512:1024])
                    p.wait_dma("tensor", tk)
                    for dt in range(4):
                        p.on("tensor", lambda t, buf=buf, dt=dt, hq=hq: t.matmul(
                            psum_mix[dt],
                            wv[:, buf, 2048 + dt * 128:2048 + (dt + 1) * 128],
                            hslot(hq), start=(hq == 0), stop=(hq == 31)),
                            inc="T")
                p.flush("tensor")
                tmix2 = p.counts["T"]
                p.wait("vector", "T", tmix2)
                for dt in range(4):
                    p.on("vector", lambda v, dt=dt, c4=c4: v.scalar_tensor_tensor(
                        xt[4 + dt][:, bass.ts(c4, 512)], psum_mix[dt], 0.5,
                        xt[4 + dt][:, bass.ts(c4, 512)], ALU.mult, ALU.add))
                p.wait("tensor", "V", p.counts["V"])

    p.flush("tensor")
    p.mark("vector")


def emit_l1(p, c, parts=("f1", "a", "f2")):
    if "f1" in parts:
        emit_l1_ffn(p, c, 0)
    if "a" in parts:
        emit_l1_attn(p, c)
    if "f2" in parts:
        emit_l1_ffn(p, c, 1)


def emit_l1_attn(p, c):
    nc, wb, xs, hs = c["nc"], c["wb"], c["xs"], c["hs"]
    af, ab, row = c["af"], c["ab"], c["row"]
    wbuf, smal, stat, cst = c["wbuf"], c["smal"], c["stat"], c["cst"]
    ps_a, ps_b, ps_c, ps_d, ps_big = (c["ps_a"], c["ps_b"], c["ps_c"],
                                      c["ps_d"], c["ps_big"])
    xt, ht = c["xtiles"], c["hstiles"]

    emit_ln(p, c, xt, ht)
    vln = p.counts["V"]

    p.wait("gpsimd", "V", vln)
    p.on("gpsimd", lambda g: g.iota(af(6)[:, 0:256], pattern=[[1, 256]],
                                    base=0, channel_multiplier=-1,
                                    allow_small_or_imprecise_dtypes=True))
    gio = p.mark("gpsimd")
    p.wait("vector", "G", gio[1])
    p.on("vector", lambda v: v.tensor_scalar(
        smal[:, 224:480], af(6)[:, 0:256], 0.0, None, ALU.is_ge))
    p.on("vector", lambda v: v.tensor_scalar(
        af(6)[:, 256:512], af(6)[:, 0:256], 128.0, None, ALU.is_le))
    p.on("vector", lambda v: v.tensor_tensor(
        smal[:, 224:480], smal[:, 224:480], af(6)[:, 256:512], ALU.mult))
    p.on("vector", lambda v: v.tensor_scalar(
        smal[:, 480:736], af(6)[:, 0:256], 0.0, None, ALU.is_le))
    p.on("vector", lambda v: v.tensor_scalar(
        af(6)[:, 256:512], af(6)[:, 0:256], -128.0, None, ALU.is_ge))
    p.on("vector", lambda v: v.tensor_tensor(
        smal[:, 480:736], smal[:, 480:736], af(6)[:, 256:512], ALU.mult))
    mask1 = smal[:, 224:480]
    mask0 = smal[:, 480:736]

    p.wait("sync", c["wb_ready_g"][0], c["wb_ready_g"][1])
    wq = wb("L1wqkvT")
    wo = wb("L1woutT")
    wvb = wbuf[:, 0:6144].rearrange("p (b k f) -> p b k f", b=2, k=8)
    sc = af(6)[:, 0:256]
    scg = af(6)[:, 256:272]
    eb = ab(13)[:, 0:256]
    ebT = ab(13)[:, 256:512].rearrange("p (a b) -> p a b", a=2)
    vT = ab(13)[:, 512:640].rearrange("p (a b) -> p a b", a=2)
    ag = ab(13)[:, 640:656]
    agT = ab(13)[0:16, 656:784]
    vgT = ab(14)[0:16, 0:512]

    for half in range(2):
        p.flush("tensor")
        p.wait("sync", "T", p.counts["T"])
        p.wait("tensor", "V", vln if half == 0 else p.counts["V"])
        last_mm = {}
        for g3 in range(4):
            buf = g3 % 2
            if g3 >= 2:
                p.wait("sync", "T", last_mm[buf])
            tks = [p.dma("sync", wvb[:, buf, kt, :],
                         wq[bass.ts(kt, 128),
                            half * 1536 + g3 * 384:half * 1536 + (g3 + 1) * 384])
                   for kt in range(8)]
            p.wait_dma("tensor", tks)
            for m3 in range(3):
                mt = g3 * 3 + m3
                for ch in range(4):
                    pst = ps_a if (m3 * 4 + ch) % 2 == 0 else ps_b
                    if (g3, m3, ch) != (0, 0, 0):
                        p.wait("tensor", "V", p.counts["V"])
                    for kt in range(8):
                        p.on("tensor", lambda t, buf=buf, kt=kt, m3=m3, ch=ch,
                             pst=pst: t.matmul(
                                 pst[:, 0:512],
                                 wvb[:, buf, kt, m3 * 128:(m3 + 1) * 128],
                                 ht[kt][:, bass.ts(ch, 512)],
                                 start=(kt == 0), stop=(kt == 7)), inc="T")
                    p.flush("tensor")
                    tv = p.counts["T"]
                    p.wait("vector", "T", tv)
                    p.on("vector", lambda v, mt=mt, ch=ch, pst=pst: v.tensor_copy(
                        ab(mt)[:, bass.ts(ch, 512)], pst[:, 0:512]))
            last_mm[buf] = p.counts["T"]
        vqkv = p.counts["V"]

        p.wait("tensor", "V", vqkv)
        for hh in range(8):
            po_ = 64 * (hh % 2)
            vslc = ab(8 + hh // 2)[po_:po_ + 64, 0:1921:128]
            idslc = c["idb"][po_:po_ + 64, po_:po_ + 64]
            pst = (ps_c if hh % 2 == 0 else ps_d)[:].bitcast(BF16)
            if hh >= 2:
                p.wait("tensor", "V", p.counts["V"])
            p.on("tensor", lambda t, vslc=vslc, pst=pst, idslc=idslc: t.transpose(
                pst[0:16, 0:64], vslc, idslc), inc="T")
            p.flush("tensor")
            tv = p.counts["T"]
            p.wait("vector", "T", tv)
            p.on("vector", lambda v, hh=hh, pst=pst: v.tensor_copy(
                vgT[:, hh * 64:(hh + 1) * 64], pst[0:16, 0:64]))
        vvg = p.counts["V"]
        p.wait("tensor", "V", vvg)

        for it in range(TT):
            i0 = it * 128
            wstart = 0 if it == 0 else i0 - 128
            msk = mask0 if it == 0 else mask1
            for dh in range(4):
                for hh2 in range(2):
                    hh = 2 * dh + hh2
                    po = 64 * hh2
                    qslc = ab(dh)
                    kslc = ab(4 + dh)
                    vslc = ab(8 + dh)
                    qs = qslc[po:po + 64, :]
                    ks = kslc[po:po + 64, :]
                    vs = vslc[po:po + 64, :]
                    p.wait("tensor", "V", p.counts["V"])
                    p.wait("tensor", "S", p.counts["S"])
                    p.on("tensor", lambda t, qs=qs, ks=ks, i0=i0,
                         wstart=wstart: t.matmul(
                             ps_a[:, 0:256], qs[:, i0:i0 + 128],
                             ks[:, wstart:wstart + 256], start=True, stop=True),
                         inc="T")
                    p.on("tensor", lambda t, qs=qs, ks=ks, i0=i0: t.matmul(
                        ps_a[:, 256:272], qs[:, i0:i0 + 128],
                        ks[:, 0:1921:128], start=True, stop=True), inc="T")
                    p.flush("tensor")
                    tsc = p.counts["T"]
                    p.wait("vector", "T", tsc)
                    p.on("vector", lambda v, msk=msk: v.scalar_tensor_tensor(
                        sc, ps_a[:, 0:256], 0.125, msk, ALU.mult, ALU.mult))
                    p.on("vector", lambda v: v.tensor_scalar(
                        scg, ps_a[:, 256:272], 0.125, None, ALU.mult))
                    p.on("vector", lambda v: v.tensor_reduce(
                        stat[:, 0:1], sc, axis=AX.X, op=ALU.max))
                    p.on("vector", lambda v: v.tensor_scalar(
                        stat[:, 1:2], stat[:, 0:1], -1.0, None, ALU.mult))
                    p.on("vector", lambda v: v.tensor_reduce(
                        stat[:, 3:4], scg, axis=AX.X, op=ALU.max))
                    p.on("vector", lambda v: v.tensor_scalar(
                        stat[:, 4:5], stat[:, 3:4], -1.0, None, ALU.mult))
                    vm = p.counts["V"]
                    p.wait("scalar", "V", vm)
                    p.on("scalar", lambda s: s.activation(
                        sc, sc, AF.Exp, bias=stat[:, 1:2]))
                    p.on("scalar", lambda s: s.activation(
                        scg, scg, AF.Exp, bias=stat[:, 4:5]))
                    sel = p.counts["S"]
                    p.wait("vector", "S", sel)
                    p.on("vector", lambda v, msk=msk: v.tensor_tensor(
                        sc, sc, msk, ALU.mult))
                    p.on("vector", lambda v: v.tensor_reduce(
                        stat[:, 2:3], sc, axis=AX.X, op=ALU.add))
                    p.on("vector", lambda v: v.reciprocal(stat[:, 2:3],
                                                          stat[:, 2:3]))
                    p.on("vector", lambda v: v.tensor_scalar(
                        eb, sc, stat[:, 2:3], None, ALU.mult))
                    p.on("vector", lambda v: v.tensor_reduce(
                        stat[:, 5:6], scg, axis=AX.X, op=ALU.add))
                    p.on("vector", lambda v: v.reciprocal(stat[:, 5:6],
                                                          stat[:, 5:6]))
                    p.on("vector", lambda v: v.tensor_scalar(
                        ag, scg, stat[:, 5:6], None, ALU.mult))
                    vsm = p.counts["V"]
                    p.wait("tensor", "V", vsm)
                    for jt in range(2):
                        pst = (ps_b if jt == 0 else ps_c)[:].bitcast(BF16)
                        p.on("tensor", lambda t, jt=jt, pst=pst: t.transpose(
                            pst[0:128, 0:128], eb[:, bass.ts(jt, 128)],
                            c["idb"]), inc="T")
                        idslc = c["idb"][po:po + 64, po:po + 64]
                        p.on("tensor", lambda t, jt=jt, pst=pst, vs=vs,
                             wstart=wstart, idslc=idslc: t.transpose(
                                 pst[0:128, 128:192],
                                 vs[:, wstart + jt * 128:wstart + (jt + 1) * 128],
                                 idslc), inc="T")
                    p.on("tensor", lambda t: t.transpose(
                        ps_d[:, 0:512].bitcast(BF16)[0:16, 0:128], ag,
                        c["idb"]), inc="T")
                    p.flush("tensor")
                    ttr = p.counts["T"]
                    p.wait("vector", "T", ttr)
                    for jt in range(2):
                        pst = (ps_b if jt == 0 else ps_c)[:].bitcast(BF16)
                        p.on("vector", lambda v, jt=jt, pst=pst: v.tensor_copy(
                            ebT[:, jt, :], pst[0:128, 0:128]))
                        p.on("vector", lambda v, jt=jt, pst=pst: v.tensor_copy(
                            vT[:, jt, :], pst[0:128, 128:192]))
                    p.on("vector", lambda v: v.tensor_copy(
                        agT, ps_d[:, 0:512].bitcast(BF16)[0:16, 0:128]))
                    vtr = p.counts["V"]
                    p.wait("tensor", "V", vtr)
                    for jt in range(2):
                        p.on("tensor", lambda t, jt=jt, po=po: t.matmul(
                            ps_big[po:po + 64, 0:128], vT[:, jt, :],
                            ebT[:, jt, :], start=(jt == 0), stop=False),
                            inc="T")
                    p.on("tensor", lambda t, hh=hh, po=po: t.matmul(
                        ps_big[po:po + 64, 0:128],
                        vgT[:, hh * 64:(hh + 1) * 64], agT,
                        start=False, stop=True), inc="T")
                p.flush("tensor")
                tav = p.counts["T"]
                p.wait("vector", "T", tav)
                p.on("vector", lambda v, dh=dh, it=it: v.tensor_copy(
                    ab(16 + dh)[:, bass.ts(it, 128)], ps_big[:, 0:128]))
        vattn = p.counts["V"]

        p.flush("tensor")
        p.wait("sync", "T", p.counts["T"])
        tks = [p.dma("sync", wvb[:, 0, kt, :].rearrange("p (a b) -> p a b", a=2)
                     if False else wbuf[:, kt * 1024:(kt + 1) * 1024],
                     wo[bass.ts(half * 4 + kt, 128), :]) for kt in range(4)]
        p.wait_dma("tensor", tks)
        p.wait("tensor", "V", vattn)
        for mt in range(8):
            for ch in range(4):
                pst = ps_a if (mt * 4 + ch) % 2 == 0 else ps_b
                p.wait("tensor", "V", p.counts["V"])
                for kt in range(4):
                    p.on("tensor", lambda t, kt=kt, mt=mt, ch=ch, pst=pst:
                         t.matmul(
                             pst[:, 0:512],
                             wbuf[:, kt * 1024 + mt * 128:kt * 1024 + (mt + 1) * 128],
                             ab(16 + kt)[:, bass.ts(ch, 512)],
                             start=(kt == 0), stop=(kt == 3)), inc="T")
                p.flush("tensor")
                tv = p.counts["T"]
                p.wait("vector", "T", tv)
                if half == 0:
                    p.on("vector", lambda v, mt=mt, ch=ch, pst=pst: v.tensor_tensor(
                        xt[mt][:, bass.ts(ch, 512)], xt[mt][:, bass.ts(ch, 512)],
                        pst[:, 0:512], ALU.add))
                else:
                    p.on("vector", lambda v, mt=mt, ch=ch, pst=pst:
                         v.scalar_tensor_tensor(
                             xt[mt][:, bass.ts(ch, 512)], pst[:, 0:512],
                             smal[:, 18 + mt:19 + mt],
                             xt[mt][:, bass.ts(ch, 512)], ALU.add, ALU.add))
    p.flush("tensor")
    p.mark("vector")


def emit_xallgather(p, c, to_seq):
    xs, xg_in, xg_out = c["xs"], c["xg_in"], c["xg_out"]
    n = c["n_cores"]
    assert n == NCORES, "layer 2 requires the full 8-core mesh"
    p.barrier()
    tk = p.dma("gpsimd", xg_in.rearrange("(a p t) -> p a t", p=128, a=DT), xs[:])
    p.wait_dma("gpsimd", tk)
    p.on("gpsimd", lambda g: g.collective_compute(
        "AllGather", ALU.bypass, replica_groups=[list(range(n))],
        ins=[xg_in.opt()], outs=[xg_out.opt()]), inc="cc")
    p.wait("gpsimd", "cc")
    gm = p.mark("gpsimd")
    p.wait("sync", "G", gm[1])
    xg6 = xg_out.rearrange("(r a p c t) -> c r a p t", r=n, a=DT, p=128, c=8)
    gixcol = c["gix"][:, 1:2]
    xgrows = xg_out.rearrange("(n t) -> n t", t=256)
    tks = []
    for r in range(n):
        for a in range(DT):
            eo = (r * 8192 + a * 1024) * 256
            def mk(eng, r=r, a=a, eo=eo, gixcol=gixcol, xgrows=xgrows, xs=xs):
                return eng.indirect_dma_start(
                    out=xs[:, a, r * 256:(r + 1) * 256], out_offset=None,
                    in_=xgrows, element_offset=eo,
                    in_offset=bass.IndirectOffsetOnAxis(gixcol, 0))
            tks.append(p.dma2("gpsimd", mk))
    p.wait_dma("gpsimd", tks)
    gm2 = p.mark("gpsimd")
    for e in ("tensor", "vector", "scalar", "sync"):
        p.wait(e, "G", gm2[1])


def emit_l2_ffn(p, c):
    nc, wb, xs, hs = c["nc"], c["wb"], c["xs"], c["hs"]
    af, ab, row = c["af"], c["ab"], c["row"]
    wbuf, smal, stat = c["wbuf"], c["smal"], c["stat"]
    ps_a, ps_b, ps_c, ps_d, ps_big = (c["ps_a"], c["ps_b"], c["ps_c"],
                                      c["ps_d"], c["ps_big"])
    xt, ht = c["xtiles"], c["hstiles"]
    onesb, onesf, onesrow = c["onesb"], c["onesf"], c["onesrow"]

    emit_ln(p, c, xt, ht)
    vln = p.counts["V"]

    p.wait("sync", c["wb_ready_g"][0], c["wb_ready_g"][1])
    p.flush("tensor")
    p.wait("sync", "T", p.counts["T"])
    tk_g = p.dma("sync", wbuf[:, 0:64].rearrange("p (k f) -> p k f", k=8),
                 wb("L2wgate").rearrange("(k p) f -> p k f", p=128))
    p.wait_dma("tensor", tk_g)
    p.wait("tensor", "V", vln)
    for tt in range(TT):
        for kt in range(8):
            p.on("tensor", lambda t, tt=tt, kt=kt: t.matmul(
                ps_c[:, tt * 8:(tt + 1) * 8], ht[kt][:, bass.ts(tt, 128)],
                wbuf[:, kt * 8:(kt + 1) * 8],
                start=(tt == 0 and kt == 0), stop=(tt == TT - 1 and kt == 7)),
                inc="T")
    p.flush("tensor")
    tlg = p.counts["T"]
    p.wait("vector", "T", tlg)
    for tt in range(TT):
        p.on("vector", lambda v, tt=tt: v.tensor_copy(
            smal[:, 224 + tt * 8:232 + tt * 8], ps_c[:, tt * 8:(tt + 1) * 8]))
    for tt in range(TT):
        lg = smal[:, 224 + tt * 8:232 + tt * 8]
        p.on("vector", lambda v, lg=lg: v.max(stat[:, 0:8], lg))
        p.on("vector", lambda v: v.tensor_scalar(
            stat[:, 8:9], stat[:, 0:1], -1.0, None, ALU.mult))
        p.on("vector", lambda v, lg=lg: v.tensor_scalar(
            stat[:, 16:24], lg, stat[:, 1:2], None, ALU.is_ge))
        vge = p.counts["V"]
        p.wait("scalar", "V", vge)
        p.on("scalar", lambda s: s.activation(
            stat[:, 9:10], stat[:, 1:2], AF.Exp, bias=stat[:, 8:9]))
        p.on("scalar", lambda s, lg=lg: s.activation(
            stat[:, 24:32], lg, AF.Exp, bias=stat[:, 8:9]))
        sexp = p.counts["S"]
        p.wait("vector", "S", sexp)
        p.on("vector", lambda v: v.tensor_scalar(
            stat[:, 9:10], stat[:, 9:10], 1.0, None, ALU.add))
        p.on("vector", lambda v: v.reciprocal(stat[:, 9:10], stat[:, 9:10]))
        p.on("vector", lambda v: v.tensor_tensor(
            stat[:, 24:32], stat[:, 24:32], stat[:, 16:24], ALU.mult))
        p.on("vector", lambda v: v.tensor_scalar(
            stat[:, 24:32], stat[:, 24:32], stat[:, 9:10], None, ALU.mult))
        p.on("vector", lambda v, lg=lg: v.tensor_scalar(
            lg, stat[:, 24:32], 0.25, None, ALU.mult))
    vgates = p.counts["V"]
    p.wait("tensor", "V", vgates)
    prev_tk = None
    for e in range(8):
        gcol = smal[:, 224 + e:224 + e + 121:8]
        pst = ps_a if e % 2 == 0 else ps_b
        if e >= 2:
            p.wait("tensor", "V", p.counts["V"])
        p.on("tensor", lambda t, gcol=gcol, pst=pst: t.transpose(
            pst[0:16, 0:128], gcol, c["idf"]), inc="T")
        p.flush("tensor")
        tv = p.counts["T"]
        p.wait("vector", "T", tv)
        if prev_tk is not None:
            p.wait_dma("vector", prev_tk)
        p.on("vector", lambda v, pst=pst: v.tensor_copy(
            smal[0:16, 352:480], pst[0:16, 0:128]))
        vv = p.counts["V"]
        p.wait("gpsimd", "V", vv)
        prev_tk = p.dma("gpsimd", c["grow"][e:e + 1, :].rearrange(
            "a (b f) -> (a b) f", b=16), smal[0:16, 352:480])
    p.wait_dma("gpsimd", prev_tk)
    gmark = p.mark("gpsimd")
    p.wait("sync", "G", gmark[1])

    p.flush("tensor")
    p.wait("sync", "T", p.counts["T"])
    tks = [p.dma("sync", wbuf[:, kt * 512:(kt + 1) * 512],
                 wb("L2wd")[bass.ts(kt, 128), :]) for kt in range(8)]
    p.wait_dma("tensor", tks)
    for mt in range(4):
        for ch in range(4):
            pst = ps_a if (mt * 4 + ch) % 2 == 0 else ps_b
            p.wait("tensor", "S", p.counts["S"])
            for kt in range(8):
                p.on("tensor", lambda t, kt=kt, mt=mt, ch=ch, pst=pst: t.matmul(
                    pst[:, 0:512],
                    wbuf[:, kt * 512 + mt * 128:kt * 512 + (mt + 1) * 128],
                    ht[kt][:, bass.ts(ch, 512)],
                    start=(kt == 0), stop=(kt == 7)), inc="T")
            p.flush("tensor")
            tv = p.counts["T"]
            p.wait("scalar", "T", tv)
            p.on("scalar", lambda s, mt=mt, ch=ch, pst=pst: s.activation(
                ab(mt)[:, bass.ts(ch, 512)], pst[:, 0:512], AF.Relu,
                bias=smal[:, 42 + mt:43 + mt]))
    sh = p.counts["S"]
    p.wait("vector", "S", sh)
    for mt in range(4):
        for ch in range(4):
            p.flush("tensor")
            p.wait("sync", "T", p.counts["T"])
            tka = p.dma("sync", smal[0:1, 512:1024],
                        c["grow"][2 * mt:2 * mt + 1, bass.ts(ch, 512)])
            tkb = p.dma("sync", c["row"](0)[:, 0:512],
                        c["grow"][2 * mt + 1:2 * mt + 2, bass.ts(ch, 512)])
            p.wait_dma("tensor", [tka, tkb])
            p.wait("tensor", "V", p.counts["V"])
            p.on("tensor", lambda t: t.matmul(
                ps_c[0:64, 0:512], onesrow[:, 0:64], smal[0:1, 512:1024],
                start=True, stop=True), inc="T")
            p.on("tensor", lambda t: t.matmul(
                ps_c[64:128, 0:512], onesrow[:, 0:64], c["row"](0)[:, 0:512],
                start=True, stop=True), inc="T")
            p.flush("tensor")
            tv = p.counts["T"]
            p.wait("vector", "T", tv)
            p.on("vector", lambda v, mt=mt, ch=ch: v.tensor_tensor(
                ab(mt)[:, bass.ts(ch, 512)], ab(mt)[:, bass.ts(ch, 512)],
                ps_c[:, 0:512], ALU.mult))
    vhg = p.counts["V"]
    tk_gr = p.dma("gpsimd", ab(4)[0:8, 0:2048],
                  c["grow"][0:8, :])
    p.flush("tensor")
    p.wait("sync", "T", p.counts["T"])
    tks = [p.dma("sync", wbuf[:, 4096 + kt * 1024:4096 + (kt + 1) * 1024],
                 wb("L2wu")[bass.ts(kt, 128), :]) for kt in range(4)]
    tkbu = p.dma("sync", ab(5)[0:8, 0:1024], wb("L2bu"))
    p.wait_dma("tensor", tks + [tk_gr, tkbu])
    p.wait("tensor", "V", vhg)
    for mt in range(8):
        for ch in range(4):
            pst = ps_a if (mt * 4 + ch) % 2 == 0 else ps_b
            p.wait("tensor", "V", p.counts["V"])
            for kt in range(4):
                p.on("tensor", lambda t, kt=kt, mt=mt, ch=ch, pst=pst: t.matmul(
                    pst[:, 0:512],
                    wbuf[:, 4096 + kt * 1024 + mt * 128:4096 + kt * 1024 + (mt + 1) * 128],
                    ab(kt)[:, bass.ts(ch, 512)],
                    start=(kt == 0), stop=False), inc="T")
            p.on("tensor", lambda t, mt=mt, ch=ch, pst=pst: t.matmul(
                pst[:, 0:512], ab(5)[0:8, mt * 128:(mt + 1) * 128],
                ab(4)[0:8, bass.ts(ch, 512)],
                start=False, stop=True), inc="T")
            p.flush("tensor")
            tv = p.counts["T"]
            p.wait("vector", "T", tv)
            p.on("vector", lambda v, mt=mt, ch=ch, pst=pst: v.tensor_tensor(
                xt[mt][:, bass.ts(ch, 512)], xt[mt][:, bass.ts(ch, 512)],
                pst[:, 0:512], ALU.add))
    p.flush("tensor")
    p.mark("vector")


def emit_l2_attn(p, c):
    nc, wb, xs, hs = c["nc"], c["wb"], c["xs"], c["hs"]
    af, ab, row = c["af"], c["ab"], c["row"]
    wbuf, smal, stat = c["wbuf"], c["smal"], c["stat"]
    ps_a, ps_b, ps_c, ps_d, ps_big = (c["ps_a"], c["ps_b"], c["ps_c"],
                                      c["ps_d"], c["ps_big"])
    xt, ht = c["xtiles"], c["hstiles"]

    hf32 = [af(dt) for dt in range(DT)]
    emit_ln(p, c, xt, hf32)
    vln = p.counts["V"]
    for dt in range(DT):
        p.on("vector", lambda v, dt=dt: v.tensor_tensor(
            xt[dt], xt[dt], hf32[dt], ALU.add))
    emit_ln(p, c, hf32, ht, sq_in_dst=True)
    for dt in range(DT):
        p.on("vector", lambda v, dt=dt: v.tensor_scalar(
            ht[dt], ht[dt], smal[:, 26 + dt:27 + dt], smal[:, 34 + dt:35 + dt],
            ALU.mult, ALU.add))
    vxin = p.counts["V"]

    p.wait("sync", c["wb_ready_g"][0], c["wb_ready_g"][1])
    wq = wb("L2wqkvT")
    wo = wb("L2woutT")
    fmt = wbuf[:, 6144:8192]
    def tm(comp, t):
        return ab(2 * comp + t // 8)[:, (t % 8) * 256:(t % 8) * 256 + 256]
    def outf(t):
        return af(6 + t // 4)[:, (t % 4) * 256:(t % 4) * 256 + 256]

    for qtr in range(4):
        for comp in range(3):
            for ft in range(2):
                col0 = comp * 1024 + qtr * 256 + ft * 128
                p.flush("tensor")
                p.wait("sync", "T", p.counts["T"])
                tks = [p.dma("sync", wbuf[:, kt * 128:(kt + 1) * 128],
                             wq[bass.ts(kt, 128), col0:col0 + 128])
                       for kt in range(8)]
                p.wait_dma("tensor", tks)
                if (qtr, comp, ft) == (0, 0, 0):
                    p.wait("tensor", "V", vxin)
                for ch in range(4):
                    pst = ps_a if ch % 2 == 0 else ps_b
                    p.wait("tensor", "V", p.counts["V"])
                    for kt in range(8):
                        p.on("tensor", lambda t, kt=kt, ch=ch, pst=pst: t.matmul(
                            pst[:, 0:512], wbuf[:, kt * 128:(kt + 1) * 128],
                            ht[kt][:, bass.ts(ch, 512)],
                            start=(kt == 0), stop=(kt == 7)), inc="T")
                    p.flush("tensor")
                    tv = p.counts["T"]
                    p.wait("vector", "T", tv)
                    if comp == 0:
                        p.on("vector", lambda v, ch=ch, pst=pst: v.tensor_scalar(
                            fmt[:, bass.ts(ch, 512)], pst[:, 0:512],
                            0.03125, None, ALU.mult))
                    else:
                        p.on("vector", lambda v, ch=ch, pst=pst: v.tensor_copy(
                            fmt[:, bass.ts(ch, 512)], pst[:, 0:512]))
                vfm = p.counts["V"]
                p.wait("tensor", "V", vfm)
                for t in range(16):
                    pst = (ps_c if t % 2 == 0 else ps_d)[:].bitcast(BF16)
                    if t >= 2:
                        p.wait("tensor", "V", p.counts["V"])
                    p.on("tensor", lambda t_, t=t, pst=pst: t_.transpose(
                        pst[0:128, 0:128], fmt[:, bass.ts(t, 128)], c["idb"]),
                        inc="T")
                    p.flush("tensor")
                    tv = p.counts["T"]
                    p.wait("vector", "T", tv)
                    p.on("vector", lambda v, t=t, comp=comp, ft=ft, pst=pst:
                         v.tensor_copy(tm(comp, t)[:, ft * 128:(ft + 1) * 128],
                                       pst[0:128, 0:128]))
        vtm = p.counts["V"]

        for nc2 in range(2):
            for i in range(8):
                ti = 2 * i + nc2
                for j in range(8):
                    tj = 2 * j + nc2
                    p.on("vector", lambda v, ti=ti, tj=tj: v.tensor_tensor(
                        af(5)[:, 0:256], tm(0, ti), tm(1, tj), ALU.mult))
                    p.on("vector", lambda v, j=j: v.tensor_reduce(
                        stat[:, 0:32].rearrange("p (h j) -> p h j", j=8)
                        [:, :, j:j + 1],
                        af(5)[:, 0:256].rearrange("p (h d) -> p h d", d=64),
                        axis=AX.X, op=ALU.add))
                Sv = stat[:, 0:32].rearrange("p (h j) -> p h j", j=8)
                p.on("vector", lambda v, Sv=Sv: v.tensor_reduce(
                    stat[:, 32:36][:, :, None], Sv, axis=AX.X, op=ALU.max))
                p.on("vector", lambda v: v.tensor_scalar(
                    stat[:, 36:40], stat[:, 32:36], -1.0, None, ALU.mult))
                p.on("vector", lambda v, Sv=Sv: v.tensor_tensor(
                    Sv, Sv, stat[:, 36:40][:, :, None].broadcast_to(
                        [128, 4, 8]), ALU.add))
                vs = p.counts["V"]
                p.wait("scalar", "V", vs)
                p.on("scalar", lambda s: s.activation(
                    stat[:, 0:32], stat[:, 0:32], AF.Exp))
                se = p.counts["S"]
                p.wait("vector", "S", se)
                p.on("vector", lambda v, Sv=Sv: v.tensor_reduce(
                    stat[:, 32:36][:, :, None], Sv, axis=AX.X, op=ALU.add))
                p.on("vector", lambda v: v.reciprocal(stat[:, 32:36],
                                                      stat[:, 32:36]))
                p.on("vector", lambda v, Sv=Sv: v.tensor_tensor(
                    Sv, Sv, stat[:, 32:36][:, :, None].broadcast_to(
                        [128, 4, 8]), ALU.mult))
                for j in range(8):
                    tj = 2 * j + nc2
                    a_b = stat[:, 0:32].rearrange("p (h j) -> p h j", j=8)[
                        :, :, j:j + 1].broadcast_to([128, 4, 64])
                    if j == 0:
                        p.on("vector", lambda v, ti=ti, tj=tj, a_b=a_b:
                             v.tensor_tensor(
                                 outf(ti).rearrange("p (h d) -> p h d", d=64),
                                 tm(2, tj).rearrange("p (h d) -> p h d", d=64),
                                 a_b, ALU.mult))
                    else:
                        p.on("vector", lambda v, tj=tj, a_b=a_b: v.tensor_tensor(
                            af(5)[:, 0:256].rearrange("p (h d) -> p h d", d=64),
                            tm(2, tj).rearrange("p (h d) -> p h d", d=64),
                            a_b, ALU.mult))
                        p.on("vector", lambda v, ti=ti: v.tensor_tensor(
                            outf(ti), outf(ti), af(5)[:, 0:256], ALU.add))
        vout = p.counts["V"]

        p.wait("tensor", "V", vout)
        for ft in range(2):
            for t in range(16):
                pst = ps_c if t % 2 == 0 else ps_d
                p.wait("tensor", "V", p.counts["V"])
                p.on("tensor", lambda t_, t=t, ft=ft, pst=pst: t_.transpose(
                    pst[:, 0:128], outf(t)[:, ft * 128:(ft + 1) * 128],
                    c["idf"]), inc="T")
                p.flush("tensor")
                tv = p.counts["T"]
                p.wait("vector", "T", tv)
                p.on("vector", lambda v, t=t, pst=pst: v.tensor_copy(
                    fmt[:, bass.ts(t, 128)],
                    (ps_c if t % 2 == 0 else ps_d)[:, 0:128]))
            vfm2 = p.counts["V"]
            kt = qtr * 2 + ft
            p.flush("tensor")
            p.wait("sync", "T", p.counts["T"])
            tkw = p.dma("sync", wbuf[:, 2048:3072], wo[bass.ts(kt, 128), :])
            p.wait_dma("tensor", tkw)
            p.wait("tensor", "V", vfm2)
            for mt in range(8):
                for ch in range(4):
                    pst = ps_a if (mt * 4 + ch) % 2 == 0 else ps_b
                    p.wait("tensor", "V", p.counts["V"])
                    p.on("tensor", lambda t, mt=mt, ch=ch, pst=pst: t.matmul(
                        pst[:, 0:512],
                        wbuf[:, 2048 + mt * 128:2048 + (mt + 1) * 128],
                        fmt[:, bass.ts(ch, 512)],
                        start=True, stop=True), inc="T")
                    p.flush("tensor")
                    tv = p.counts["T"]
                    p.wait("vector", "T", tv)
                    p.on("vector", lambda v, mt=mt, ch=ch, pst=pst: v.tensor_tensor(
                        xt[mt][:, bass.ts(ch, 512)], xt[mt][:, bass.ts(ch, 512)],
                        pst[:, 0:512], ALU.add))
    p.flush("tensor")
    p.mark("vector")


def emit_l2(p, c, parts=("f1", "a", "f2")):
    emit_xallgather(p, c, to_seq=True)
    if "f1" in parts:
        emit_l2_ffn(p, c)
    if "a" in parts:
        emit_l2_attn(p, c)
    if "f2" in parts:
        emit_l2_ffn(p, c)
    emit_xallgather(p, c, to_seq=False)


def emit_l3_attn(p, c):
    nc, wb, xs, hs = c["nc"], c["wb"], c["xs"], c["hs"]
    af, ab, row = c["af"], c["ab"], c["row"]
    wbuf, smal, stat = c["wbuf"], c["smal"], c["stat"]
    ps_a, ps_b, ps_c, ps_d, ps_big = (c["ps_a"], c["ps_b"], c["ps_c"],
                                      c["ps_d"], c["ps_big"])
    xt, ht = c["xtiles"], c["hstiles"]
    arena = c["arena"]

    emit_ln(p, c, xt, ht)
    vln = p.counts["V"]

    def vtm(t):
        return arena[:, 16384 + t * 1024:16384 + (t + 1) * 1024]
    qb = lambda: arena[:, 32768:33792].rearrange("p (a b) -> p a b", a=8)
    PT = ab(17)[:, 0:640].rearrange("p (a b) -> p a b", a=5)
    Pb = ab(17)[:, 640:1280]
    m256 = ab(17)[:, 1280:1920]
    s_sb = af(9)[:, 0:640]
    av_sb = af(9)[:, 1024:2048]
    smal_bf = smal[:].bitcast(BF16)
    vrow = smal_bf[0:1, 1024:2048]

    p.wait("gpsimd", "V", vln)
    p.on("gpsimd", lambda g: g.iota(s_sb, pattern=[[1, 640]], base=0,
                                    channel_multiplier=-1,
                                    allow_small_or_imprecise_dtypes=True))
    gio = p.mark("gpsimd")
    p.wait("vector", "G", gio[1])
    def mkmask(dst, lo, hi, srcw):
        p.on("vector", lambda v: v.tensor_scalar(
            af(9)[:, 768:768 + srcw], s_sb[:, 0:srcw], float(lo), None,
            ALU.is_ge))
        p.on("vector", lambda v: v.tensor_scalar(
            af(9)[:, 1408:1408 + srcw], s_sb[:, 0:srcw], float(hi), None,
            ALU.is_le))
        p.on("vector", lambda v: v.tensor_tensor(
            dst, af(9)[:, 768:768 + srcw], af(9)[:, 1408:1408 + srcw],
            ALU.mult))
    mkmask(m256, 0, 511, 640)
    vmask = p.counts["V"]

    p.wait("sync", c["wb_ready_g"][0], c["wb_ready_g"][1])
    p.flush("tensor")
    p.wait("sync", "T", p.counts["T"])
    wv_ = wb("L3wv")
    tks = [p.dma("sync", wbuf[:, kt * 1024:(kt + 1) * 1024],
                 wv_[bass.ts(kt, 128), :]) for kt in range(8)]
    p.wait_dma("tensor", tks)
    p.wait("tensor", "V", vln)
    for t in range(16):
        for chh in range(2):
            pst = ps_a if (t * 2 + chh) % 2 == 0 else ps_b
            if (t, chh) != (0, 0):
                p.wait("tensor", "V", p.counts["V"])
            for ct in range(8):
                p.on("tensor", lambda t_, t=t, chh=chh, ct=ct, pst=pst:
                     t_.matmul(
                         pst[:, 0:512], ht[ct][:, bass.ts(t, 128)],
                         wbuf[:, ct * 1024 + chh * 512:ct * 1024 + (chh + 1) * 512],
                         start=(ct == 0), stop=(ct == 7)), inc="T")
            p.flush("tensor")
            tv = p.counts["T"]
            p.wait("vector", "T", tv)
            p.on("vector", lambda v, t=t, chh=chh, pst=pst: v.tensor_copy(
                vtm(t)[:, bass.ts(chh, 512)], pst[:, 0:512]))
    vvt = p.counts["V"]
    p.wait("tensor", "V", vvt)
    for chh in range(2):
        pst = ps_c if chh == 0 else ps_d
        for t in range(16):
            p.on("tensor", lambda t_, t=t, chh=chh, pst=pst: t_.matmul(
                pst[0:1, 0:512], c["onesb"], vtm(t)[:, bass.ts(chh, 512)],
                start=(t == 0), stop=(t == 15)), inc="T")
    p.flush("tensor")
    tvt = p.counts["T"]
    p.wait("vector", "T", tvt)
    p.on("vector", lambda v: v.tensor_copy(vrow[:, 0:512], ps_c[0:1, 0:512]))
    p.on("vector", lambda v: v.tensor_copy(vrow[:, 512:1024], ps_d[0:1, 0:512]))
    vvr = p.counts["V"]

    wk = wb("L3wk")
    wvb = wbuf[:, 0:6144].rearrange("p (b k f) -> p b k f", b=2, k=8)
    last_mm = {}
    p.flush("tensor")
    p.wait("sync", "T", p.counts["T"])
    groups = [(0, 3), (3, 3), (6, 2)]
    for gi, (m0, gn) in enumerate(groups):
        buf = gi % 2
        if gi >= 2:
            p.wait("sync", "T", last_mm[buf])
        tks = [p.dma("sync", wvb[:, buf, kt, 0:gn * 128],
                     wk[bass.ts(kt, 128), m0 * 128:(m0 + gn) * 128])
               for kt in range(8)]
        p.wait_dma("tensor", tks)
        for m3 in range(gn):
            mt = m0 + m3
            for ch in range(4):
                pst = ps_a if (m3 * 4 + ch) % 2 == 0 else ps_b
                p.wait("tensor", "V", p.counts["V"])
                for kt in range(8):
                    p.on("tensor", lambda t, buf=buf, kt=kt, m3=m3, ch=ch,
                         pst=pst: t.matmul(
                             pst[:, 0:512],
                             wvb[:, buf, kt, m3 * 128:(m3 + 1) * 128],
                             ht[kt][:, bass.ts(ch, 512)],
                             start=(kt == 0), stop=(kt == 7)), inc="T")
                p.flush("tensor")
                tv = p.counts["T"]
                p.wait("vector", "T", tv)
                p.on("vector", lambda v, mt=mt, ch=ch, pst=pst: v.tensor_copy(
                    ab(mt)[:, bass.ts(ch, 512)], pst[:, 0:512]))
        last_mm[buf] = p.counts["T"]
    vkf = p.counts["V"]
    p.wait("tensor", "V", vkf)

    wq_ = wb("L3wq")
    for it in range(TT):
        i0 = it * 128
        w0 = max(0, i0 - 256)
        w1 = min(2048, i0 + 384)
        w = w1 - w0
        p.flush("tensor")
        p.wait("sync", "T", p.counts["T"])
        tks = [p.dma("sync", wbuf[:, kt * 1024:(kt + 1) * 1024],
                     wq_[bass.ts(kt, 128), :]) for kt in range(8)]
        p.wait_dma("tensor", tks)
        p.wait("tensor", "V", p.counts["V"])
        p.wait("tensor", "S", p.counts["S"])
        for ft in range(8):
            pst = ps_c if ft % 2 == 0 else ps_d
            if ft >= 2:
                p.wait("tensor", "V", p.counts["V"])
            for ct in range(8):
                p.on("tensor", lambda t, ft=ft, ct=ct, i0=i0, pst=pst: t.matmul(
                    pst[:, 0:128],
                    wbuf[:, ct * 1024 + ft * 128:ct * 1024 + (ft + 1) * 128],
                    ht[ct][:, i0:i0 + 128],
                    start=(ct == 0), stop=(ct == 7)), inc="T")
            p.flush("tensor")
            tv = p.counts["T"]
            p.wait("vector", "T", tv)
            p.on("vector", lambda v, ft=ft, pst=pst: v.tensor_copy(
                qb()[:, ft, :], pst[:, 0:128]))
        vq = p.counts["V"]
        p.wait("tensor", "V", vq)
        nch = (w + 511) // 512
        for ct in range(8):
            for chw in range(nch):
                cw0 = chw * 512
                cw1 = min(w, cw0 + 512)
                p.on("tensor", lambda t, ct=ct, cw0=cw0, cw1=cw1, w0=w0: t.matmul(
                    ps_big[:, cw0:cw1], qb()[:, ct, :],
                    ab(ct)[:, w0 + cw0:w0 + cw1],
                    start=(ct == 0), stop=(ct == 7)), inc="T")
        p.flush("tensor")
        tsc = p.counts["T"]
        p.wait("vector", "T", tsc)
        if it <= 1:
            lo, hi, srcw = ((-256, 255, 384) if it == 0 else (-128, 383, 512))
            dst = ab(16)[:, 1024:1024 + srcw]
            p.wait("gpsimd", "V", p.counts["V"])
            p.on("gpsimd", lambda g, srcw=srcw: g.iota(
                af(9)[:, 768:768 + srcw], pattern=[[1, srcw]], base=0,
                channel_multiplier=-1,
                allow_small_or_imprecise_dtypes=True))
            gio2 = p.mark("gpsimd")
            p.wait("vector", "G", gio2[1])
            p.on("vector", lambda v, dst=dst, lo=lo, srcw=srcw: v.tensor_scalar(
                dst, af(9)[:, 768:768 + srcw], float(lo), None, ALU.is_ge))
            p.on("vector", lambda v, hi=hi, srcw=srcw: v.tensor_scalar(
                af(9)[:, 1536:1536 + srcw], af(9)[:, 768:768 + srcw],
                float(hi), None, ALU.is_le))
            p.on("vector", lambda v, dst=dst, srcw=srcw: v.tensor_tensor(
                dst, dst, af(9)[:, 1536:1536 + srcw], ALU.mult))
            msk = dst
        else:
            msk = m256[:, 0:w]
        p.on("vector", lambda v, w=w, msk=msk: v.scalar_tensor_tensor(
            s_sb[:, 0:w], ps_big[:, 0:w], 0.03125, msk, ALU.mult, ALU.mult))
        p.on("vector", lambda v, w=w: v.tensor_reduce(
            stat[:, 0:1], s_sb[:, 0:w], axis=AX.X, op=ALU.max))
        p.on("vector", lambda v: v.tensor_scalar(
            stat[:, 1:2], stat[:, 0:1], -1.0, None, ALU.mult))
        vs = p.counts["V"]
        p.wait("scalar", "V", vs)
        p.on("scalar", lambda s, w=w: s.activation(
            s_sb[:, 0:w], s_sb[:, 0:w], AF.Exp, bias=stat[:, 1:2]))
        p.on("scalar", lambda s: s.activation(
            stat[:, 2:3], stat[:, 1:2], AF.Exp))
        se = p.counts["S"]
        p.wait("vector", "S", se)
        p.on("vector", lambda v, w=w: v.tensor_scalar(
            s_sb[:, 0:w], s_sb[:, 0:w], stat[:, 2:3], None, ALU.subtract))
        p.on("vector", lambda v, w=w: v.tensor_reduce(
            stat[:, 3:4], s_sb[:, 0:w], axis=AX.X, op=ALU.add))
        p.on("vector", lambda v: v.scalar_tensor_tensor(
            stat[:, 3:4], stat[:, 2:3], 2048.0, stat[:, 3:4], ALU.mult,
            ALU.add))
        p.on("vector", lambda v: v.reciprocal(stat[:, 3:4], stat[:, 3:4]))
        p.on("vector", lambda v, w=w: v.tensor_scalar(
            Pb[:, 0:w], s_sb[:, 0:w], stat[:, 3:4], None, ALU.mult))
        p.on("vector", lambda v: v.tensor_tensor(
            stat[:, 4:5], stat[:, 2:3], stat[:, 3:4], ALU.mult))
        vP = p.counts["V"]
        p.wait("tensor", "V", vP)
        nwt = (w + 127) // 128
        for jt in range(nwt):
            pst = (ps_c if jt % 2 == 0 else ps_d)[:].bitcast(BF16)
            if jt >= 2:
                p.wait("tensor", "V", p.counts["V"])
            jw = min(128, w - jt * 128)
            p.on("tensor", lambda t, jt=jt, jw=jw, pst=pst: t.transpose(
                pst[0:jw, 0:128], Pb[:, jt * 128:jt * 128 + jw], c["idb"]),
                inc="T")
            p.flush("tensor")
            tv = p.counts["T"]
            p.wait("vector", "T", tv)
            p.on("vector", lambda v, jt=jt, jw=jw, pst=pst: v.tensor_copy(
                PT[:, jt, :][0:jw, :], pst[0:jw, 0:128]))
        p.wait("tensor", "V", p.counts["V"])
        p.on("tensor", lambda t: t.transpose(
            ps_c[0:1, 0:128], stat[:, 4:5], c["idf"]), inc="T")
        p.flush("tensor")
        tem = p.counts["T"]
        p.wait("vector", "T", tem)
        p.on("vector", lambda v: v.tensor_copy(
            smal_bf[0:1, 896:1024], ps_c[0:1, 0:128]))
        vtr = p.counts["V"]
        p.wait("tensor", "V", vtr)
        jt0 = w0 // 128
        for jt in range(nwt):
            jw = min(128, w - jt * 128)
            for chh in range(2):
                p.on("tensor", lambda t, jt=jt, jw=jw, chh=chh, jt0=jt0:
                     t.matmul(
                         ps_big[:, 1024 + chh * 512:1024 + (chh + 1) * 512],
                         PT[:, jt, :][0:jw, :],
                         vtm(jt0 + jt)[0:jw, bass.ts(chh, 512)],
                         start=(jt == 0), stop=False), inc="T")
        for chh in range(2):
            p.on("tensor", lambda t, chh=chh: t.matmul(
                ps_big[:, 1024 + chh * 512:1024 + (chh + 1) * 512],
                smal_bf[0:1, 896:1024], vrow[:, bass.ts(chh, 512)],
                start=False, stop=True), inc="T")
        p.flush("tensor")
        tav = p.counts["T"]
        p.wait("vector", "T", tav)
        p.on("vector", lambda v: v.tensor_copy(av_sb, ps_big[:, 1024:2048]))
        vav = p.counts["V"]
        p.wait("tensor", "V", vav)
        for dt in range(DT):
            pst = ps_c if dt % 2 == 0 else ps_d
            if dt >= 2:
                p.wait("tensor", "V", p.counts["V"])
            p.on("tensor", lambda t, dt=dt, pst=pst: t.transpose(
                pst[:, 0:128], av_sb[:, bass.ts(dt, 128)], c["idf"]), inc="T")
            p.flush("tensor")
            tv = p.counts["T"]
            p.wait("vector", "T", tv)
            p.on("vector", lambda v, dt=dt, it=it, pst=pst: v.tensor_tensor(
                xt[dt][:, bass.ts(it, 128)], xt[dt][:, bass.ts(it, 128)],
                pst[:, 0:128], ALU.add))
    p.flush("tensor")
    p.mark("vector")


def emit_l3_ffn(p, c):
    nc, wb, xs, hs = c["nc"], c["wb"], c["xs"], c["hs"]
    af, ab, row = c["af"], c["ab"], c["row"]
    wbuf, smal, stat, cstb = c["wbuf"], c["smal"], c["stat"], c["cstb"]
    ps_a, ps_b, ps_c, ps_d, ps_big = (c["ps_a"], c["ps_b"], c["ps_c"],
                                      c["ps_d"], c["ps_big"])
    xt, ht = c["xtiles"], c["hstiles"]
    io = c["io"]

    emit_ln(p, c, xt, ht)
    vln = p.counts["V"]
    for dt in range(DT):
        p.on("vector", lambda v, dt=dt: v.scalar_tensor_tensor(
            xt[dt], ht[dt], 0.5, xt[dt], ALU.mult, ALU.add))

    p.wait("sync", c["wb_ready_g"][0], c["wb_ready_g"][1])
    p.flush("tensor")
    p.wait("sync", "T", p.counts["T"])
    tk1 = [p.dma("sync", wbuf[:, kt * 128:(kt + 1) * 128],
                 wb("L3wp1")[bass.ts(kt, 128), :]) for kt in range(8)]
    tk2 = p.dma("sync", wbuf[:, 1024:1032], wb("L3wp2"))
    p.wait_dma("tensor", tk1)
    p.wait("tensor", "V", vln)
    p.wait("tensor", "S", p.counts["S"])
    for kt in range(8):
        p.on("tensor", lambda t, kt=kt: t.matmul(
            ps_c[:, 0:1], wbuf[:, kt * 128:(kt + 1) * 128],
            ht[kt][:, 2047:2048], start=(kt == 0), stop=(kt == 7)), inc="T")
    p.flush("tensor")
    th1 = p.counts["T"]
    p.wait("scalar", "T", th1)
    p.on("scalar", lambda s: s.activation(
        cstb[:, 130:131], ps_c[:, 0:1], AF.Gelu, bias=smal[:, 46:47]))
    sh1 = p.counts["S"]
    p.wait("tensor", "S", sh1)
    p.wait_dma("tensor", tk2)
    p.on("tensor", lambda t: t.matmul(
        ps_d[0:1, 0:8], cstb[:, 130:131], wbuf[:, 1024:1032],
        start=True, stop=True), inc="T")
    p.flush("tensor")
    tlg = p.counts["T"]
    p.wait("vector", "T", tlg)
    p.on("vector", lambda v: v.tensor_tensor(
        smal[0:1, 66:74], ps_d[0:1, 0:8], smal[0:1, 58:66], ALU.add))
    p.on("vector", lambda v: v.max(smal[0:1, 74:82], smal[0:1, 66:74]))
    p.on("vector", lambda v: v.max_index(c["idxt"][0:1, 0:8],
                                         smal[0:1, 74:82], smal[0:1, 66:74]))
    vsel = p.counts["V"]
    p.on("vector", lambda v: v.tensor_copy(smal[0:1, 90:91],
                                           c["idxt"][0:1, 0:1]))
    vself = p.counts["V"]
    p.wait("gpsimd", "V", vself)
    p.on("gpsimd", lambda g: g.iota(stat[:, 49:50], pattern=[[0, 1]], base=0,
                                    channel_multiplier=1,
                                    allow_small_or_imprecise_dtypes=True))
    giot = p.mark("gpsimd")
    p.wait("tensor", "V", vself)
    p.on("tensor", lambda t: t.matmul(ps_d[:, 4:5], c["onesrow"],
                                      smal[0:1, 90:91], start=True, stop=True),
         inc="T")
    p.flush("tensor")
    tsel = p.counts["T"]
    p.wait("vector", "T", tsel)
    p.wait("vector", "G", giot[1])
    p.on("vector", lambda v: v.tensor_copy(stat[:, 48:49], ps_d[:, 4:5]))
    for col, mul in ((2, 512.0), (3, 1024.0), (4, 128.0)):
        p.on("vector", lambda v, mul=mul: v.scalar_tensor_tensor(
            stat[:, 50:51], stat[:, 48:49], mul, stat[:, 49:50], ALU.mult,
            ALU.add))
        p.on("vector", lambda v, col=col: v.tensor_copy(
            c["gix"][:, col:col + 1], stat[:, 50:51]))
    vgix = p.counts["V"]
    p.wait("gpsimd", "V", vgix)
    tkb = [p.dma2("gpsimd", lambda eng: eng.indirect_dma_start(
               out=smal[:, 66:82], out_offset=None,
               in_=io["b1"].rearrange("a p t -> (a p) t"),
               in_offset=bass.IndirectOffsetOnAxis(c["gix"][:, 4:5], 0))),
           p.dma2("gpsimd", lambda eng: eng.indirect_dma_start(
               out=smal[:, 82:90], out_offset=None,
               in_=io["b2"].rearrange("a p t -> (a p) t"),
               in_offset=bass.IndirectOffsetOnAxis(c["gix"][:, 4:5], 0)))]
    p.wait_dma("scalar", tkb)
    p.wait_dma("vector", tkb)
    p.on("vector", lambda v: v.tensor_scalar(
        stat[:, 40:48], smal[:, 47:55], 0.15, None, ALU.mult))
    vbo = p.counts["V"]

    wvb = wbuf[:, 0:4096].rearrange("p (b k f) -> p b k f", b=2, k=8)
    for ch in range(4):
        for wv_ in range(8):
            buf = wv_ % 2
            p.flush("tensor")
            p.wait("gpsimd", "T", p.counts["T"])
            w1flat = [c["wbrows"]("L3w1a", 2048), c["wbrows"]("L3w1b", 2048)]
            tks = [p.dma2("gpsimd",
                          lambda eng, kt=kt, buf=buf, wv_=wv_, w1flat=w1flat:
                          eng.indirect_dma_start(
                       out=wvb[:, buf, kt, 0:256], out_offset=None,
                       in_=w1flat[kt // 4][0],
                       element_offset=(w1flat[kt // 4][1]
                                       + (kt % 4) * 128 * 2048 + wv_ * 256),
                       in_offset=bass.IndirectOffsetOnAxis(c["gix"][:, 2:3], 0)))
                   for kt in range(8)]
            p.wait_dma("tensor", tks)
            for m2 in range(2):
                mt = wv_ * 2 + m2
                pst = ps_a if m2 % 2 == 0 else ps_b
                p.wait("tensor", "S", p.counts["S"])
                p.wait("tensor", "V", p.counts["V"])
                for kt in range(8):
                    p.on("tensor", lambda t, buf=buf, kt=kt, m2=m2, ch=ch,
                         pst=pst: t.matmul(
                             pst[:, 0:512],
                             wvb[:, buf, kt, m2 * 128:(m2 + 1) * 128],
                             ht[kt][:, bass.ts(ch, 512)],
                             start=(kt == 0), stop=(kt == 7)), inc="T")
                p.flush("tensor")
                tv = p.counts["T"]
                p.wait("scalar", "T", tv)
                p.on("scalar", lambda s, mt=mt, pst=pst: s.activation(
                    c["arena"][:, mt * 512:(mt + 1) * 512], pst[:, 0:512],
                    AF.Gelu, bias=smal[:, 66 + mt:67 + mt]))
        sh = p.counts["S"]
        p.wait("tensor", "S", sh)
        for mt in range(8):
            buf = mt % 2
            p.flush("tensor")
            p.wait("gpsimd", "T", p.counts["T"])
            w2flat = [c["wbrows"]("L3w2a", 1024), c["wbrows"]("L3w2b", 1024)]
            tks = [p.dma2("gpsimd",
                          lambda eng, kt=kt, buf=buf, mt=mt, w2flat=w2flat:
                          eng.indirect_dma_start(
                       out=wbuf[:, 4096 + buf * 2048 + kt * 128:
                                4096 + buf * 2048 + (kt + 1) * 128],
                       out_offset=None, in_=w2flat[kt // 8][0],
                       element_offset=(w2flat[kt // 8][1]
                                       + (kt % 8) * 128 * 1024 + mt * 128),
                       in_offset=bass.IndirectOffsetOnAxis(c["gix"][:, 3:4], 0)))
                   for kt in range(16)]
            p.wait_dma("tensor", tks)
            pst = ps_a if mt % 2 == 0 else ps_b
            p.wait("tensor", "S", p.counts["S"])
            p.wait("tensor", "V", p.counts["V"])
            for kt in range(16):
                p.on("tensor", lambda t, buf=buf, kt=kt, mt=mt, ch=ch,
                     pst=pst: t.matmul(
                         pst[:, 0:512],
                         wbuf[:, 4096 + buf * 2048 + kt * 128:
                              4096 + buf * 2048 + (kt + 1) * 128],
                         c["arena"][:, kt * 512:(kt + 1) * 512],
                         start=(kt == 0), stop=(kt == 15)), inc="T")
            p.flush("tensor")
            tv = p.counts["T"]
            p.wait("vector", "T", tv)
            p.on("vector", lambda v, mt=mt, pst=pst: v.tensor_scalar(
                c["arena"][:, 16384 + mt * 512:16384 + (mt + 1) * 512],
                pst[:, 0:512], smal[:, 82 + mt:83 + mt], None, ALU.add))
        sr = p.counts["V"]
        p.wait("tensor", "V", sr)
        wo_ = wb("L3wo")
        p.flush("tensor")
        p.wait("sync", "T", p.counts["T"])
        tks = [p.dma("sync", wbuf[:, 8192 - 8192 + kt * 1024:(kt + 1) * 1024]
                     if False else wbuf[:, kt * 1024:(kt + 1) * 1024],
                     wo_[bass.ts(kt, 128), :]) for kt in range(8)]
        p.wait_dma("tensor", tks)
        p.wait("tensor", "S", p.counts["S"])
        for mt in range(8):
            pst = ps_a if mt % 2 == 0 else ps_b
            p.wait("tensor", "V", p.counts["V"])
            for kt in range(8):
                p.on("tensor", lambda t, kt=kt, mt=mt, pst=pst: t.matmul(
                    pst[:, 0:512],
                    wbuf[:, kt * 1024 + mt * 128:kt * 1024 + (mt + 1) * 128],
                    c["arena"][:, 16384 + kt * 512:16384 + (kt + 1) * 512],
                    start=(kt == 0), stop=(kt == 7)), inc="T")
            p.flush("tensor")
            tv = p.counts["T"]
            p.wait("vector", "T", tv)
            p.on("vector", lambda v, mt=mt, ch=ch, pst=pst:
                 v.scalar_tensor_tensor(
                     xt[mt][:, bass.ts(ch, 512)], pst[:, 0:512], 0.15,
                     xt[mt][:, bass.ts(ch, 512)], ALU.mult, ALU.add))
            p.on("vector", lambda v, mt=mt, ch=ch: v.tensor_scalar(
                xt[mt][:, bass.ts(ch, 512)], xt[mt][:, bass.ts(ch, 512)],
                stat[:, 40 + mt:41 + mt], None, ALU.add))
    p.flush("tensor")
    p.mark("vector")


def emit_l3(p, c, parts=("f1", "a", "f2")):
    if "f1" in parts:
        emit_l3_ffn(p, c)
    if "a" in parts:
        emit_l3_attn(p, c)
    if "f2" in parts:
        emit_l3_ffn(p, c)


def make_in_maps(inputs, n_cores=NCORES):
    x = np.ascontiguousarray(inputs["L0_attn_x"], dtype=np.float32)
    img = host_pack_wb(inputs)
    fvec = host_pack_fvec(inputs)
    in_maps = []
    for b in range(n_cores):
        m = {
            "x_in": np.ascontiguousarray(x[b].T),
            "conv_src": (img.reshape(-1) if n_cores == 1 else
                         np.concatenate([img[0, b * WB_HSH:(b + 1) * WB_HSH],
                                         img[1, b * WB_HSH:(b + 1) * WB_HSH]])),
            "fvec": fvec,
            "cid": np.array([b], dtype=np.int32),
            "gidx": np.stack([b * 8192 + np.arange(128, dtype=np.uint32),
                              np.arange(128, dtype=np.uint32) * 8 + b]),
            "b1": np.ascontiguousarray(
                np.asarray(inputs["L3_ffn_b1"], np.float32)
                .reshape(8, 16, 128).transpose(0, 2, 1)),
            "b2": np.ascontiguousarray(
                np.asarray(inputs["L3_ffn_b2"], np.float32)
                .reshape(8, 8, 128).transpose(0, 2, 1)),
        }
        in_maps.append(m)
    return in_maps


def kernel(**inputs):
    nc = build_nc()
    in_maps = make_in_maps(inputs)
    res = run_bass_kernel_spmd(nc, in_maps, core_ids=list(range(NCORES)))
    return np.stack([res.results[b]["out"].T for b in range(NCORES)], axis=0)


# version 07
# speedup vs baseline: 1.8777x; 1.8777x over previous
import math
import numpy as np

import concourse.bass as bass
import concourse.mybir as mybir
from concourse.bass_utils import run_bass_kernel_spmd

F32 = mybir.dt.float32
BF16 = mybir.dt.bfloat16
I32 = mybir.dt.int32
AF = mybir.ActivationFunctionType
ALU = mybir.AluOpType
AX = mybir.AxisListType

D = 1024
T = 2048
DT = D // 128
TT = T // 128
NCORES = 8
LN_EPS = 1e-5

ENGINES = ("tensor", "vector", "scalar", "gpsimd", "sync")



def _wb_table():
    t = []
    def add(key, shape):
        t.append((key, tuple(shape)))
    for e in range(7):
        add(("L1wg", e), (32, 128, 8, 128))
        add(("L1wu", e), (32, 128, 8, 128))
        add(("L1wd", e), (4096, D))
    add(("L1swg", 0), (32, 128, 8, 128))
    add(("L1swu", 0), (32, 128, 8, 128))
    add(("L1swd", 0), (4096, D))
    add(("L3w1a", 0), (8, 512, 2048))
    add(("L3w1b", 0), (8, 512, 2048))
    add(("L3w2a", 0), (8, 1024, D))
    add(("L3w2b", 0), (8, 1024, D))
    add(("L3wot", 0), (8, 128, 8, 128))
    add(("L3wp1", 0), (D, 128))
    add(("L3wp2", 0), (128, 8))
    add(("L0wrvkT", 0), (D, 3 * D))
    add(("L0woutT", 0), (D, D))
    add(("L0qprojT", 0), (D, 128))
    add(("L0routerT", 0), (128, 48))
    add(("L0auxkT", 0), (128, 4096))
    add(("L0auxv", 0), (4096, D))
    add(("L0uncw1T", 0), (D, 256))
    add(("L0maskrow", 0), (1, 4096))
    add(("L0ones", 0), (1, T))
    add(("L0uncw2T", 0), (256, 1))
    add(("L1wqkvT", 0), (D, 3 * D))
    add(("L1woutT", 0), (D, D))
    add(("L1routerT", 0), (D, 7))
    add(("L2wqkvT", 0), (D, 3 * D))
    add(("L2woutT", 0), (D, D))
    add(("L2wgate", 0), (D, 8))
    add(("L2wd", 0), (D, 512))
    add(("L2wu", 0), (512, D))
    add(("L2bu", 0), (8, D))
    add(("L3wq", 0), (D, D))
    add(("L3wk", 0), (D, D))
    add(("L3wv", 0), (D, D))
    nbins = 2 * NCORES
    fills = [0] * nbins
    binof = {}
    for key, shape in sorted(t, key=lambda kv: -int(np.prod(kv[1]))):
        n = (int(np.prod(shape)) + 31) // 32 * 32
        j = min(range(nbins), key=lambda i: (fills[i], i))
        binof[key] = (j, fills[j])
        fills[j] += n
    hsh = (max(fills) + 511) // 512 * 512
    offsets = {}
    for key, shape in t:
        j, off = binof[key]
        offsets[key] = (j % 2, (j // 2) * hsh + off)
    return dict(t), offsets, hsh

WB_SHAPES, WB_OFFSETS, WB_HSH = _wb_table()
WB_SHARD = 2 * WB_HSH
WB_HALF_TOTAL = NCORES * WB_HSH


def host_pack_wb(inp):
    img = np.zeros((2, WB_HALF_TOTAL), dtype=np.float32)
    def put(key, arr):
        a = np.ascontiguousarray(arr, dtype=np.float32)
        assert a.shape == WB_SHAPES[key], (key, a.shape, WB_SHAPES[key])
        h, o = WB_OFFSETS[key]
        img[h, o:o + a.size] = a.reshape(-1)
    def tile_gu(w):
        return w.reshape(8, 128, 32, 128).transpose(2, 1, 0, 3)
    for e in range(7):
        put(("L1wg", e), tile_gu(inp["L1_ffn_w_gate"][e]))
        put(("L1wu", e), tile_gu(inp["L1_ffn_w_up"][e]))
        put(("L1wd", e), inp["L1_ffn_w_down"][e])
    put(("L1swg", 0), tile_gu(inp["L1_ffn_sw_gate"][0]))
    put(("L1swu", 0), tile_gu(inp["L1_ffn_sw_up"][0]))
    put(("L1swd", 0), inp["L1_ffn_sw_down"][0])
    put(("L3w1a", 0), inp["L3_ffn_W1"][:, 0:512, :])
    put(("L3w1b", 0), inp["L3_ffn_W1"][:, 512:1024, :])
    put(("L3w2a", 0), inp["L3_ffn_W2"][:, 0:1024, :])
    put(("L3w2b", 0), inp["L3_ffn_W2"][:, 1024:2048, :])
    put(("L3wot", 0), inp["L3_ffn_Wo"].reshape(8, 128, 8, 128)
        .transpose(2, 1, 0, 3))
    put(("L3wp1", 0), inp["L3_ffn_Wp1"])
    put(("L3wp2", 0), inp["L3_ffn_Wp2"])
    wrvkT = np.ascontiguousarray(inp["L0_attn_W_rvk"], dtype=np.float32).T
    g = np.empty_like(wrvkT)
    for ct in range(8):
        g[:, ct * 384 + 0:ct * 384 + 128] = wrvkT[:, 0 * 1024 + ct * 128:0 * 1024 + (ct + 1) * 128]
        g[:, ct * 384 + 128:ct * 384 + 256] = wrvkT[:, 1 * 1024 + ct * 128:1 * 1024 + (ct + 1) * 128]
        g[:, ct * 384 + 256:ct * 384 + 384] = wrvkT[:, 2 * 1024 + ct * 128:2 * 1024 + (ct + 1) * 128]
    put(("L0wrvkT", 0), g)
    put(("L0woutT", 0), inp["L0_attn_W_out"].T)
    put(("L0qprojT", 0), inp["L0_ffn_q_proj_w"].T)
    put(("L0routerT", 0), inp["L0_ffn_router_w"].T)
    put(("L0auxkT", 0), inp["L0_ffn_aux_keys"].T)
    put(("L0auxv", 0), inp["L0_ffn_aux_values"])
    put(("L0uncw1T", 0), inp["L0_ffn_unc_w1"].T)
    put(("L0maskrow", 0), inp["L0_ffn_reliability_mask"].reshape(1, 4096))
    put(("L0ones", 0), np.ones((1, T), np.float32))
    put(("L0uncw2T", 0), inp["L0_ffn_unc_w2"].T)
    wqt = np.ascontiguousarray(inp["L1_attn_w_qkv"], np.float32).T
    halfA = np.concatenate([wqt[:, 0:512], wqt[:, 1024:1536], wqt[:, 2048:2560]], 1)
    halfB = np.concatenate([wqt[:, 512:1024], wqt[:, 1536:2048], wqt[:, 2560:3072]], 1)
    put(("L1wqkvT", 0), np.concatenate([halfA, halfB], 1))
    put(("L1woutT", 0), inp["L1_attn_w_out"].T)
    put(("L1routerT", 0), inp["L1_ffn_router_w"].T)
    w2t = np.ascontiguousarray(inp["L2_attn_w_qkv"], np.float32).T
    w2v = w2t.reshape(1024, 16, 3, 64)
    w2p = np.concatenate([w2v[:, :, 0], w2v[:, :, 1], w2v[:, :, 2]],
                         axis=1).reshape(1024, 3072)
    put(("L2wqkvT", 0), w2p)
    put(("L2woutT", 0), inp["L2_attn_w_out"].T)
    put(("L2wgate", 0), inp["L2_ffn_w_gate"])
    put(("L2wd", 0), inp["L2_ffn_Wd"].transpose(1, 0, 2).reshape(D, 512))
    put(("L2wu", 0), inp["L2_ffn_Wu"].reshape(512, D))
    put(("L2bu", 0), inp["L2_ffn_bu"])
    put(("L3wq", 0), inp["L3_attn_WQ"])
    put(("L3wk", 0), inp["L3_attn_WK"])
    put(("L3wv", 0), inp["L3_attn_WV"])
    return img


FV_COLS = 66

def host_pack_fvec(inp):
    f = np.zeros((128, FV_COLS), dtype=np.float32)
    def colmaj(v):
        return np.asarray(v, np.float32).reshape(-1, 128).T
    f[:, 0:8] = colmaj(inp["L0_attn_time_decay"])
    f[:, 8:16] = colmaj(inp["L0_attn_time_first"])
    f[:, 16:18] = colmaj(inp["L0_ffn_unc_b1"])
    f[:, 18:26] = colmaj(inp["L1_attn_b_out"])
    f[:, 26:34] = colmaj(inp["L2_attn_ln_w"])
    f[:, 34:42] = colmaj(inp["L2_attn_ln_b"])
    f[:, 42:46] = colmaj(inp["L2_ffn_bd"].reshape(-1))
    f[:, 46:47] = colmaj(inp["L3_ffn_bp1"])
    f[:, 47:55] = colmaj(inp["L3_ffn_bo"])
    f[0, 55] = np.float32(inp["L0_ffn_unc_b2"].reshape(-1)[0])
    f[0, 56] = np.float32(np.asarray(inp["L0_ffn_gate_w1"]).reshape(-1)[0])
    f[0, 57] = np.float32(np.asarray(inp["L0_ffn_gate_bias"]).reshape(-1)[0])
    f[0, 58:66] = np.asarray(inp["L3_ffn_bp2"], np.float32).reshape(-1)
    return f


class Prog:

    def __init__(self, nc):
        self.nc = nc
        self.ops = {e: [] for e in ENGINES}
        self.sem_handles = {}
        self.counts = {}
        self.dma_ix = {}

    def add_sem(self, name, handle):
        self.sem_handles[name] = handle
        self.counts[name] = 0

    AUTO_SERIAL = ("vector", "scalar", "gpsimd")

    def on(self, engine, fn, inc=None, inc_by=1, serial=None):
        if inc is None:
            inc = self.ESEM[engine]
        sem = self.sem_handles[inc]
        if serial is None:
            serial = engine in self.AUTO_SERIAL
        pre = None
        if serial:
            own = self.ESEM[engine]
            ov = self.counts[own]
            if ov > 0:
                pre = (self.sem_handles[own], ov)
        def wrapped(eng, fn=fn, sem=sem, n=inc_by, pre=pre):
            if pre is not None:
                eng.wait_ge(pre[0], pre[1])
            fn(eng).then_inc(sem, n)
        self.ops[engine].append(wrapped)
        self.counts[inc] += inc_by
        return self.counts[inc]

    def wait(self, engine, sem_name, val=None):
        if val is None:
            val = self.counts[sem_name]
        assert val <= self.counts[sem_name], (sem_name, val, self.counts[sem_name])
        if val <= 0:
            return
        sem = self.sem_handles[sem_name]
        self.ops[engine].append(lambda eng, s=sem, v=val: eng.wait_ge(s, v))

    ESEM = {"tensor": "T", "vector": "V", "scalar": "S", "gpsimd": "G", "sync": "Y"}

    def softsync(self, engines=("tensor", "vector", "scalar")):
        self.flush("tensor")
        cur = {e: self.counts[self.ESEM[e]] for e in engines}
        for e in engines:
            for e2 in engines:
                if e2 != e and cur[e2] > 0:
                    self.wait(e, self.ESEM[e2], cur[e2])

    def flush(self, engine):
        s = self.ESEM[engine]
        self.wait(engine, s, self.counts[s])

    def mark(self, engine):
        s = self.ESEM[engine]
        v = self.on(engine, lambda eng: eng.nop(), inc=s)
        return (s, v)

    def barrier(self, engines=("tensor", "vector", "scalar", "gpsimd", "sync")):
        marks = [(e, self.mark(e)) for e in engines]
        for e in engines:
            for e2, (s, v) in marks:
                if e2 != e:
                    self.wait(e, s, v)

    DMA_R = 4

    def dma(self, queue, out, in_, transpose=False, slow_ok=False):
        if queue not in self.dma_ix:
            self.dma_ix[queue] = 0
        k = self.dma_ix[queue]
        self.dma_ix[queue] = k + 1
        qc = {"sync": "y", "scalar": "c", "gpsimd": "g", "vector": "v"}[queue]
        s = f"d{qc}{k % self.DMA_R}"
        sem = self.sem_handles[s]
        prev = (k // self.DMA_R) * 16
        def fn(eng, out=out, in_=in_, sem=sem, prev=prev, transpose=transpose,
               slow_ok=slow_ok):
            if prev > 0:
                eng.wait_ge(sem, prev)
            if transpose:
                eng.dma_start(out=out, in_=in_, transpose=True).then_inc(sem, 16)
            elif slow_ok:
                eng.dma_start(out=out, in_=in_,
                              allow_slow_non_contiguous=True).then_inc(sem, 16)
            else:
                eng.dma_start(out=out, in_=in_).then_inc(sem, 16)
        self.ops[queue].append(fn)
        self.counts[s] += 16
        return (s, prev + 16)

    def dma2(self, queue, mkfn):
        if queue not in self.dma_ix:
            self.dma_ix[queue] = 0
        k = self.dma_ix[queue]
        self.dma_ix[queue] = k + 1
        qc = {"sync": "y", "scalar": "c", "gpsimd": "g", "vector": "v"}[queue]
        s = f"d{qc}{k % self.DMA_R}"
        sem = self.sem_handles[s]
        prev = (k // self.DMA_R) * 16
        def fn(eng, mkfn=mkfn, sem=sem, prev=prev):
            if prev > 0:
                eng.wait_ge(sem, prev)
            try:
                mkfn(eng).then_inc(sem, 16)
            except Exception:
                import traceback as _tb
                _tb.print_exc()
                raise
        self.ops[queue].append(fn)
        self.counts[s] += 16
        return (s, prev + 16)

    def wait_dma(self, engine, tokens):
        if isinstance(tokens, tuple):
            tokens = [tokens]
        best = {}
        for s, v in tokens:
            best[s] = max(best.get(s, 0), v)
        for s, v in best.items():
            self.wait(engine, s, v)



def emit_ln(p, c, src_tiles, dst_tiles, out_f32_tiles=None, want_var_row=False,
            sq_in_dst=False):
    ps_a, ps_b, ps_big = c["ps_a"], c["ps_b"], c["ps_big"]
    af, row = c["af"], c["row"]
    hftile = [af(dt) for dt in range(DT)]
    onescol = c["onesf"]
    onesrow = c["onesrow"]

    p.softsync()
    for ch in range(4):
        if ch >= 2:
            p.wait("tensor", "V", p.counts["V"] - (1 if ch == 2 else 0))
        for dt in range(DT):
            p.on("tensor", lambda t, ch=ch, dt=dt: t.matmul(
                ps_a[0:1, 0:512] if ch % 2 == 0 else ps_b[0:1, 0:512],
                onescol, src_tiles[dt][:, bass.ts(ch, 512)],
                start=(dt == 0), stop=(dt == DT - 1)), inc="T")
        tv = p.counts["T"]
        p.wait("vector", "T", tv)
        p.on("vector", lambda v, ch=ch: v.tensor_copy(
            row(0)[:, bass.ts(ch, 512)],
            ps_a[0:1, 0:512] if ch % 2 == 0 else ps_b[0:1, 0:512]), inc="V")
    p.flush("tensor")
    tsum1 = p.counts["T"]
    sqm = []
    sqt = dst_tiles if sq_in_dst else hftile
    for dt in range(DT):
        if dt == 0:
            p.wait("scalar", "T", tsum1)
        p.on("scalar", lambda s, dt=dt: s.activation(
            sqt[dt], src_tiles[dt], AF.Square), inc="S")
        sqm.append(p.counts["S"])
    p.on("vector", lambda v: v.tensor_scalar(
        row(0), row(0), -1.0 / D, None, ALU.mult), inc="V")
    p.on("vector", lambda v: v.tensor_tensor(
        row(1), row(0), row(0), ALU.mult), inc="V")
    vneg = p.counts["V"]
    for ch in range(4):
        for dt in range(DT):
            if ch == 0:
                p.wait("tensor", "S", sqm[dt])
            if dt == 0:
                p.wait("tensor", "V", vneg if ch < 2 else p.counts["V"])
            p.on("tensor", lambda t, ch=ch, dt=dt: t.matmul(
                ps_a[0:1, 0:512] if ch % 2 == 0 else ps_b[0:1, 0:512],
                c["onesb"] if sq_in_dst else onescol,
                sqt[dt][:, bass.ts(ch, 512)],
                start=(dt == 0), stop=(dt == DT - 1)), inc="T")
        tv = p.counts["T"]
        p.wait("vector", "T", tv)
        p.on("vector", lambda v, ch=ch: v.scalar_tensor_tensor(
            row(1)[:, bass.ts(ch, 512)],
            ps_a[0:1, 0:512] if ch % 2 == 0 else ps_b[0:1, 0:512],
            1.0 / D, row(1)[:, bass.ts(ch, 512)], ALU.mult, ALU.subtract),
            inc="V")
    if want_var_row:
        p.on("vector", lambda v: v.tensor_copy(c["var_row_dst"], row(1)), inc="V")
    vv = p.counts["V"]
    p.wait("scalar", "V", vv)
    p.on("scalar", lambda s: s.activation(
        row(1), row(1), AF.Sqrt, bias=c["epscol"][0:1, :]), inc="S")
    p.wait("vector", "S")
    p.on("vector", lambda v: v.reciprocal(row(1), row(1)), inc="V")
    vstat = p.counts["V"]

    p.wait("tensor", "V", vstat)
    for ch in range(4):
        p.on("tensor", lambda t, ch=ch: t.matmul(
            ps_big[:, bass.ts(ch, 512)], onesrow, row(0)[:, bass.ts(ch, 512)],
            start=True, stop=True), inc="T")
    trep = p.counts["T"]
    p.wait("vector", "T", trep)
    for dt in range(DT):
        p.on("vector", lambda v, dt=dt: v.tensor_tensor(
            hftile[dt], src_tiles[dt], ps_big[:, 0:T], ALU.add), inc="V")
    v1 = p.counts["V"]
    p.wait("tensor", "V", v1)
    for ch in range(4):
        p.on("tensor", lambda t, ch=ch: t.matmul(
            ps_big[:, bass.ts(ch, 512)], onesrow, row(1)[:, bass.ts(ch, 512)],
            start=True, stop=True), inc="T")
    trep2 = p.counts["T"]
    p.wait("vector", "T", trep2)
    for dt in range(DT):
        p.on("vector", lambda v, dt=dt: v.tensor_tensor(
            dst_tiles[dt], hftile[dt], ps_big[:, 0:T], ALU.mult), inc="V")
        if out_f32_tiles is not None:
            p.on("vector", lambda v, dt=dt: v.tensor_tensor(
                out_f32_tiles[dt], hftile[dt], ps_big[:, 0:T], ALU.mult),
                inc="V")
    p.mark("vector")


def build_nc(phases=("l0", "l1", "l2", "l3"), n_cores=NCORES):
    import contextlib
    nc = bass.Bass("TRN2", target_bir_lowering=False, debug=False)
    est = contextlib.ExitStack()

    io = {}
    io["x_in"] = nc.dram_tensor("x_in", [D, T], F32, kind="ExternalInput").ap()
    conv_n = WB_SHARD if n_cores > 1 else 2 * WB_HALF_TOTAL
    io["conv_src"] = nc.dram_tensor("conv_src", [conv_n], F32,
                                    kind="ExternalInput").ap()
    io["fvec"] = nc.dram_tensor("fvec", [128, FV_COLS], F32,
                                kind="ExternalInput").ap()
    io["cid"] = nc.dram_tensor("cid", [1], I32, kind="ExternalInput").ap()
    io["gidx"] = nc.dram_tensor("gidx", [2, 128], mybir.dt.uint32,
                                kind="ExternalInput").ap()
    io["b1"] = nc.dram_tensor("b1", [8, 128, 16], F32, kind="ExternalInput").ap()
    io["b2"] = nc.dram_tensor("b2", [8, 128, 8], F32, kind="ExternalInput").ap()
    io["out"] = nc.dram_tensor("out", [D, T], F32, kind="ExternalOutput").ap()

    wb_shard = nc.dram_tensor("wb_shard", [WB_SHARD], BF16).ap()
    wb_allA = nc.dram_tensor("wb_allA", [WB_HALF_TOTAL], BF16).ap()
    wb_allB = nc.dram_tensor("wb_allB", [WB_HALF_TOTAL], BF16).ap()
    ar_in = nc.dram_tensor("ar_in", [4160], F32).ap()
    grow = nc.dram_tensor("grow", [8, T], F32).ap()

    ar_out = nc.dram_tensor("ar_out", [4160], F32).ap()
    xg_in = nc.dram_tensor("xg_in", [D * T], F32).ap()
    xg_out = nc.dram_tensor("xg_out", [NCORES * D * T], F32).ap()

    def wb(key, e=0):
        k = key if isinstance(key, tuple) else (key, e)
        shape = WB_SHAPES[k]
        h, off = WB_OFFSETS[k]
        n = int(np.prod(shape))
        flat = (wb_allA if h == 0 else wb_allB)[off:off + n]
        names = " ".join(f"a{i}" for i in range(len(shape)))
        return flat.rearrange(f"({names}) -> {names}",
                              **{f"a{i}": s for i, s in enumerate(shape)})

    def sb(name, shape, dtype):
        return est.enter_context(nc.sbuf_tensor(name, shape, dtype))

    xs = sb("xs", [128, DT, T], F32)
    hs = sb("hs", [128, DT, T], BF16)
    arena = sb("arena", [128, 40960], BF16)
    wbuf = sb("wbuf", [128, 8192], BF16)
    stat = sb("stat", [128, 64], F32)
    cst = sb("cst", [128, 8], F32)
    cidt = sb("cidt", [1, 1], I32)
    idxt = sb("idxt", [1, 8], mybir.dt.uint32)
    gix = sb("gix", [128, 8], mybir.dt.uint32)
    cstb = sb("cstb", [128, 272], BF16)
    smal = sb("smal", [128, 1024], F32)
    idf = sb("idf", [128, 128], F32)
    arena_f32 = arena[:].bitcast(F32)

    def af(slot, n=1):
        return arena_f32[:, slot * T:(slot + n) * T]

    def ab(slot, n=1):
        return arena[:, slot * T:(slot + n) * T]

    def row(i):
        return arena_f32[0:1, (8 + i) * T:(9 + i) * T]
    hf = None

    ps_big = est.enter_context(nc.psum_tensor("ps_big", [128, 2048], F32))
    ps_a = est.enter_context(nc.psum_tensor("ps_a", [128, 512], F32))
    ps_b = est.enter_context(nc.psum_tensor("ps_b", [128, 512], F32))
    ps_c = est.enter_context(nc.psum_tensor("ps_c", [128, 512], F32))
    ps_d = est.enter_context(nc.psum_tensor("ps_d", [128, 512], F32))

    p = Prog(nc)
    semnames = ["T", "V", "S", "G", "Y", "cc"]
    for q in "gcvy":
        semnames += [f"d{q}{i}" for i in range(Prog.DMA_R)]
    for s in semnames:
        p.add_sem(s, est.enter_context(nc.semaphore(s)))

    def wbrows(key, width):
        h, off = WB_OFFSETS[key if isinstance(key, tuple) else (key, 0)]
        half = wb_allA if h == 0 else wb_allB
        n = (WB_HALF_TOTAL // width) * width
        return half[0:n].rearrange("(r w) -> r w", w=width), off

    c = dict(nc=nc, io=io, wb=wb, wbrows=wbrows,
             xs=xs, hs=hs, af=af, ab=ab, row=row,
             arena=arena, cidt=cidt, idxt=idxt, gix=gix,
             wbuf=wbuf, stat=stat, cst=cst, cstb=cstb, smal=smal,
             idf=idf, ps_big=ps_big, ps_a=ps_a, ps_b=ps_b, ps_c=ps_c, ps_d=ps_d,
             ar_in=ar_in, ar_out=ar_out, xg_in=xg_in, xg_out=xg_out, grow=grow,
             wb_shard=wb_shard, wb_allA=wb_allA, wb_allB=wb_allB,
             n_cores=n_cores)

    emit_prologue(p, c)

    xtiles = [xs[:, dt, :] for dt in range(DT)]
    c["xtiles"] = xtiles
    c["hstiles"] = [hs[:, dt, :] for dt in range(DT)]

    if "l0a" in phases:
        emit_l0(p, c, parts=("a",))
    elif "l0f" in phases:
        emit_l0(p, c, parts=("f1",))
    elif "l0" in phases:
        emit_l0(p, c)
    if "l1a" in phases:
        emit_l1(p, c, parts=("a",))
    elif "l1f" in phases:
        emit_l1(p, c, parts=("f1",))
    elif "l1" in phases:
        emit_l1(p, c)
    if "xag" in phases:
        emit_xallgather(p, c, to_seq=True)
        emit_xallgather(p, c, to_seq=False)
    elif "l2f" in phases:
        emit_l2(p, c, parts=("f1",))
    elif "l2at" in phases:
        emit_l2(p, c, parts=("a",))
    elif "l2" in phases:
        emit_l2(p, c)
    if "l3a" in phases:
        emit_l3(p, c, parts=("a",))
    elif "l3f" in phases:
        emit_l3(p, c, parts=("f1",))
    elif "l3" in phases:
        emit_l3(p, c)

    emit_epilogue(p, c)

    with nc.Block() as block:
        for ename in ENGINES:
            deco = getattr(block, ename)
            def make(ops):
                def _(eng):
                    for fn in ops:
                        fn(eng)
                return _
            deco(make(p.ops[ename]))
    est.close()
    return nc


def rowslice(c):
    return c["rows"][15:16, 0:128]


def emit_prologue(p, c):
    nc, io = c["nc"], c["io"]
    xs, cst, cstb, idf, smal = c["xs"], c["cst"], c["cstb"], c["idf"], c["smal"]
    af = c["af"]

    tk_x = p.dma("sync", xs[:], io["x_in"].rearrange("(a p) t -> p a t", p=128))
    tk_f = p.dma("sync", smal[:, 0:FV_COLS], io["fvec"])
    tk_c = p.dma("sync", c["cidt"][0:1, 0:1], io["cid"][None, :])
    tk_g = p.dma("gpsimd", c["gix"][:, 0:2], io["gidx"].rearrange("a p -> p a"),
                 slow_ok=True)
    t0, t1 = af(0), af(1)
    p.on("gpsimd", lambda g: g.iota(t0[:, 0:128], pattern=[[1, 128]],
                                    base=0, channel_multiplier=0,
                                    allow_small_or_imprecise_dtypes=True), inc="G")
    p.on("gpsimd", lambda g: g.iota(t1[:, 0:128], pattern=[[0, 128]],
                                    base=0, channel_multiplier=1,
                                    allow_small_or_imprecise_dtypes=True), inc="G")
    p.wait("vector", "G")
    p.on("vector", lambda v: v.tensor_tensor(idf[:], t0[:, 0:128],
                                             t1[:, 0:128], ALU.is_equal),
         inc="V")
    p.on("vector", lambda v: v.tensor_copy(cstb[:, 0:128], idf[:]), inc="V")
    p.on("vector", lambda v: v.memset(cst[:, 0:1], 1.0), inc="V")
    p.on("vector", lambda v: v.memset(cst[:, 1:2], LN_EPS), inc="V")
    p.on("vector", lambda v: v.memset(cst[:, 2:3], -1.0), inc="V")
    p.on("vector", lambda v: v.memset(cstb[:, 128:129], 1.0), inc="V")
    p.on("vector", lambda v: v.memset(cstb[0:1, 144:272], 1.0), inc="V")
    p.on("vector", lambda v: v.memset(c["smal"][0:1, 96:224], 1.0), inc="V")
    c["idb"] = cstb[:, 0:128]
    c["onesf"] = cst[:, 0:1]
    c["epscol"] = cst[:, 1:2]
    c["negone"] = cst[:, 2:3]
    c["onesb"] = cstb[:, 128:129]
    c["onesrowb"] = cstb[0:1, 144:272]
    c["onesrow"] = c["smal"][0:1, 96:224]
    c["idf"] = idf[:]

    CH = 1 << 20
    conv_tk = []
    if c["n_cores"] == 1:
        for h, dst in ((0, c["wb_allA"]), (1, c["wb_allB"])):
            n = WB_HALF_TOTAL
            for i in range((n + CH - 1) // CH):
                a, b = i * CH, min((i + 1) * CH, n)
                conv_tk.append(p.dma("gpsimd", dst[a:b],
                                     io["conv_src"][h * n + a:h * n + b]))
        p.wait_dma("gpsimd", conv_tk)
        m = p.mark("gpsimd")
        c["wb_ready_g"] = m
        _skip = True
    else:
        _skip = False
        nch = (WB_SHARD + CH - 1) // CH
        for i in range(nch):
            a, b = i * CH, min((i + 1) * CH, WB_SHARD)
            conv_tk.append(p.dma("gpsimd", c["wb_shard"][a:b],
                                 io["conv_src"][a:b]))
        p.wait_dma("gpsimd", conv_tk)
    if not _skip and c["n_cores"] > 1:
        p.on("gpsimd", lambda g: g.collective_compute(
            "AllGather", ALU.bypass,
            replica_groups=[list(range(c["n_cores"]))],
            ins=[c["wb_shard"][0:WB_HSH].opt()], outs=[c["wb_allA"].opt()]),
            inc="cc")
        p.on("gpsimd", lambda g: g.collective_compute(
            "AllGather", ALU.bypass,
            replica_groups=[list(range(c["n_cores"]))],
            ins=[c["wb_shard"][WB_HSH:].opt()], outs=[c["wb_allB"].opt()]),
            inc="cc")
        p.wait("gpsimd", "cc")
        c["wb_ready"] = ("cc", p.counts["cc"])
    if not _skip:
        m = p.mark("gpsimd")
        c["wb_ready_g"] = m
    p.wait_dma("sync", [tk_x, tk_f, tk_c, tk_g])
    sy, syv = p.mark("sync")
    vm = p.counts["V"]
    for e in ("tensor", "vector", "scalar"):
        p.wait(e, sy, syv)
        if e != "vector":
            p.wait(e, "V", vm)


def emit_epilogue(p, c):
    p.barrier()
    tk = p.dma("sync", c["io"]["out"].rearrange("(a p) t -> p a t", p=128),
               c["xs"][:])
    p.wait_dma("sync", tk)
    qeng = {"y": "sync", "c": "scalar", "g": "gpsimd", "v": "vector"}
    for s, v in sorted(p.counts.items()):
        if len(s) == 3 and s[0] == "d" and v > 0:
            p.wait(qeng[s[1]], s, v)


def emit_l0_attn(p, c):
    nc, wb, xs, hs = c["nc"], c["wb"], c["xs"], c["hs"]
    af, ab, row = c["af"], c["ab"], c["row"]
    wbuf, smal, stat = c["wbuf"], c["smal"], c["stat"]
    ps_a, ps_b = c["ps_a"], c["ps_b"]
    xt, ht = c["xtiles"], c["hstiles"]

    emit_ln(p, c, xt, ht)

    p.on("scalar", lambda s: s.activation(stat[:, 0:8], smal[:, 0:8], AF.Exp))
    p.on("scalar", lambda s: s.activation(stat[:, 0:8], stat[:, 0:8], AF.Exp,
                                          scale=c["negone"]))
    sdec = p.counts["S"]
    p.wait("vector", "S", sdec)

    wrvk = wb("L0wrvkT")
    wvb = wbuf[:, 0:6144].rearrange("p (b k f) -> p b k f", b=2, k=8)
    rsig = wbuf[:, 6144:8192]

    rwkv = [ab(i) for i in range(8)]
    fK, fV, fEK, fEKV, fEUK, fA, fB = 4, 5, 6, 7, 8, 9, 9

    p.wait("sync", c["wb_ready_g"][0], c["wb_ready_g"][1])
    p.flush("tensor")
    p.wait("sync", "T", p.counts["T"])
    last_mm_using = {}
    for ct in range(8):
        buf = ct % 2
        if ct >= 2:
            p.wait("sync", "T", last_mm_using[buf])
        tks = [p.dma("sync", wvb[:, buf, kt, :],
                     wrvk[bass.ts(kt, 128), bass.ts(ct, 384)])
               for kt in range(8)]
        p.wait_dma("tensor", tks)
        if ct == 0:
            p.wait("tensor", "V")
        for part, cols in (("v", 128), ("k", 256), ("r", 0)):
            for ch in range(4):
                pst = ps_a if ch % 2 == 0 else ps_b
                if (ct, part, ch) != (0, "v", 0):
                    p.wait("tensor", "V", p.counts["V"])
                    p.wait("tensor", "S", p.counts["S"])
                for kt in range(8):
                    p.on("tensor", lambda t, buf=buf, kt=kt, cols=cols, ch=ch,
                         pst=pst: t.matmul(
                             pst[:, 0:512],
                             wvb[:, buf, kt, cols:cols + 128],
                             ht[kt][:, bass.ts(ch, 512)],
                             start=(kt == 0), stop=(kt == 7)), inc="T")
                tv = p.counts["T"]
                if part == "v":
                    p.wait("vector", "T", tv)
                    p.on("vector", lambda v, ch=ch, pst=pst: v.tensor_copy(
                        af(5)[:, bass.ts(ch, 512)], pst[:, 0:512]))
                elif part == "k":
                    p.wait("vector", "T", tv)
                    p.on("vector", lambda v, ch=ch, pst=pst: v.tensor_copy(
                        af(4)[:, bass.ts(ch, 512)], pst[:, 0:512]))
                else:
                    p.wait("scalar", "T", tv)
                    p.on("scalar", lambda s, ch=ch, pst=pst: s.activation(
                        rsig[:, bass.ts(ch, 512)], pst[:, 0:512], AF.Sigmoid))
        last_mm_using[buf] = p.counts["T"]
        p.wait("scalar", "V")
        p.on("scalar", lambda s: s.activation(af(6), af(4), AF.Exp))
        p.on("scalar", lambda s, ct=ct: s.activation(
            af(8), af(4), AF.Exp, bias=smal[:, 8 + ct:9 + ct]))
        sk = p.counts["S"]
        p.wait("vector", "S", sk)
        p.on("vector", lambda v: v.tensor_tensor(af(7), af(6), af(5), ALU.mult))
        dbc = stat[:, ct:ct + 1].broadcast_to([128, T])
        p.on("vector", lambda v, dbc=dbc: v.tensor_tensor_scan(
            af(9), dbc, af(7), 0.0, ALU.mult, ALU.add))
        p.on("vector", lambda v, dbc=dbc: v.tensor_tensor_scan(
            af(4), dbc, af(6), 0.0, ALU.mult, ALU.add))
        p.on("vector", lambda v: v.tensor_tensor(af(5), af(8), af(5), ALU.mult))
        p.on("vector", lambda v: v.tensor_tensor(
            af(5)[:, 1:T], af(5)[:, 1:T], af(9)[:, 0:T - 1], ALU.add))
        p.on("vector", lambda v: v.tensor_copy(af(6)[:, 0:1], af(8)[:, 0:1]))
        p.on("vector", lambda v: v.tensor_tensor(
            af(6)[:, 1:T], af(8)[:, 1:T], af(4)[:, 0:T - 1], ALU.add))
        p.on("vector", lambda v: v.reciprocal(af(8), af(6)))
        p.on("vector", lambda v: v.tensor_tensor(af(5), af(5), af(8), ALU.mult))
        p.wait("vector", "S")
        p.on("vector", lambda v, ct=ct: v.tensor_tensor(
            rwkv[ct], af(5), rsig[:, 0:T], ALU.mult))
    vwkv = p.counts["V"]

    wout = wb("L0woutT")
    wob = wbuf[:].rearrange("p (b k f) -> p b k f", b=2, k=4)
    p.wait("sync", "T")
    p.wait("sync", "V")
    tks = [p.dma("sync", wob[:, half, kt, :],
                 wout[bass.ts(half * 4 + kt, 128), :])
           for half in range(2) for kt in range(4)]
    p.wait_dma("tensor", tks)
    p.wait("tensor", "V", vwkv)
    for mt in range(8):
        for ch in range(4):
            pst = ps_a if (mt * 4 + ch) % 2 == 0 else ps_b
            if (mt, ch) != (0, 0):
                p.wait("tensor", "V", p.counts["V"])
            for kt in range(8):
                p.on("tensor", lambda t, mt=mt, ch=ch, kt=kt, pst=pst: t.matmul(
                    pst[:, 0:512],
                    wob[:, kt // 4, kt % 4, bass.ts(mt, 128)],
                    ab(kt)[:, bass.ts(ch, 512)],
                    start=(kt == 0), stop=(kt == 7)), inc="T")
            tv = p.counts["T"]
            p.wait("vector", "T", tv)
            p.on("vector", lambda v, mt=mt, ch=ch, pst=pst: v.tensor_tensor(
                xt[mt][:, bass.ts(ch, 512)], xt[mt][:, bass.ts(ch, 512)],
                pst[:, 0:512], ALU.add))
    p.flush("tensor")
    p.mark("vector")


def emit_l0(p, c, parts=("f1", "a", "f2")):
    if "f1" in parts:
        emit_l0_ffn(p, c, 0)
    if "a" in parts:
        emit_l0_attn(p, c)
    if "f2" in parts:
        emit_l0_ffn(p, c, 1)


def emit_l0_ffn(p, c, slot):
    nc, wb, xs, hs = c["nc"], c["wb"], c["xs"], c["hs"]
    af, ab, row = c["af"], c["ab"], c["row"]
    wbuf, smal, stat, cst = c["wbuf"], c["smal"], c["stat"], c["cst"]
    ps_a, ps_b, ps_c, ps_d, ps_big = (c["ps_a"], c["ps_b"], c["ps_c"],
                                      c["ps_d"], c["ps_big"])
    xt, ht = c["xtiles"], c["hstiles"]
    onesb, onesf, onesrow = c["onesb"], c["onesf"], c["onesrow"]

    emit_ln(p, c, xt, ht)
    vln = p.counts["V"]

    for dt in range(DT):
        p.on("vector", lambda v, dt=dt: v.scalar_tensor_tensor(
            xt[dt], ht[dt], 0.5, xt[dt], ALU.mult, ALU.add))

    p.wait("scalar", "V", vln)
    for dt in range(DT):
        p.on("scalar", lambda s, dt=dt: s.activation(af(dt), ht[dt], AF.Square))
    ssq = p.counts["S"]
    p.wait("tensor", "V", vln)
    for tt in range(TT):
        for dt in range(DT):
            p.on("tensor", lambda t, tt=tt, dt=dt: t.matmul(
                ps_c[:, tt:tt + 1], ht[dt][:, bass.ts(tt, 128)], onesb,
                start=(tt == 0 and dt == 0), stop=(tt == TT - 1 and dt == DT - 1)),
                inc="T")
    p.wait("tensor", "S", ssq)
    for tt in range(TT):
        for dt in range(DT):
            p.on("tensor", lambda t, tt=tt, dt=dt: t.matmul(
                ps_d[:, tt:tt + 1], af(dt)[:, bass.ts(tt, 128)], onesf,
                start=(tt == 0 and dt == 0), stop=(tt == TT - 1 and dt == DT - 1)),
                inc="T")
    p.flush("tensor")
    tsum = p.counts["T"]
    p.wait("vector", "T", tsum)
    p.on("vector", lambda v: v.tensor_scalar(
        stat[:, 16:32], ps_c[:, 0:16], 1.0 / D, None, ALU.mult))
    p.on("vector", lambda v: v.tensor_tensor(
        stat[:, 48:64], stat[:, 16:32], stat[:, 16:32], ALU.mult))
    p.on("vector", lambda v: v.scalar_tensor_tensor(
        stat[:, 32:48], ps_d[:, 0:16], 1.0 / D, stat[:, 48:64], ALU.mult,
        ALU.subtract))
    vvar = p.counts["V"]
    p.wait("scalar", "V", vvar)
    p.on("scalar", lambda s: s.activation(
        stat[:, 48:64], stat[:, 32:48], AF.Ln, bias=onesf))
    slv = p.counts["S"]
    p.wait("vector", "S", slv)
    p.on("vector", lambda v: v.tensor_reduce(
        stat[:, 16:17], stat[:, 48:64], axis=AX.X, op=ALU.add))
    vred = p.counts["V"]
    p.wait("tensor", "V", vred)
    p.on("tensor", lambda t: t.matmul(ps_c[0:1, 32:33], stat[:, 16:17], onesf,
                                      start=True, stop=True), inc="T")
    p.flush("tensor")
    tps = p.counts["T"]
    p.wait("vector", "T", tps)
    p.on("vector", lambda v: v.tensor_copy(stat[0:1, 17:18], ps_c[0:1, 32:33]))
    vpart = p.counts["V"]
    p.wait("gpsimd", "V", vpart)
    ar_off = 4096 + slot
    tk = p.dma("gpsimd", c["ar_in"][ar_off:ar_off + 1], stat[0:1, 17:18])
    p.wait_dma("gpsimd", tk)
    if c["n_cores"] > 1:
        p.on("gpsimd", lambda g: g.collective_compute(
            "AllReduce", ALU.add, replica_groups=[list(range(c["n_cores"]))],
            ins=[c["ar_in"][ar_off:ar_off + 1].opt()],
            outs=[c["ar_out"][ar_off:ar_off + 1].opt()]), inc="cc")
        p.wait("gpsimd", "cc")
    else:
        tk = p.dma("gpsimd", c["ar_out"][ar_off:ar_off + 1],
                   c["ar_in"][ar_off:ar_off + 1])
        p.wait_dma("gpsimd", tk)
    tk_ar = p.dma("gpsimd", stat[0:1, 18:19], c["ar_out"][ar_off:ar_off + 1])
    garv = p.mark("gpsimd")

    p.wait("sync", c["wb_ready_g"][0], c["wb_ready_g"][1])
    p.flush("tensor")
    p.wait("sync", "T", p.counts["T"])
    w1t = wb("L0uncw1T")
    tk_w1 = [p.dma("sync", wbuf[:, bass.ts(kt, 256)], w1t[bass.ts(kt, 128), :])
             for kt in range(8)]
    w2t = wb("L0uncw2T")
    tk_w2 = [p.dma("sync", wbuf[:, 2048 + kt:2049 + kt],
                   w2t[bass.ts(kt, 128), :]) for kt in range(2)]
    qpt = wb("L0qprojT")
    tk_qp = [p.dma("sync", wbuf[:, 2304 + kt * 128:2304 + (kt + 1) * 128],
                   qpt[bass.ts(kt, 128), :]) for kt in range(8)]
    tk_rt = p.dma("sync", wbuf[:, 3328:3376], wb("L0routerT"))
    tk_ak = p.dma("sync", wbuf[:, 4096:8192], wb("L0auxkT"))
    p.wait("sync", "T", tsum)
    tk_on = p.dma("sync", ab(0)[48:49, :], wb("L0ones"))
    tk_mk = [p.dma("sync", ab(1 + i)[48:49, :],
                   wb("L0maskrow")[:, bass.ts(i, 2048)]) for i in range(2)]

    p.wait_dma("tensor", tk_w1)
    p.wait("tensor", "V", vvar)
    nch = 0
    for mt in range(2):
        for ch in range(4):
            pst = ps_a if nch % 2 == 0 else ps_b
            if nch >= 2:
                p.wait("tensor", "S", p.counts["S"])
            for kt in range(8):
                p.on("tensor", lambda t, mt=mt, ch=ch, kt=kt, pst=pst: t.matmul(
                    pst[:, 0:512],
                    wbuf[:, kt * 256 + mt * 128:kt * 256 + (mt + 1) * 128],
                    ht[kt][:, bass.ts(ch, 512)],
                    start=(kt == 0), stop=(kt == 7)), inc="T")
            p.flush("tensor")
            tv = p.counts["T"]
            p.wait("scalar", "T", tv)
            p.on("scalar", lambda s, mt=mt, ch=ch, pst=pst: s.activation(
                ab(4 + mt)[:, bass.ts(ch, 512)], pst[:, 0:512], AF.Gelu,
                bias=smal[:, 16 + mt:17 + mt]))
            nch += 1
    sh1 = p.counts["S"]
    p.wait("tensor", "S", sh1)
    p.wait_dma("tensor", tk_w2)
    for tt in range(TT):
        for kt in range(2):
            p.on("tensor", lambda t, tt=tt, kt=kt: t.matmul(
                ps_c[:, 16 + tt:17 + tt], ab(4 + kt)[:, bass.ts(tt, 128)],
                wbuf[:, 2048 + kt:2049 + kt],
                start=(tt == 0 and kt == 0), stop=(tt == TT - 1 and kt == 1)),
                inc="T")
    p.on("tensor", lambda t: t.matmul(ps_d[:, 0:1], onesrow,
                                      smal[0:1, 55:56], start=True, stop=True),
         inc="T")
    p.on("tensor", lambda t: t.matmul(ps_d[:, 1:2], onesrow,
                                      smal[0:1, 56:57], start=True, stop=True),
         inc="T")
    p.on("tensor", lambda t: t.matmul(ps_d[:, 2:3], onesrow,
                                      smal[0:1, 57:58], start=True, stop=True),
         inc="T")
    p.flush("tensor")
    tlearn = p.counts["T"]
    p.wait("vector", "T", tlearn)
    p.on("vector", lambda v: v.tensor_copy(stat[:, 44:47], ps_d[:, 0:3]))
    vb2 = p.counts["V"]
    p.wait("scalar", "V", vb2)
    p.on("scalar", lambda s: s.activation(
        stat[:, 0:16], ps_c[:, 16:32], AF.Sigmoid, bias=stat[:, 44:45]))
    ssig = p.counts["S"]

    p.wait_dma("tensor", tk_qp)
    p.wait("tensor", "S", ssig)
    for ch in range(4):
        pst = ps_a if ch % 2 == 0 else ps_b
        if ch >= 2:
            p.wait("tensor", "V", p.counts["V"])
        for kt in range(8):
            p.on("tensor", lambda t, ch=ch, kt=kt, pst=pst: t.matmul(
                pst[:, 0:512], wbuf[:, 2304 + kt * 128:2304 + (kt + 1) * 128],
                ht[kt][:, bass.ts(ch, 512)],
                start=(kt == 0), stop=(kt == 7)), inc="T")
        p.flush("tensor")
        tv = p.counts["T"]
        p.wait("vector", "T", tv)
        p.on("vector", lambda v, ch=ch, pst=pst: v.tensor_copy(
            ab(3)[:, bass.ts(ch, 512)], pst[:, 0:512]))
    vq = p.counts["V"]
    p.wait_dma("tensor", tk_rt)
    p.wait("tensor", "V", vq)
    for ch in range(4):
        pst = ps_a if ch % 2 == 0 else ps_b
        if ch >= 2:
            p.wait("tensor", "V", p.counts["V"])
        p.on("tensor", lambda t, ch=ch, pst=pst: t.matmul(
            pst[0:48, 0:512], wbuf[:, 3328:3376], ab(3)[:, bass.ts(ch, 512)],
            start=True, stop=True), inc="T")
        p.flush("tensor")
        tv = p.counts["T"]
        p.wait("vector", "T", tv)
        p.on("vector", lambda v, ch=ch, pst=pst: v.tensor_scalar(
            ab(0)[0:48, bass.ts(ch, 512)], pst[0:48, 0:512],
            0.14433756729740643, None, ALU.mult))
    p.wait_dma("tensor", tk_ak)
    for kb in range(8):
        pst = ps_a if kb % 2 == 0 else ps_b
        p.wait("tensor", "V", p.counts["V"])
        p.on("tensor", lambda t, kb=kb, pst=pst: t.matmul(
            pst[0:48, 0:512], wbuf[:, 3328:3376],
            wbuf[:, 4096 + kb * 512:4096 + (kb + 1) * 512],
            start=True, stop=True), inc="T")
        p.flush("tensor")
        tv = p.counts["T"]
        p.wait("vector", "T", tv)
        p.on("vector", lambda v, kb=kb, pst=pst: v.tensor_copy(
            ab(1)[0:48, bass.ts(kb, 512)] if kb < 4 else
            ab(2)[0:48, bass.ts(kb - 4, 512)], pst[0:48, 0:512]))
    vrk = p.counts["V"]
    t_wbuf_free = p.counts["T"]

    p.wait("vector", "G", garv[1])
    p.wait_dma("vector", tk_ar)
    p.on("vector", lambda v: v.tensor_scalar(
        stat[0:1, 18:19], stat[0:1, 18:19], 1.0 / 16384.0, 1e-6, ALU.mult,
        ALU.add))
    p.on("vector", lambda v: v.reciprocal(stat[0:1, 18:19], stat[0:1, 18:19]))
    vrg = p.counts["V"]
    p.wait("tensor", "V", vrg)
    p.on("tensor", lambda t: t.matmul(ps_d[:, 3:4], onesrow,
                                      stat[0:1, 18:19], start=True, stop=True),
         inc="T")
    p.flush("tensor")
    trg = p.counts["T"]
    p.wait("vector", "T", trg)
    p.on("vector", lambda v: v.tensor_copy(stat[:, 47:48], ps_d[:, 3:4]))
    p.on("vector", lambda v: v.scalar_tensor_tensor(
        stat[:, 16:32], stat[:, 48:64], 0.5,
        stat[:, 47:48].broadcast_to([128, 1, 16]).squeeze(1)
        if False else stat[:, 47:48].broadcast_to([128, 16]),
        ALU.mult, ALU.mult))
    p.on("vector", lambda v: v.scalar_tensor_tensor(
        stat[:, 16:32], stat[:, 0:16], 2.5, stat[:, 16:32], ALU.mult, ALU.add))
    p.on("vector", lambda v: v.tensor_scalar(
        stat[:, 16:32], stat[:, 16:32], 0.0, 5.0, ALU.max, ALU.min))
    vunc = p.counts["V"]
    p.wait("scalar", "V", vunc)
    p.on("scalar", lambda s: s.activation(
        stat[:, 0:16], stat[:, 16:32], AF.Sigmoid, bias=stat[:, 46:47],
        scale=stat[:, 45:46]))
    sgate = p.counts["S"]
    p.wait("vector", "S", sgate)

    p.wait_dma("tensor", [tk_on] + tk_mk)
    auxv = wb("L0auxv")
    wT = ab(12, 8).rearrange("p (k t) -> p k t", k=32)
    sc_mark = [p.counts["V"], p.counts["V"]]
    for g4 in range(4):
        for i4 in range(4):
            tt = g4 * 4 + i4
            ps_sc = [[ps_big[:, bass.ts(i, 512)] for i in range(4)],
                     [ps_a[:, 0:512], ps_b[:, 0:512], ps_c[:, 0:512],
                      ps_d[:, 0:512]]]
            for half in range(2):
                if sc_mark[half] > 0:
                    p.wait("tensor", "V", sc_mark[half])
                for kb4 in range(4):
                    kb = half * 4 + kb4
                    p.on("tensor", lambda t, tt=tt, kb=kb, kb4=kb4, half=half:
                         t.matmul(
                             ps_sc[half][kb4],
                             ab(0)[0:49, bass.ts(tt, 128)],
                             (ab(1) if kb < 4 else
                              ab(2))[0:49, bass.ts(kb % 4, 512)],
                             start=True, stop=True), inc="T")
                tv = p.counts["T"]
                p.wait("vector", "T", tv)
                for kb4 in range(4):
                    p.on("vector", lambda v, half=half, kb4=kb4: v.tensor_copy(
                        af(6 + half)[:, bass.ts(kb4, 512)], ps_sc[half][kb4]))
                sc_mark[half] = p.counts["V"]
            p.on("vector", lambda v: v.max(stat[:, 32:40], af(6, 2)))
            p.on("vector", lambda v: v.tensor_scalar(
                stat[:, 40:41], stat[:, 32:33], -1.0, None, ALU.mult))
            ve = p.counts["V"]
            p.wait("scalar", "V", ve)
            p.on("scalar", lambda s: s.activation(af(8), af(6), AF.Exp,
                                                  bias=stat[:, 40:41]))
            p.on("scalar", lambda s: s.activation(af(9), af(7), AF.Exp,
                                                  bias=stat[:, 40:41]))
            se = p.counts["S"]
            p.wait("vector", "S", se)
            p.on("vector", lambda v: v.tensor_scalar(
                af(6), af(6), stat[:, 39:40], None, ALU.is_ge))
            p.on("vector", lambda v: v.tensor_scalar(
                af(7), af(7), stat[:, 39:40], None, ALU.is_ge))
            p.on("vector", lambda v: v.tensor_tensor(af(8), af(8), af(6),
                                                     ALU.mult))
            p.on("vector", lambda v: v.tensor_tensor(af(9), af(9), af(7),
                                                     ALU.mult))
            p.on("vector", lambda v: v.tensor_reduce(
                stat[:, 41:42], af(8, 2), axis=AX.X, op=ALU.add))
            p.on("vector", lambda v: v.reciprocal(stat[:, 41:42],
                                                  stat[:, 41:42]))
            p.on("vector", lambda v, tt=tt: v.scalar_tensor_tensor(
                stat[:, 42:43], stat[:, tt:tt + 1], 0.5, stat[:, 41:42],
                ALU.mult, ALU.mult))
            p.on("vector", lambda v, i4=i4: v.tensor_scalar(
                ab(4 + 2 * i4), af(8), stat[:, 42:43], None, ALU.mult))
            p.on("vector", lambda v, i4=i4: v.tensor_scalar(
                ab(5 + 2 * i4), af(9), stat[:, 42:43], None, ALU.mult))
        vw = p.counts["V"]
        p.wait("tensor", "V", vw)
        pst4 = [ps_a, ps_b, ps_c, ps_d]
        vtm = [0, 0, 0, 0]
        w = 0
        for i4 in range(4):
            for kh in range(4):
                pst = pst4[w % 4][:].bitcast(BF16)
                if vtm[w % 4] > 0:
                    p.wait("tensor", "V", vtm[w % 4])
                for k8 in range(8):
                    kt = kh * 8 + k8
                    p.on("tensor", lambda t, i4=i4, kt=kt, k8=k8, pst=pst:
                         t.transpose(
                             pst[:, k8 * 128:(k8 + 1) * 128],
                             ab(4 + 2 * i4 + kt // 16)[:, bass.ts(kt % 16, 128)],
                             c["idb"]), inc="T")
                tv = p.counts["T"]
                p.wait("vector", "T", tv)
                p.on("vector", lambda v, i4=i4, kh=kh, pst=pst: v.tensor_copy(
                    wT[:, kh * 8:(kh + 1) * 8, bass.ts(i4, 128)],
                    pst.rearrange("p (a b) -> p a b", a=8)))
                vtm[w % 4] = p.counts["V"]
                w += 1
        vtr = p.counts["V"]
        p.wait("tensor", "V", vtr)
        psum_of = [ps_big[:, 0:512], ps_big[:, 512:1024], ps_big[:, 1024:1536],
                   ps_big[:, 1536:2048], ps_a[:, 0:512], ps_b[:, 0:512],
                   ps_c[:, 0:512], ps_d[:, 0:512]]
        def agg_load(hw):
            buf = hw % 2
            mk = c.get("l0agg_mark", {}).get(buf, 0)
            if mk > 0:
                p.wait("sync", "T", mk)
                p.wait("gpsimd", "T", mk)
            return [p.dma("sync" if kt < 2 else "gpsimd",
                          wbuf[:, buf * 4096 + kt * 1024:
                               buf * 4096 + (kt + 1) * 1024],
                          auxv[bass.ts(hw * 4 + kt, 128), :])
                    for kt in range(4)]
        if g4 == 0:
            c["l0agg_mark"] = {0: t_wbuf_free, 1: t_wbuf_free}
        tks = {0: agg_load(0), 1: agg_load(1)}
        for hw in range(8):
            buf = hw % 2
            p.wait_dma("tensor", tks.pop(hw))
            for dt in range(DT):
                for kt in range(4):
                    p.on("tensor", lambda t, dt=dt, kt=kt, hw=hw, buf=buf:
                         t.matmul(
                             psum_of[dt],
                             wbuf[:, buf * 4096 + kt * 1024 + dt * 128:
                                  buf * 4096 + kt * 1024 + (dt + 1) * 128],
                             wT[:, hw * 4 + kt, :],
                             start=(hw == 0 and kt == 0),
                             stop=(hw == 7 and kt == 3)), inc="T")
            c["l0agg_mark"][buf] = p.counts["T"]
            if hw + 2 < 8:
                tks[hw + 2] = agg_load(hw + 2)
        tagg = p.counts["T"]
        p.wait("vector", "T", tagg)
        for dt in range(DT):
            p.on("vector", lambda v, dt=dt, g4=g4: v.tensor_tensor(
                xt[dt][:, bass.ts(g4, 512)], xt[dt][:, bass.ts(g4, 512)],
                psum_of[dt], ALU.add))
        if g4 < 3:
            p.wait("tensor", "V", p.counts["V"])
    p.flush("tensor")
    p.mark("vector")


def emit_l1_ffn(p, c, slot):
    nc, wb, xs, hs = c["nc"], c["wb"], c["xs"], c["hs"]
    af, ab, row = c["af"], c["ab"], c["row"]
    wbuf, smal, stat, cst = c["wbuf"], c["smal"], c["stat"], c["cst"]
    ps_a, ps_b, ps_c, ps_d, ps_big = (c["ps_a"], c["ps_b"], c["ps_c"],
                                      c["ps_d"], c["ps_big"])
    xt, ht = c["xtiles"], c["hstiles"]
    onesb, onesf, onesrow = c["onesb"], c["onesf"], c["onesrow"]

    emit_ln(p, c, xt, ht)
    vln = p.counts["V"]

    p.wait("sync", c["wb_ready_g"][0], c["wb_ready_g"][1])
    p.flush("tensor")
    p.wait("sync", "T", p.counts["T"])
    tk_rt = p.dma("sync", wbuf[:, 0:56].rearrange("p (k f) -> p k f", k=8),
                  wb("L1routerT").rearrange("(k p) f -> p k f", p=128))
    p.on("vector", lambda v: v.memset(smal[:, 224:352], -1.0e38))
    vpad = p.counts["V"]
    p.wait_dma("tensor", tk_rt)
    p.wait("tensor", "V", vln)
    for tt in range(TT):
        for kt in range(8):
            p.on("tensor", lambda t, tt=tt, kt=kt: t.matmul(
                ps_c[:, tt * 7:(tt + 1) * 7], ht[kt][:, bass.ts(tt, 128)],
                wbuf[:, kt * 7:(kt + 1) * 7],
                start=(tt == 0 and kt == 0), stop=(tt == TT - 1 and kt == 7)),
                inc="T")
    p.flush("tensor")
    tlg = p.counts["T"]
    p.wait("vector", "T", tlg)
    p.wait("vector", "V", vpad)
    for tt in range(TT):
        p.on("vector", lambda v, tt=tt: v.tensor_copy(
            smal[:, 224 + tt * 8:224 + tt * 8 + 7],
            ps_c[:, tt * 7:(tt + 1) * 7]))
    for tt in range(TT):
        lg = smal[:, 224 + tt * 8:224 + tt * 8 + 8]
        p.on("vector", lambda v, lg=lg: v.max(stat[:, 0:8], lg))
        p.on("vector", lambda v: v.tensor_scalar(
            stat[:, 8:9], stat[:, 0:1], -1.0, None, ALU.mult))
        p.on("vector", lambda v, lg=lg: v.tensor_scalar(
            stat[:, 16:24], lg, stat[:, 1:2], None, ALU.is_ge))
        vge = p.counts["V"]
        p.wait("scalar", "V", vge)
        p.on("scalar", lambda s: s.activation(
            stat[:, 9:10], stat[:, 1:2], AF.Exp, bias=stat[:, 8:9]))
        p.on("scalar", lambda s, lg=lg: s.activation(
            stat[:, 24:32], lg, AF.Exp, bias=stat[:, 8:9]))
        sexp = p.counts["S"]
        p.wait("vector", "S", sexp)
        p.on("vector", lambda v: v.tensor_scalar(
            stat[:, 9:10], stat[:, 9:10], 1.0, None, ALU.add))
        p.on("vector", lambda v: v.reciprocal(stat[:, 9:10], stat[:, 9:10]))
        p.on("vector", lambda v: v.tensor_tensor(
            stat[:, 24:32], stat[:, 24:32], stat[:, 16:24], ALU.mult))
        p.on("vector", lambda v, lg=lg: v.tensor_scalar(
            lg, stat[:, 24:32], stat[:, 9:10], None, ALU.mult))
    vgates = p.counts["V"]
    p.wait("tensor", "V", vgates)
    prev_tk = None
    for e in range(7):
        gcol = smal[:, 224 + e:224 + e + 121:8].rearrange("p f -> p f")
        pst = ps_a if e % 2 == 0 else ps_b
        if e >= 2:
            p.wait("tensor", "V", p.counts["V"])
        p.on("tensor", lambda t, gcol=gcol, pst=pst: t.transpose(
            pst[0:16, 0:128], gcol, c["idf"]), inc="T")
        p.flush("tensor")
        tv = p.counts["T"]
        p.wait("vector", "T", tv)
        if prev_tk is not None:
            p.wait_dma("vector", prev_tk)
        p.on("vector", lambda v, pst=pst: v.tensor_copy(
            smal[0:16, 352:480], pst[0:16, 0:128]))
        vv = p.counts["V"]
        p.wait("gpsimd", "V", vv)
        prev_tk = p.dma("gpsimd", c["grow"][e:e + 1, :].rearrange(
            "a (b f) -> (a b) f", b=16), smal[0:16, 352:480])
    p.wait_dma("gpsimd", prev_tk)
    gmark = p.mark("gpsimd")

    wvgu = wbuf[:, 0:6144].rearrange("p (b f) -> p b f", b=3)
    wvd = wbuf[:, 6144:8192].rearrange("p (b f) -> p b f", b=4)
    hslot = lambda hq: c["arena"][:, hq * 512:(hq + 1) * 512]
    crep = af(8)[:, 0:512]
    gsus = ab(17)
    psum_mix = [ps_big[:, bass.ts(i, 512)] for i in range(4)]
    qname = ("sync", "gpsimd", "scalar")

    st = c.setdefault("l1state", {
        "gu": [0, 0, 0], "wd": [0, 0, 0, 0],
        "s": [0, 0], "v": [0, 0],
        "crep": 0, "vmix2": 0})

    def load_gu(j, wgt, wut):
        buf = j % 3
        if st["gu"][buf] > 0:
            p.wait("sync", "T", st["gu"][buf])
            p.wait("gpsimd", "T", st["gu"][buf])
        return [p.dma("sync", wvgu[:, buf, 0:1024],
                      wgt[j].rearrange("a b c -> a (b c)")),
                p.dma("gpsimd", wvgu[:, buf, 1024:2048],
                      wut[j].rearrange("a b c -> a (b c)"))]

    def load_d(j, wdt, half, q):
        buf = j % 4
        if st["wd"][buf] > 0:
            p.wait(q, "T", st["wd"][buf])
        return [p.dma(q, wvd[:, buf, 0:512],
                      wdt[bass.ts(j, 128), half * 512:(half + 1) * 512])]

    p.wait("sync", "G", gmark[1])
    for e in range(8):
        routed = e < 7
        wgt = wb("L1wg", e) if routed else wb("L1swg")
        wut = wb("L1wu", e) if routed else wb("L1swu")
        wdt = wb("L1wd", e) if routed else wb("L1swd")
        for c4 in range(4):
            if routed:
                if st["crep"] > 0:
                    p.wait("sync", "T", st["crep"])
                tk = p.dma("sync", smal[0:1, 512:1024],
                           c["grow"][e:e + 1, bass.ts(c4, 512)])
                p.wait_dma("tensor", tk)
                if st["v"][0] > 0:
                    p.wait("tensor", "V", st["v"][0])
                p.on("tensor", lambda t: t.matmul(
                    ps_d[:, 0:512], onesrow, smal[0:1, 512:1024],
                    start=True, stop=True), inc="T")
                st["crep"] = p.counts["T"]
                p.wait("vector", "T", st["crep"])
                p.on("vector", lambda v: v.tensor_copy(crep, ps_d[:, 0:512]))
                p.wait("tensor", "V", p.counts["V"])
            tks_gu = {j: load_gu(j, wgt, wut) for j in range(3)}
            tks_d = {j: load_d(j, wdt, 0, qname[2]) for j in range(4)}
            vh = {}
            for hq in range(32):
                buf, par = hq % 3, hq % 2
                pst_g = [ps_c, ps_a][par]
                pst_u = [ps_d, ps_b][par]
                p.wait_dma("tensor", tks_gu.pop(hq) + tks_d.pop(hq))
                if st["s"][par] > 0:
                    p.wait("tensor", "S", st["s"][par])
                if st["v"][par] > 0:
                    p.wait("tensor", "V", st["v"][par])
                for kt in range(8):
                    p.on("tensor", lambda t, buf=buf, kt=kt, c4=c4, pst_g=pst_g:
                         t.matmul(pst_g[:, 0:512],
                                  wvgu[:, buf, kt * 128:(kt + 1) * 128],
                                  ht[kt][:, bass.ts(c4, 512)],
                                  start=(kt == 0), stop=(kt == 7)), inc="T")
                tgv = p.counts["T"]
                for kt in range(8):
                    p.on("tensor", lambda t, buf=buf, kt=kt, c4=c4, pst_u=pst_u:
                         t.matmul(pst_u[:, 0:512],
                                  wvgu[:, buf, 1024 + kt * 128:
                                       1024 + (kt + 1) * 128],
                                  ht[kt][:, bass.ts(c4, 512)],
                                  start=(kt == 0), stop=(kt == 7)), inc="T")
                st["gu"][buf] = p.counts["T"]
                p.wait("scalar", "T", tgv)
                p.on("scalar", lambda s, par=par, pst_g=pst_g: s.activation(
                    gsus[:, par * 1024:par * 1024 + 512], pst_g[:, 0:512],
                    AF.Silu))
                st["s"][par] = p.counts["S"]
                p.wait("vector", "T", st["gu"][buf])
                if routed:
                    p.on("vector", lambda v, par=par, pst_u=pst_u:
                         v.tensor_tensor(
                             gsus[:, par * 1024 + 512:par * 1024 + 1024],
                             pst_u[:, 0:512], crep, ALU.mult))
                else:
                    p.on("vector", lambda v, par=par, pst_u=pst_u:
                         v.tensor_copy(
                             gsus[:, par * 1024 + 512:par * 1024 + 1024],
                             pst_u[:, 0:512]))
                st["v"][par] = p.counts["V"]
                p.wait("vector", "S", st["s"][par])
                p.on("vector", lambda v, hq=hq, par=par: v.tensor_tensor(
                    hslot(hq), gsus[:, par * 1024:par * 1024 + 512],
                    gsus[:, par * 1024 + 512:par * 1024 + 1024], ALU.mult))
                vh[hq] = p.counts["V"]
                if hq >= 1:
                    if hq == 1 and st["vmix2"] > 0:
                        p.wait("tensor", "V", st["vmix2"])
                    p.wait("tensor", "V", vh[hq - 1])
                    for dt in range(4):
                        p.on("tensor", lambda t, dt=dt, bufm=(hq - 1) % 4,
                             hq=hq: t.matmul(
                                 psum_mix[dt], wvd[:, bufm, dt * 128:
                                                  (dt + 1) * 128],
                                 hslot(hq - 1), start=(hq - 1 == 0),
                                 stop=False), inc="T")
                    st["wd"][(hq - 1) % 4] = p.counts["T"]
                if hq + 3 < 32:
                    tks_gu[hq + 3] = load_gu(hq + 3, wgt, wut)
                    if hq >= 1:
                        tks_d[hq + 3] = load_d(hq + 3, wdt, 0, qname[2])
            p.wait("tensor", "V", vh[31])
            for dt in range(4):
                p.on("tensor", lambda t, dt=dt: t.matmul(
                    psum_mix[dt], wvd[:, 31 % 4, dt * 128:(dt + 1) * 128],
                    hslot(31), start=False, stop=True), inc="T")
            st["wd"][31 % 4] = p.counts["T"]
            tmix = p.counts["T"]
            p.wait("vector", "T", tmix)
            for dt in range(4):
                p.on("vector", lambda v, dt=dt, c4=c4: v.scalar_tensor_tensor(
                    xt[dt][:, bass.ts(c4, 512)], psum_mix[dt], 0.5,
                    xt[dt][:, bass.ts(c4, 512)], ALU.mult, ALU.add))
            vmix = p.counts["V"]
            tks2 = {j: load_d(j, wdt, 1, qname[j % 3]) for j in range(4)}
            for hq in range(32):
                p.wait_dma("tensor", tks2.pop(hq))
                if hq == 0:
                    p.wait("tensor", "V", vmix)
                for dt in range(4):
                    p.on("tensor", lambda t, dt=dt, hq=hq: t.matmul(
                        psum_mix[dt], wvd[:, hq % 4, dt * 128:(dt + 1) * 128],
                        hslot(hq), start=(hq == 0), stop=(hq == 31)), inc="T")
                st["wd"][hq % 4] = p.counts["T"]
                if hq + 4 < 32:
                    tks2[hq + 4] = load_d(hq + 4, wdt, 1, qname[(hq + 4) % 3])
            tmix2 = p.counts["T"]
            p.wait("vector", "T", tmix2)
            for dt in range(4):
                p.on("vector", lambda v, dt=dt, c4=c4: v.scalar_tensor_tensor(
                    xt[4 + dt][:, bass.ts(c4, 512)], psum_mix[dt], 0.5,
                    xt[4 + dt][:, bass.ts(c4, 512)], ALU.mult, ALU.add))
            st["vmix2"] = p.counts["V"]
    p.wait("tensor", "V", p.counts["V"])

    p.flush("tensor")
    p.mark("vector")


def emit_l1(p, c, parts=("f1", "a", "f2")):
    if "f1" in parts:
        emit_l1_ffn(p, c, 0)
    if "a" in parts:
        emit_l1_attn(p, c)
    if "f2" in parts:
        emit_l1_ffn(p, c, 1)


def emit_l1_attn(p, c):
    nc, wb, xs, hs = c["nc"], c["wb"], c["xs"], c["hs"]
    af, ab, row = c["af"], c["ab"], c["row"]
    wbuf, smal, stat, cst = c["wbuf"], c["smal"], c["stat"], c["cst"]
    ps_a, ps_b, ps_c, ps_d, ps_big = (c["ps_a"], c["ps_b"], c["ps_c"],
                                      c["ps_d"], c["ps_big"])
    xt, ht = c["xtiles"], c["hstiles"]

    emit_ln(p, c, xt, ht)
    vln = p.counts["V"]

    p.wait("gpsimd", "V", vln)
    p.on("gpsimd", lambda g: g.iota(af(6)[:, 0:256], pattern=[[1, 256]],
                                    base=0, channel_multiplier=-1,
                                    allow_small_or_imprecise_dtypes=True))
    gio = p.mark("gpsimd")
    p.wait("vector", "G", gio[1])
    p.on("vector", lambda v: v.tensor_scalar(
        smal[:, 224:480], af(6)[:, 0:256], 0.0, None, ALU.is_ge))
    p.on("vector", lambda v: v.tensor_scalar(
        af(6)[:, 256:512], af(6)[:, 0:256], 128.0, None, ALU.is_le))
    p.on("vector", lambda v: v.tensor_tensor(
        smal[:, 224:480], smal[:, 224:480], af(6)[:, 256:512], ALU.mult))
    p.on("vector", lambda v: v.tensor_scalar(
        smal[:, 480:736], af(6)[:, 0:256], 0.0, None, ALU.is_le))
    p.on("vector", lambda v: v.tensor_scalar(
        af(6)[:, 256:512], af(6)[:, 0:256], -128.0, None, ALU.is_ge))
    p.on("vector", lambda v: v.tensor_tensor(
        smal[:, 480:736], smal[:, 480:736], af(6)[:, 256:512], ALU.mult))
    mask1 = smal[:, 224:480]
    mask0 = smal[:, 480:736]

    p.wait("sync", c["wb_ready_g"][0], c["wb_ready_g"][1])
    wq = wb("L1wqkvT")
    wo = wb("L1woutT")
    wvb = wbuf[:, 0:6144].rearrange("p (b k f) -> p b k f", b=2, k=8)
    sc = af(6)[:, 0:256]
    scg = af(6)[:, 256:272]
    eb = ab(13)[:, 0:256]
    ebT = ab(13)[:, 256:512].rearrange("p (a b) -> p a b", a=2)
    vT = ab(13)[:, 512:640].rearrange("p (a b) -> p a b", a=2)
    ag = ab(13)[:, 640:656]
    agT = ab(13)[0:16, 656:784]
    vgT = ab(14)[0:16, 0:512]

    for half in range(2):
        p.flush("tensor")
        p.wait("sync", "T", p.counts["T"])
        p.wait("tensor", "V", vln if half == 0 else p.counts["V"])
        last_mm = {}
        for g3 in range(4):
            buf = g3 % 2
            if g3 >= 2:
                p.wait("sync", "T", last_mm[buf])
            tks = [p.dma("sync", wvb[:, buf, kt, :],
                         wq[bass.ts(kt, 128),
                            half * 1536 + g3 * 384:half * 1536 + (g3 + 1) * 384])
                   for kt in range(8)]
            p.wait_dma("tensor", tks)
            for m3 in range(3):
                mt = g3 * 3 + m3
                for ch in range(4):
                    pst = ps_a if (m3 * 4 + ch) % 2 == 0 else ps_b
                    if (g3, m3, ch) != (0, 0, 0):
                        p.wait("tensor", "V", p.counts["V"])
                    for kt in range(8):
                        p.on("tensor", lambda t, buf=buf, kt=kt, m3=m3, ch=ch,
                             pst=pst: t.matmul(
                                 pst[:, 0:512],
                                 wvb[:, buf, kt, m3 * 128:(m3 + 1) * 128],
                                 ht[kt][:, bass.ts(ch, 512)],
                                 start=(kt == 0), stop=(kt == 7)), inc="T")
                    p.flush("tensor")
                    tv = p.counts["T"]
                    p.wait("vector", "T", tv)
                    p.on("vector", lambda v, mt=mt, ch=ch, pst=pst: v.tensor_copy(
                        ab(mt)[:, bass.ts(ch, 512)], pst[:, 0:512]))
            last_mm[buf] = p.counts["T"]
        vqkv = p.counts["V"]

        p.wait("tensor", "V", vqkv)
        for hh in range(8):
            po_ = 64 * (hh % 2)
            vslc = ab(8 + hh // 2)[po_:po_ + 64, 0:1921:128]
            idslc = c["idb"][po_:po_ + 64, po_:po_ + 64]
            pst = (ps_c if hh % 2 == 0 else ps_d)[:].bitcast(BF16)
            if hh >= 2:
                p.wait("tensor", "V", p.counts["V"])
            p.on("tensor", lambda t, vslc=vslc, pst=pst, idslc=idslc: t.transpose(
                pst[0:16, 0:64], vslc, idslc), inc="T")
            p.flush("tensor")
            tv = p.counts["T"]
            p.wait("vector", "T", tv)
            p.on("vector", lambda v, hh=hh, pst=pst: v.tensor_copy(
                vgT[:, hh * 64:(hh + 1) * 64], pst[0:16, 0:64]))
        vvg = p.counts["V"]
        p.wait("tensor", "V", vvg)

        for it in range(TT):
            i0 = it * 128
            wstart = 0 if it == 0 else i0 - 128
            msk = mask0 if it == 0 else mask1
            for dh in range(4):
                for hh2 in range(2):
                    hh = 2 * dh + hh2
                    po = 64 * hh2
                    qslc = ab(dh)
                    kslc = ab(4 + dh)
                    vslc = ab(8 + dh)
                    qs = qslc[po:po + 64, :]
                    ks = kslc[po:po + 64, :]
                    vs = vslc[po:po + 64, :]
                    p.wait("tensor", "V", p.counts["V"])
                    p.wait("tensor", "S", p.counts["S"])
                    p.on("tensor", lambda t, qs=qs, ks=ks, i0=i0,
                         wstart=wstart: t.matmul(
                             ps_a[:, 0:256], qs[:, i0:i0 + 128],
                             ks[:, wstart:wstart + 256], start=True, stop=True),
                         inc="T")
                    p.on("tensor", lambda t, qs=qs, ks=ks, i0=i0: t.matmul(
                        ps_a[:, 256:272], qs[:, i0:i0 + 128],
                        ks[:, 0:1921:128], start=True, stop=True), inc="T")
                    p.flush("tensor")
                    tsc = p.counts["T"]
                    p.wait("vector", "T", tsc)
                    p.on("vector", lambda v, msk=msk: v.scalar_tensor_tensor(
                        sc, ps_a[:, 0:256], 0.125, msk, ALU.mult, ALU.mult))
                    p.on("vector", lambda v: v.tensor_scalar(
                        scg, ps_a[:, 256:272], 0.125, None, ALU.mult))
                    p.on("vector", lambda v: v.tensor_reduce(
                        stat[:, 0:1], sc, axis=AX.X, op=ALU.max))
                    p.on("vector", lambda v: v.tensor_scalar(
                        stat[:, 1:2], stat[:, 0:1], -1.0, None, ALU.mult))
                    p.on("vector", lambda v: v.tensor_reduce(
                        stat[:, 3:4], scg, axis=AX.X, op=ALU.max))
                    p.on("vector", lambda v: v.tensor_scalar(
                        stat[:, 4:5], stat[:, 3:4], -1.0, None, ALU.mult))
                    vm = p.counts["V"]
                    p.wait("scalar", "V", vm)
                    p.on("scalar", lambda s: s.activation(
                        sc, sc, AF.Exp, bias=stat[:, 1:2]))
                    p.on("scalar", lambda s: s.activation(
                        scg, scg, AF.Exp, bias=stat[:, 4:5]))
                    sel = p.counts["S"]
                    p.wait("vector", "S", sel)
                    p.on("vector", lambda v, msk=msk: v.tensor_tensor(
                        sc, sc, msk, ALU.mult))
                    p.on("vector", lambda v: v.tensor_reduce(
                        stat[:, 2:3], sc, axis=AX.X, op=ALU.add))
                    p.on("vector", lambda v: v.reciprocal(stat[:, 2:3],
                                                          stat[:, 2:3]))
                    p.on("vector", lambda v: v.tensor_scalar(
                        eb, sc, stat[:, 2:3], None, ALU.mult))
                    p.on("vector", lambda v: v.tensor_reduce(
                        stat[:, 5:6], scg, axis=AX.X, op=ALU.add))
                    p.on("vector", lambda v: v.reciprocal(stat[:, 5:6],
                                                          stat[:, 5:6]))
                    p.on("vector", lambda v: v.tensor_scalar(
                        ag, scg, stat[:, 5:6], None, ALU.mult))
                    vsm = p.counts["V"]
                    p.wait("tensor", "V", vsm)
                    for jt in range(2):
                        pst = (ps_b if jt == 0 else ps_c)[:].bitcast(BF16)
                        p.on("tensor", lambda t, jt=jt, pst=pst: t.transpose(
                            pst[0:128, 0:128], eb[:, bass.ts(jt, 128)],
                            c["idb"]), inc="T")
                        idslc = c["idb"][po:po + 64, po:po + 64]
                        p.on("tensor", lambda t, jt=jt, pst=pst, vs=vs,
                             wstart=wstart, idslc=idslc: t.transpose(
                                 pst[0:128, 128:192],
                                 vs[:, wstart + jt * 128:wstart + (jt + 1) * 128],
                                 idslc), inc="T")
                    p.on("tensor", lambda t: t.transpose(
                        ps_d[:, 0:512].bitcast(BF16)[0:16, 0:128], ag,
                        c["idb"]), inc="T")
                    p.flush("tensor")
                    ttr = p.counts["T"]
                    p.wait("vector", "T", ttr)
                    for jt in range(2):
                        pst = (ps_b if jt == 0 else ps_c)[:].bitcast(BF16)
                        p.on("vector", lambda v, jt=jt, pst=pst: v.tensor_copy(
                            ebT[:, jt, :], pst[0:128, 0:128]))
                        p.on("vector", lambda v, jt=jt, pst=pst: v.tensor_copy(
                            vT[:, jt, :], pst[0:128, 128:192]))
                    p.on("vector", lambda v: v.tensor_copy(
                        agT, ps_d[:, 0:512].bitcast(BF16)[0:16, 0:128]))
                    vtr = p.counts["V"]
                    p.wait("tensor", "V", vtr)
                    for jt in range(2):
                        p.on("tensor", lambda t, jt=jt, po=po: t.matmul(
                            ps_big[po:po + 64, 0:128], vT[:, jt, :],
                            ebT[:, jt, :], start=(jt == 0), stop=False),
                            inc="T")
                    p.on("tensor", lambda t, hh=hh, po=po: t.matmul(
                        ps_big[po:po + 64, 0:128],
                        vgT[:, hh * 64:(hh + 1) * 64], agT,
                        start=False, stop=True), inc="T")
                p.flush("tensor")
                tav = p.counts["T"]
                p.wait("vector", "T", tav)
                p.on("vector", lambda v, dh=dh, it=it: v.tensor_copy(
                    ab(16 + dh)[:, bass.ts(it, 128)], ps_big[:, 0:128]))
        vattn = p.counts["V"]

        p.flush("tensor")
        p.wait("sync", "T", p.counts["T"])
        tks = [p.dma("sync", wvb[:, 0, kt, :].rearrange("p (a b) -> p a b", a=2)
                     if False else wbuf[:, kt * 1024:(kt + 1) * 1024],
                     wo[bass.ts(half * 4 + kt, 128), :]) for kt in range(4)]
        p.wait_dma("tensor", tks)
        p.wait("tensor", "V", vattn)
        for mt in range(8):
            for ch in range(4):
                pst = ps_a if (mt * 4 + ch) % 2 == 0 else ps_b
                p.wait("tensor", "V", p.counts["V"])
                for kt in range(4):
                    p.on("tensor", lambda t, kt=kt, mt=mt, ch=ch, pst=pst:
                         t.matmul(
                             pst[:, 0:512],
                             wbuf[:, kt * 1024 + mt * 128:kt * 1024 + (mt + 1) * 128],
                             ab(16 + kt)[:, bass.ts(ch, 512)],
                             start=(kt == 0), stop=(kt == 3)), inc="T")
                p.flush("tensor")
                tv = p.counts["T"]
                p.wait("vector", "T", tv)
                if half == 0:
                    p.on("vector", lambda v, mt=mt, ch=ch, pst=pst: v.tensor_tensor(
                        xt[mt][:, bass.ts(ch, 512)], xt[mt][:, bass.ts(ch, 512)],
                        pst[:, 0:512], ALU.add))
                else:
                    p.on("vector", lambda v, mt=mt, ch=ch, pst=pst:
                         v.scalar_tensor_tensor(
                             xt[mt][:, bass.ts(ch, 512)], pst[:, 0:512],
                             smal[:, 18 + mt:19 + mt],
                             xt[mt][:, bass.ts(ch, 512)], ALU.add, ALU.add))
    p.flush("tensor")
    p.mark("vector")


def emit_xallgather(p, c, to_seq):
    xs, xg_in, xg_out = c["xs"], c["xg_in"], c["xg_out"]
    n = c["n_cores"]
    assert n == NCORES, "layer 2 requires the full 8-core mesh"
    p.barrier()
    tk = p.dma("gpsimd", xg_in.rearrange("(a p t) -> p a t", p=128, a=DT), xs[:])
    p.wait_dma("gpsimd", tk)
    p.on("gpsimd", lambda g: g.collective_compute(
        "AllGather", ALU.bypass, replica_groups=[list(range(n))],
        ins=[xg_in.opt()], outs=[xg_out.opt()]), inc="cc")
    p.wait("gpsimd", "cc")
    gm = p.mark("gpsimd")
    p.wait("sync", "G", gm[1])
    xg6 = xg_out.rearrange("(r a p c t) -> c r a p t", r=n, a=DT, p=128, c=8)
    gixcol = c["gix"][:, 1:2]
    xgrows = xg_out.rearrange("(n t) -> n t", t=256)
    tks = []
    for r in range(n):
        for a in range(DT):
            eo = (r * 8192 + a * 1024) * 256
            def mk(eng, r=r, a=a, eo=eo, gixcol=gixcol, xgrows=xgrows, xs=xs):
                return eng.indirect_dma_start(
                    out=xs[:, a, r * 256:(r + 1) * 256], out_offset=None,
                    in_=xgrows, element_offset=eo,
                    in_offset=bass.IndirectOffsetOnAxis(gixcol, 0))
            tks.append(p.dma2("gpsimd", mk))
    p.wait_dma("gpsimd", tks)
    gm2 = p.mark("gpsimd")
    for e in ("tensor", "vector", "scalar", "sync"):
        p.wait(e, "G", gm2[1])


def emit_l2_ffn(p, c):
    nc, wb, xs, hs = c["nc"], c["wb"], c["xs"], c["hs"]
    af, ab, row = c["af"], c["ab"], c["row"]
    wbuf, smal, stat = c["wbuf"], c["smal"], c["stat"]
    ps_a, ps_b, ps_c, ps_d, ps_big = (c["ps_a"], c["ps_b"], c["ps_c"],
                                      c["ps_d"], c["ps_big"])
    xt, ht = c["xtiles"], c["hstiles"]
    onesb, onesf, onesrow = c["onesb"], c["onesf"], c["onesrow"]

    emit_ln(p, c, xt, ht)
    vln = p.counts["V"]

    p.wait("sync", c["wb_ready_g"][0], c["wb_ready_g"][1])
    p.flush("tensor")
    p.wait("sync", "T", p.counts["T"])
    tk_g = p.dma("sync", wbuf[:, 0:64].rearrange("p (k f) -> p k f", k=8),
                 wb("L2wgate").rearrange("(k p) f -> p k f", p=128))
    p.wait_dma("tensor", tk_g)
    p.wait("tensor", "V", vln)
    for tt in range(TT):
        for kt in range(8):
            p.on("tensor", lambda t, tt=tt, kt=kt: t.matmul(
                ps_c[:, tt * 8:(tt + 1) * 8], ht[kt][:, bass.ts(tt, 128)],
                wbuf[:, kt * 8:(kt + 1) * 8],
                start=(tt == 0 and kt == 0), stop=(tt == TT - 1 and kt == 7)),
                inc="T")
    p.flush("tensor")
    tlg = p.counts["T"]
    p.wait("vector", "T", tlg)
    for tt in range(TT):
        p.on("vector", lambda v, tt=tt: v.tensor_copy(
            smal[:, 224 + tt * 8:232 + tt * 8], ps_c[:, tt * 8:(tt + 1) * 8]))
    for tt in range(TT):
        lg = smal[:, 224 + tt * 8:232 + tt * 8]
        p.on("vector", lambda v, lg=lg: v.max(stat[:, 0:8], lg))
        p.on("vector", lambda v: v.tensor_scalar(
            stat[:, 8:9], stat[:, 0:1], -1.0, None, ALU.mult))
        p.on("vector", lambda v, lg=lg: v.tensor_scalar(
            stat[:, 16:24], lg, stat[:, 1:2], None, ALU.is_ge))
        vge = p.counts["V"]
        p.wait("scalar", "V", vge)
        p.on("scalar", lambda s: s.activation(
            stat[:, 9:10], stat[:, 1:2], AF.Exp, bias=stat[:, 8:9]))
        p.on("scalar", lambda s, lg=lg: s.activation(
            stat[:, 24:32], lg, AF.Exp, bias=stat[:, 8:9]))
        sexp = p.counts["S"]
        p.wait("vector", "S", sexp)
        p.on("vector", lambda v: v.tensor_scalar(
            stat[:, 9:10], stat[:, 9:10], 1.0, None, ALU.add))
        p.on("vector", lambda v: v.reciprocal(stat[:, 9:10], stat[:, 9:10]))
        p.on("vector", lambda v: v.tensor_tensor(
            stat[:, 24:32], stat[:, 24:32], stat[:, 16:24], ALU.mult))
        p.on("vector", lambda v: v.tensor_scalar(
            stat[:, 24:32], stat[:, 24:32], stat[:, 9:10], None, ALU.mult))
        p.on("vector", lambda v, lg=lg: v.tensor_scalar(
            lg, stat[:, 24:32], 0.25, None, ALU.mult))
    vgates = p.counts["V"]
    p.wait("tensor", "V", vgates)
    prev_tk = None
    for e in range(8):
        gcol = smal[:, 224 + e:224 + e + 121:8]
        pst = ps_a if e % 2 == 0 else ps_b
        if e >= 2:
            p.wait("tensor", "V", p.counts["V"])
        p.on("tensor", lambda t, gcol=gcol, pst=pst: t.transpose(
            pst[0:16, 0:128], gcol, c["idf"]), inc="T")
        p.flush("tensor")
        tv = p.counts["T"]
        p.wait("vector", "T", tv)
        if prev_tk is not None:
            p.wait_dma("vector", prev_tk)
        p.on("vector", lambda v, pst=pst: v.tensor_copy(
            smal[0:16, 352:480], pst[0:16, 0:128]))
        vv = p.counts["V"]
        p.wait("gpsimd", "V", vv)
        prev_tk = p.dma("gpsimd", c["grow"][e:e + 1, :].rearrange(
            "a (b f) -> (a b) f", b=16), smal[0:16, 352:480])
    p.wait_dma("gpsimd", prev_tk)
    gmark = p.mark("gpsimd")
    p.wait("sync", "G", gmark[1])

    p.flush("tensor")
    p.wait("sync", "T", p.counts["T"])
    tks = [p.dma("sync", wbuf[:, kt * 512:(kt + 1) * 512],
                 wb("L2wd")[bass.ts(kt, 128), :]) for kt in range(8)]
    p.wait_dma("tensor", tks)
    for mt in range(4):
        for ch in range(4):
            pst = ps_a if (mt * 4 + ch) % 2 == 0 else ps_b
            p.wait("tensor", "S", p.counts["S"])
            for kt in range(8):
                p.on("tensor", lambda t, kt=kt, mt=mt, ch=ch, pst=pst: t.matmul(
                    pst[:, 0:512],
                    wbuf[:, kt * 512 + mt * 128:kt * 512 + (mt + 1) * 128],
                    ht[kt][:, bass.ts(ch, 512)],
                    start=(kt == 0), stop=(kt == 7)), inc="T")
            p.flush("tensor")
            tv = p.counts["T"]
            p.wait("scalar", "T", tv)
            p.on("scalar", lambda s, mt=mt, ch=ch, pst=pst: s.activation(
                ab(mt)[:, bass.ts(ch, 512)], pst[:, 0:512], AF.Relu,
                bias=smal[:, 42 + mt:43 + mt]))
    sh = p.counts["S"]
    p.wait("vector", "S", sh)
    for mt in range(4):
        for ch in range(4):
            p.flush("tensor")
            p.wait("sync", "T", p.counts["T"])
            tka = p.dma("sync", smal[0:1, 512:1024],
                        c["grow"][2 * mt:2 * mt + 1, bass.ts(ch, 512)])
            tkb = p.dma("sync", c["row"](0)[:, 0:512],
                        c["grow"][2 * mt + 1:2 * mt + 2, bass.ts(ch, 512)])
            p.wait_dma("tensor", [tka, tkb])
            p.wait("tensor", "V", p.counts["V"])
            p.on("tensor", lambda t: t.matmul(
                ps_c[0:64, 0:512], onesrow[:, 0:64], smal[0:1, 512:1024],
                start=True, stop=True), inc="T")
            p.on("tensor", lambda t: t.matmul(
                ps_c[64:128, 0:512], onesrow[:, 0:64], c["row"](0)[:, 0:512],
                start=True, stop=True), inc="T")
            p.flush("tensor")
            tv = p.counts["T"]
            p.wait("vector", "T", tv)
            p.on("vector", lambda v, mt=mt, ch=ch: v.tensor_tensor(
                ab(mt)[:, bass.ts(ch, 512)], ab(mt)[:, bass.ts(ch, 512)],
                ps_c[:, 0:512], ALU.mult))
    vhg = p.counts["V"]
    tk_gr = p.dma("gpsimd", ab(4)[0:8, 0:2048],
                  c["grow"][0:8, :])
    p.flush("tensor")
    p.wait("sync", "T", p.counts["T"])
    tks = [p.dma("sync", wbuf[:, 4096 + kt * 1024:4096 + (kt + 1) * 1024],
                 wb("L2wu")[bass.ts(kt, 128), :]) for kt in range(4)]
    tkbu = p.dma("sync", ab(5)[0:8, 0:1024], wb("L2bu"))
    p.wait_dma("tensor", tks + [tk_gr, tkbu])
    p.wait("tensor", "V", vhg)
    for mt in range(8):
        for ch in range(4):
            pst = ps_a if (mt * 4 + ch) % 2 == 0 else ps_b
            p.wait("tensor", "V", p.counts["V"])
            for kt in range(4):
                p.on("tensor", lambda t, kt=kt, mt=mt, ch=ch, pst=pst: t.matmul(
                    pst[:, 0:512],
                    wbuf[:, 4096 + kt * 1024 + mt * 128:4096 + kt * 1024 + (mt + 1) * 128],
                    ab(kt)[:, bass.ts(ch, 512)],
                    start=(kt == 0), stop=False), inc="T")
            p.on("tensor", lambda t, mt=mt, ch=ch, pst=pst: t.matmul(
                pst[:, 0:512], ab(5)[0:8, mt * 128:(mt + 1) * 128],
                ab(4)[0:8, bass.ts(ch, 512)],
                start=False, stop=True), inc="T")
            p.flush("tensor")
            tv = p.counts["T"]
            p.wait("vector", "T", tv)
            p.on("vector", lambda v, mt=mt, ch=ch, pst=pst: v.tensor_tensor(
                xt[mt][:, bass.ts(ch, 512)], xt[mt][:, bass.ts(ch, 512)],
                pst[:, 0:512], ALU.add))
    p.flush("tensor")
    p.mark("vector")


def emit_l2_attn(p, c):
    nc, wb, xs, hs = c["nc"], c["wb"], c["xs"], c["hs"]
    af, ab, row = c["af"], c["ab"], c["row"]
    wbuf, smal, stat = c["wbuf"], c["smal"], c["stat"]
    ps_a, ps_b, ps_c, ps_d, ps_big = (c["ps_a"], c["ps_b"], c["ps_c"],
                                      c["ps_d"], c["ps_big"])
    xt, ht = c["xtiles"], c["hstiles"]

    hf32 = [af(dt) for dt in range(DT)]
    emit_ln(p, c, xt, hf32)
    vln = p.counts["V"]
    for dt in range(DT):
        p.on("vector", lambda v, dt=dt: v.tensor_tensor(
            xt[dt], xt[dt], hf32[dt], ALU.add))
    emit_ln(p, c, hf32, ht, sq_in_dst=True)
    for dt in range(DT):
        p.on("vector", lambda v, dt=dt: v.tensor_scalar(
            ht[dt], ht[dt], smal[:, 26 + dt:27 + dt], smal[:, 34 + dt:35 + dt],
            ALU.mult, ALU.add))
    vxin = p.counts["V"]

    p.wait("sync", c["wb_ready_g"][0], c["wb_ready_g"][1])
    wq = wb("L2wqkvT")
    wo = wb("L2woutT")
    fmt = wbuf[:, 6144:8192]
    def tm(comp, t):
        return ab(2 * comp + t // 8)[:, (t % 8) * 256:(t % 8) * 256 + 256]
    def outf(t):
        return af(6 + t // 4)[:, (t % 4) * 256:(t % 4) * 256 + 256]

    for qtr in range(4):
        for comp in range(3):
            for ft in range(2):
                col0 = comp * 1024 + qtr * 256 + ft * 128
                p.flush("tensor")
                p.wait("sync", "T", p.counts["T"])
                tks = [p.dma("sync", wbuf[:, kt * 128:(kt + 1) * 128],
                             wq[bass.ts(kt, 128), col0:col0 + 128])
                       for kt in range(8)]
                p.wait_dma("tensor", tks)
                if (qtr, comp, ft) == (0, 0, 0):
                    p.wait("tensor", "V", vxin)
                for ch in range(4):
                    pst = ps_a if ch % 2 == 0 else ps_b
                    p.wait("tensor", "V", p.counts["V"])
                    for kt in range(8):
                        p.on("tensor", lambda t, kt=kt, ch=ch, pst=pst: t.matmul(
                            pst[:, 0:512], wbuf[:, kt * 128:(kt + 1) * 128],
                            ht[kt][:, bass.ts(ch, 512)],
                            start=(kt == 0), stop=(kt == 7)), inc="T")
                    p.flush("tensor")
                    tv = p.counts["T"]
                    p.wait("vector", "T", tv)
                    if comp == 0:
                        p.on("vector", lambda v, ch=ch, pst=pst: v.tensor_scalar(
                            fmt[:, bass.ts(ch, 512)], pst[:, 0:512],
                            0.03125, None, ALU.mult))
                    else:
                        p.on("vector", lambda v, ch=ch, pst=pst: v.tensor_copy(
                            fmt[:, bass.ts(ch, 512)], pst[:, 0:512]))
                vfm = p.counts["V"]
                p.wait("tensor", "V", vfm)
                for t in range(16):
                    pst = (ps_c if t % 2 == 0 else ps_d)[:].bitcast(BF16)
                    if t >= 2:
                        p.wait("tensor", "V", p.counts["V"])
                    p.on("tensor", lambda t_, t=t, pst=pst: t_.transpose(
                        pst[0:128, 0:128], fmt[:, bass.ts(t, 128)], c["idb"]),
                        inc="T")
                    p.flush("tensor")
                    tv = p.counts["T"]
                    p.wait("vector", "T", tv)
                    p.on("vector", lambda v, t=t, comp=comp, ft=ft, pst=pst:
                         v.tensor_copy(tm(comp, t)[:, ft * 128:(ft + 1) * 128],
                                       pst[0:128, 0:128]))
        vtm = p.counts["V"]

        for nc2 in range(2):
            for i in range(8):
                ti = 2 * i + nc2
                for j in range(8):
                    tj = 2 * j + nc2
                    p.on("vector", lambda v, ti=ti, tj=tj: v.tensor_tensor(
                        af(5)[:, 0:256], tm(0, ti), tm(1, tj), ALU.mult))
                    p.on("vector", lambda v, j=j: v.tensor_reduce(
                        stat[:, 0:32].rearrange("p (h j) -> p h j", j=8)
                        [:, :, j:j + 1],
                        af(5)[:, 0:256].rearrange("p (h d) -> p h d", d=64),
                        axis=AX.X, op=ALU.add))
                Sv = stat[:, 0:32].rearrange("p (h j) -> p h j", j=8)
                p.on("vector", lambda v, Sv=Sv: v.tensor_reduce(
                    stat[:, 32:36][:, :, None], Sv, axis=AX.X, op=ALU.max))
                p.on("vector", lambda v: v.tensor_scalar(
                    stat[:, 36:40], stat[:, 32:36], -1.0, None, ALU.mult))
                p.on("vector", lambda v, Sv=Sv: v.tensor_tensor(
                    Sv, Sv, stat[:, 36:40][:, :, None].broadcast_to(
                        [128, 4, 8]), ALU.add))
                vs = p.counts["V"]
                p.wait("scalar", "V", vs)
                p.on("scalar", lambda s: s.activation(
                    stat[:, 0:32], stat[:, 0:32], AF.Exp))
                se = p.counts["S"]
                p.wait("vector", "S", se)
                p.on("vector", lambda v, Sv=Sv: v.tensor_reduce(
                    stat[:, 32:36][:, :, None], Sv, axis=AX.X, op=ALU.add))
                p.on("vector", lambda v: v.reciprocal(stat[:, 32:36],
                                                      stat[:, 32:36]))
                p.on("vector", lambda v, Sv=Sv: v.tensor_tensor(
                    Sv, Sv, stat[:, 32:36][:, :, None].broadcast_to(
                        [128, 4, 8]), ALU.mult))
                for j in range(8):
                    tj = 2 * j + nc2
                    a_b = stat[:, 0:32].rearrange("p (h j) -> p h j", j=8)[
                        :, :, j:j + 1].broadcast_to([128, 4, 64])
                    if j == 0:
                        p.on("vector", lambda v, ti=ti, tj=tj, a_b=a_b:
                             v.tensor_tensor(
                                 outf(ti).rearrange("p (h d) -> p h d", d=64),
                                 tm(2, tj).rearrange("p (h d) -> p h d", d=64),
                                 a_b, ALU.mult))
                    else:
                        p.on("vector", lambda v, tj=tj, a_b=a_b: v.tensor_tensor(
                            af(5)[:, 0:256].rearrange("p (h d) -> p h d", d=64),
                            tm(2, tj).rearrange("p (h d) -> p h d", d=64),
                            a_b, ALU.mult))
                        p.on("vector", lambda v, ti=ti: v.tensor_tensor(
                            outf(ti), outf(ti), af(5)[:, 0:256], ALU.add))
        vout = p.counts["V"]

        p.wait("tensor", "V", vout)
        for ft in range(2):
            for t in range(16):
                pst = ps_c if t % 2 == 0 else ps_d
                p.wait("tensor", "V", p.counts["V"])
                p.on("tensor", lambda t_, t=t, ft=ft, pst=pst: t_.transpose(
                    pst[:, 0:128], outf(t)[:, ft * 128:(ft + 1) * 128],
                    c["idf"]), inc="T")
                p.flush("tensor")
                tv = p.counts["T"]
                p.wait("vector", "T", tv)
                p.on("vector", lambda v, t=t, pst=pst: v.tensor_copy(
                    fmt[:, bass.ts(t, 128)],
                    (ps_c if t % 2 == 0 else ps_d)[:, 0:128]))
            vfm2 = p.counts["V"]
            kt = qtr * 2 + ft
            p.flush("tensor")
            p.wait("sync", "T", p.counts["T"])
            tkw = p.dma("sync", wbuf[:, 2048:3072], wo[bass.ts(kt, 128), :])
            p.wait_dma("tensor", tkw)
            p.wait("tensor", "V", vfm2)
            for mt in range(8):
                for ch in range(4):
                    pst = ps_a if (mt * 4 + ch) % 2 == 0 else ps_b
                    p.wait("tensor", "V", p.counts["V"])
                    p.on("tensor", lambda t, mt=mt, ch=ch, pst=pst: t.matmul(
                        pst[:, 0:512],
                        wbuf[:, 2048 + mt * 128:2048 + (mt + 1) * 128],
                        fmt[:, bass.ts(ch, 512)],
                        start=True, stop=True), inc="T")
                    p.flush("tensor")
                    tv = p.counts["T"]
                    p.wait("vector", "T", tv)
                    p.on("vector", lambda v, mt=mt, ch=ch, pst=pst: v.tensor_tensor(
                        xt[mt][:, bass.ts(ch, 512)], xt[mt][:, bass.ts(ch, 512)],
                        pst[:, 0:512], ALU.add))
    p.flush("tensor")
    p.mark("vector")


def emit_l2(p, c, parts=("f1", "a", "f2")):
    emit_xallgather(p, c, to_seq=True)
    if "f1" in parts:
        emit_l2_ffn(p, c)
    if "a" in parts:
        emit_l2_attn(p, c)
    if "f2" in parts:
        emit_l2_ffn(p, c)
    emit_xallgather(p, c, to_seq=False)


def emit_l3_attn(p, c):
    nc, wb, xs, hs = c["nc"], c["wb"], c["xs"], c["hs"]
    af, ab, row = c["af"], c["ab"], c["row"]
    wbuf, smal, stat = c["wbuf"], c["smal"], c["stat"]
    ps_a, ps_b, ps_c, ps_d, ps_big = (c["ps_a"], c["ps_b"], c["ps_c"],
                                      c["ps_d"], c["ps_big"])
    xt, ht = c["xtiles"], c["hstiles"]
    arena = c["arena"]

    emit_ln(p, c, xt, ht)
    vln = p.counts["V"]

    def vtm(t):
        return arena[:, 16384 + t * 1024:16384 + (t + 1) * 1024]
    qb = lambda: arena[:, 32768:33792].rearrange("p (a b) -> p a b", a=8)
    PT = ab(17)[:, 0:640].rearrange("p (a b) -> p a b", a=5)
    Pb = ab(17)[:, 640:1280]
    m256 = ab(17)[:, 1280:1920]
    s_sb = af(9)[:, 0:640]
    av_sb = af(9)[:, 1024:2048]
    smal_bf = smal[:].bitcast(BF16)
    vrow = smal_bf[0:1, 1024:2048]

    p.wait("gpsimd", "V", vln)
    p.on("gpsimd", lambda g: g.iota(s_sb, pattern=[[1, 640]], base=0,
                                    channel_multiplier=-1,
                                    allow_small_or_imprecise_dtypes=True))
    gio = p.mark("gpsimd")
    p.wait("vector", "G", gio[1])
    def mkmask(dst, lo, hi, srcw):
        p.on("vector", lambda v: v.tensor_scalar(
            af(9)[:, 768:768 + srcw], s_sb[:, 0:srcw], float(lo), None,
            ALU.is_ge))
        p.on("vector", lambda v: v.tensor_scalar(
            af(9)[:, 1408:1408 + srcw], s_sb[:, 0:srcw], float(hi), None,
            ALU.is_le))
        p.on("vector", lambda v: v.tensor_tensor(
            dst, af(9)[:, 768:768 + srcw], af(9)[:, 1408:1408 + srcw],
            ALU.mult))
    mkmask(m256, 0, 511, 640)
    vmask = p.counts["V"]

    p.wait("sync", c["wb_ready_g"][0], c["wb_ready_g"][1])
    p.flush("tensor")
    p.wait("sync", "T", p.counts["T"])
    wv_ = wb("L3wv")
    tks = [p.dma("sync", wbuf[:, kt * 1024:(kt + 1) * 1024],
                 wv_[bass.ts(kt, 128), :]) for kt in range(8)]
    p.wait_dma("tensor", tks)
    p.wait("tensor", "V", vln)
    for t in range(16):
        for chh in range(2):
            pst = ps_a if (t * 2 + chh) % 2 == 0 else ps_b
            if (t, chh) != (0, 0):
                p.wait("tensor", "V", p.counts["V"])
            for ct in range(8):
                p.on("tensor", lambda t_, t=t, chh=chh, ct=ct, pst=pst:
                     t_.matmul(
                         pst[:, 0:512], ht[ct][:, bass.ts(t, 128)],
                         wbuf[:, ct * 1024 + chh * 512:ct * 1024 + (chh + 1) * 512],
                         start=(ct == 0), stop=(ct == 7)), inc="T")
            p.flush("tensor")
            tv = p.counts["T"]
            p.wait("vector", "T", tv)
            p.on("vector", lambda v, t=t, chh=chh, pst=pst: v.tensor_copy(
                vtm(t)[:, bass.ts(chh, 512)], pst[:, 0:512]))
    vvt = p.counts["V"]
    p.wait("tensor", "V", vvt)
    for chh in range(2):
        pst = ps_c if chh == 0 else ps_d
        for t in range(16):
            p.on("tensor", lambda t_, t=t, chh=chh, pst=pst: t_.matmul(
                pst[0:1, 0:512], c["onesb"], vtm(t)[:, bass.ts(chh, 512)],
                start=(t == 0), stop=(t == 15)), inc="T")
    p.flush("tensor")
    tvt = p.counts["T"]
    p.wait("vector", "T", tvt)
    p.on("vector", lambda v: v.tensor_copy(vrow[:, 0:512], ps_c[0:1, 0:512]))
    p.on("vector", lambda v: v.tensor_copy(vrow[:, 512:1024], ps_d[0:1, 0:512]))
    vvr = p.counts["V"]

    wk = wb("L3wk")
    wvb = wbuf[:, 0:6144].rearrange("p (b k f) -> p b k f", b=2, k=8)
    last_mm = {}
    p.flush("tensor")
    p.wait("sync", "T", p.counts["T"])
    groups = [(0, 3), (3, 3), (6, 2)]
    for gi, (m0, gn) in enumerate(groups):
        buf = gi % 2
        if gi >= 2:
            p.wait("sync", "T", last_mm[buf])
        tks = [p.dma("sync", wvb[:, buf, kt, 0:gn * 128],
                     wk[bass.ts(kt, 128), m0 * 128:(m0 + gn) * 128])
               for kt in range(8)]
        p.wait_dma("tensor", tks)
        for m3 in range(gn):
            mt = m0 + m3
            for ch in range(4):
                pst = ps_a if (m3 * 4 + ch) % 2 == 0 else ps_b
                p.wait("tensor", "V", p.counts["V"])
                for kt in range(8):
                    p.on("tensor", lambda t, buf=buf, kt=kt, m3=m3, ch=ch,
                         pst=pst: t.matmul(
                             pst[:, 0:512],
                             wvb[:, buf, kt, m3 * 128:(m3 + 1) * 128],
                             ht[kt][:, bass.ts(ch, 512)],
                             start=(kt == 0), stop=(kt == 7)), inc="T")
                p.flush("tensor")
                tv = p.counts["T"]
                p.wait("vector", "T", tv)
                p.on("vector", lambda v, mt=mt, ch=ch, pst=pst: v.tensor_copy(
                    ab(mt)[:, bass.ts(ch, 512)], pst[:, 0:512]))
        last_mm[buf] = p.counts["T"]
    vkf = p.counts["V"]
    p.wait("tensor", "V", vkf)

    wq_ = wb("L3wq")
    for it in range(TT):
        i0 = it * 128
        w0 = max(0, i0 - 256)
        w1 = min(2048, i0 + 384)
        w = w1 - w0
        p.flush("tensor")
        p.wait("sync", "T", p.counts["T"])
        tks = [p.dma("sync", wbuf[:, kt * 1024:(kt + 1) * 1024],
                     wq_[bass.ts(kt, 128), :]) for kt in range(8)]
        p.wait_dma("tensor", tks)
        p.wait("tensor", "V", p.counts["V"])
        p.wait("tensor", "S", p.counts["S"])
        for ft in range(8):
            pst = ps_c if ft % 2 == 0 else ps_d
            if ft >= 2:
                p.wait("tensor", "V", p.counts["V"])
            for ct in range(8):
                p.on("tensor", lambda t, ft=ft, ct=ct, i0=i0, pst=pst: t.matmul(
                    pst[:, 0:128],
                    wbuf[:, ct * 1024 + ft * 128:ct * 1024 + (ft + 1) * 128],
                    ht[ct][:, i0:i0 + 128],
                    start=(ct == 0), stop=(ct == 7)), inc="T")
            p.flush("tensor")
            tv = p.counts["T"]
            p.wait("vector", "T", tv)
            p.on("vector", lambda v, ft=ft, pst=pst: v.tensor_copy(
                qb()[:, ft, :], pst[:, 0:128]))
        vq = p.counts["V"]
        p.wait("tensor", "V", vq)
        nch = (w + 511) // 512
        for ct in range(8):
            for chw in range(nch):
                cw0 = chw * 512
                cw1 = min(w, cw0 + 512)
                p.on("tensor", lambda t, ct=ct, cw0=cw0, cw1=cw1, w0=w0: t.matmul(
                    ps_big[:, cw0:cw1], qb()[:, ct, :],
                    ab(ct)[:, w0 + cw0:w0 + cw1],
                    start=(ct == 0), stop=(ct == 7)), inc="T")
        p.flush("tensor")
        tsc = p.counts["T"]
        p.wait("vector", "T", tsc)
        if it <= 1:
            lo, hi, srcw = ((-256, 255, 384) if it == 0 else (-128, 383, 512))
            dst = ab(16)[:, 1024:1024 + srcw]
            p.wait("gpsimd", "V", p.counts["V"])
            p.on("gpsimd", lambda g, srcw=srcw: g.iota(
                af(9)[:, 768:768 + srcw], pattern=[[1, srcw]], base=0,
                channel_multiplier=-1,
                allow_small_or_imprecise_dtypes=True))
            gio2 = p.mark("gpsimd")
            p.wait("vector", "G", gio2[1])
            p.on("vector", lambda v, dst=dst, lo=lo, srcw=srcw: v.tensor_scalar(
                dst, af(9)[:, 768:768 + srcw], float(lo), None, ALU.is_ge))
            p.on("vector", lambda v, hi=hi, srcw=srcw: v.tensor_scalar(
                af(9)[:, 1536:1536 + srcw], af(9)[:, 768:768 + srcw],
                float(hi), None, ALU.is_le))
            p.on("vector", lambda v, dst=dst, srcw=srcw: v.tensor_tensor(
                dst, dst, af(9)[:, 1536:1536 + srcw], ALU.mult))
            msk = dst
        else:
            msk = m256[:, 0:w]
        p.on("vector", lambda v, w=w, msk=msk: v.scalar_tensor_tensor(
            s_sb[:, 0:w], ps_big[:, 0:w], 0.03125, msk, ALU.mult, ALU.mult))
        p.on("vector", lambda v, w=w: v.tensor_reduce(
            stat[:, 0:1], s_sb[:, 0:w], axis=AX.X, op=ALU.max))
        p.on("vector", lambda v: v.tensor_scalar(
            stat[:, 1:2], stat[:, 0:1], -1.0, None, ALU.mult))
        vs = p.counts["V"]
        p.wait("scalar", "V", vs)
        p.on("scalar", lambda s, w=w: s.activation(
            s_sb[:, 0:w], s_sb[:, 0:w], AF.Exp, bias=stat[:, 1:2]))
        p.on("scalar", lambda s: s.activation(
            stat[:, 2:3], stat[:, 1:2], AF.Exp))
        se = p.counts["S"]
        p.wait("vector", "S", se)
        p.on("vector", lambda v, w=w: v.tensor_scalar(
            s_sb[:, 0:w], s_sb[:, 0:w], stat[:, 2:3], None, ALU.subtract))
        p.on("vector", lambda v, w=w: v.tensor_reduce(
            stat[:, 3:4], s_sb[:, 0:w], axis=AX.X, op=ALU.add))
        p.on("vector", lambda v: v.scalar_tensor_tensor(
            stat[:, 3:4], stat[:, 2:3], 2048.0, stat[:, 3:4], ALU.mult,
            ALU.add))
        p.on("vector", lambda v: v.reciprocal(stat[:, 3:4], stat[:, 3:4]))
        p.on("vector", lambda v, w=w: v.tensor_scalar(
            Pb[:, 0:w], s_sb[:, 0:w], stat[:, 3:4], None, ALU.mult))
        p.on("vector", lambda v: v.tensor_tensor(
            stat[:, 4:5], stat[:, 2:3], stat[:, 3:4], ALU.mult))
        vP = p.counts["V"]
        p.wait("tensor", "V", vP)
        nwt = (w + 127) // 128
        for jt in range(nwt):
            pst = (ps_c if jt % 2 == 0 else ps_d)[:].bitcast(BF16)
            if jt >= 2:
                p.wait("tensor", "V", p.counts["V"])
            jw = min(128, w - jt * 128)
            p.on("tensor", lambda t, jt=jt, jw=jw, pst=pst: t.transpose(
                pst[0:jw, 0:128], Pb[:, jt * 128:jt * 128 + jw], c["idb"]),
                inc="T")
            p.flush("tensor")
            tv = p.counts["T"]
            p.wait("vector", "T", tv)
            p.on("vector", lambda v, jt=jt, jw=jw, pst=pst: v.tensor_copy(
                PT[:, jt, :][0:jw, :], pst[0:jw, 0:128]))
        p.wait("tensor", "V", p.counts["V"])
        p.on("tensor", lambda t: t.transpose(
            ps_c[0:1, 0:128], stat[:, 4:5], c["idf"]), inc="T")
        p.flush("tensor")
        tem = p.counts["T"]
        p.wait("vector", "T", tem)
        p.on("vector", lambda v: v.tensor_copy(
            smal_bf[0:1, 896:1024], ps_c[0:1, 0:128]))
        vtr = p.counts["V"]
        p.wait("tensor", "V", vtr)
        jt0 = w0 // 128
        for jt in range(nwt):
            jw = min(128, w - jt * 128)
            for chh in range(2):
                p.on("tensor", lambda t, jt=jt, jw=jw, chh=chh, jt0=jt0:
                     t.matmul(
                         ps_big[:, 1024 + chh * 512:1024 + (chh + 1) * 512],
                         PT[:, jt, :][0:jw, :],
                         vtm(jt0 + jt)[0:jw, bass.ts(chh, 512)],
                         start=(jt == 0), stop=False), inc="T")
        for chh in range(2):
            p.on("tensor", lambda t, chh=chh: t.matmul(
                ps_big[:, 1024 + chh * 512:1024 + (chh + 1) * 512],
                smal_bf[0:1, 896:1024], vrow[:, bass.ts(chh, 512)],
                start=False, stop=True), inc="T")
        p.flush("tensor")
        tav = p.counts["T"]
        p.wait("vector", "T", tav)
        p.on("vector", lambda v: v.tensor_copy(av_sb, ps_big[:, 1024:2048]))
        vav = p.counts["V"]
        p.wait("tensor", "V", vav)
        for dt in range(DT):
            pst = ps_c if dt % 2 == 0 else ps_d
            if dt >= 2:
                p.wait("tensor", "V", p.counts["V"])
            p.on("tensor", lambda t, dt=dt, pst=pst: t.transpose(
                pst[:, 0:128], av_sb[:, bass.ts(dt, 128)], c["idf"]), inc="T")
            p.flush("tensor")
            tv = p.counts["T"]
            p.wait("vector", "T", tv)
            p.on("vector", lambda v, dt=dt, it=it, pst=pst: v.tensor_tensor(
                xt[dt][:, bass.ts(it, 128)], xt[dt][:, bass.ts(it, 128)],
                pst[:, 0:128], ALU.add))
    p.flush("tensor")
    p.mark("vector")


def emit_l3_ffn(p, c):
    nc, wb, xs, hs = c["nc"], c["wb"], c["xs"], c["hs"]
    af, ab, row = c["af"], c["ab"], c["row"]
    wbuf, smal, stat, cstb = c["wbuf"], c["smal"], c["stat"], c["cstb"]
    ps_a, ps_b, ps_c, ps_d, ps_big = (c["ps_a"], c["ps_b"], c["ps_c"],
                                      c["ps_d"], c["ps_big"])
    xt, ht = c["xtiles"], c["hstiles"]
    io = c["io"]

    emit_ln(p, c, xt, ht)
    vln = p.counts["V"]
    for dt in range(DT):
        p.on("vector", lambda v, dt=dt: v.scalar_tensor_tensor(
            xt[dt], ht[dt], 0.5, xt[dt], ALU.mult, ALU.add))

    p.wait("sync", c["wb_ready_g"][0], c["wb_ready_g"][1])
    p.flush("tensor")
    p.wait("sync", "T", p.counts["T"])
    tk1 = [p.dma("sync", wbuf[:, kt * 128:(kt + 1) * 128],
                 wb("L3wp1")[bass.ts(kt, 128), :]) for kt in range(8)]
    tk2 = p.dma("sync", wbuf[:, 1024:1032], wb("L3wp2"))
    p.wait_dma("tensor", tk1)
    p.wait("tensor", "V", vln)
    p.wait("tensor", "S", p.counts["S"])
    for kt in range(8):
        p.on("tensor", lambda t, kt=kt: t.matmul(
            ps_c[:, 0:1], wbuf[:, kt * 128:(kt + 1) * 128],
            ht[kt][:, 2047:2048], start=(kt == 0), stop=(kt == 7)), inc="T")
    p.flush("tensor")
    th1 = p.counts["T"]
    p.wait("scalar", "T", th1)
    p.on("scalar", lambda s: s.activation(
        cstb[:, 130:131], ps_c[:, 0:1], AF.Gelu, bias=smal[:, 46:47]))
    sh1 = p.counts["S"]
    p.wait("tensor", "S", sh1)
    p.wait_dma("tensor", tk2)
    p.on("tensor", lambda t: t.matmul(
        ps_d[0:1, 0:8], cstb[:, 130:131], wbuf[:, 1024:1032],
        start=True, stop=True), inc="T")
    p.flush("tensor")
    tlg = p.counts["T"]
    p.wait("vector", "T", tlg)
    p.on("vector", lambda v: v.tensor_tensor(
        smal[0:1, 66:74], ps_d[0:1, 0:8], smal[0:1, 58:66], ALU.add))
    p.on("vector", lambda v: v.max(smal[0:1, 74:82], smal[0:1, 66:74]))
    p.on("vector", lambda v: v.max_index(c["idxt"][0:1, 0:8],
                                         smal[0:1, 74:82], smal[0:1, 66:74]))
    vsel = p.counts["V"]
    p.on("vector", lambda v: v.tensor_copy(smal[0:1, 90:91],
                                           c["idxt"][0:1, 0:1]))
    vself = p.counts["V"]
    p.wait("gpsimd", "V", vself)
    p.on("gpsimd", lambda g: g.iota(stat[:, 49:50], pattern=[[0, 1]], base=0,
                                    channel_multiplier=1,
                                    allow_small_or_imprecise_dtypes=True))
    giot = p.mark("gpsimd")
    p.wait("tensor", "V", vself)
    p.on("tensor", lambda t: t.matmul(ps_d[:, 4:5], c["onesrow"],
                                      smal[0:1, 90:91], start=True, stop=True),
         inc="T")
    p.flush("tensor")
    tsel = p.counts["T"]
    p.wait("vector", "T", tsel)
    p.wait("vector", "G", giot[1])
    p.on("vector", lambda v: v.tensor_copy(stat[:, 48:49], ps_d[:, 4:5]))
    for col, mul in ((2, 512.0), (3, 1024.0), (4, 128.0)):
        p.on("vector", lambda v, mul=mul: v.scalar_tensor_tensor(
            stat[:, 50:51], stat[:, 48:49], mul, stat[:, 49:50], ALU.mult,
            ALU.add))
        p.on("vector", lambda v, col=col: v.tensor_copy(
            c["gix"][:, col:col + 1], stat[:, 50:51]))
    vgix = p.counts["V"]
    p.wait("gpsimd", "V", vgix)
    tkb = [p.dma2("gpsimd", lambda eng: eng.indirect_dma_start(
               out=smal[:, 66:82], out_offset=None,
               in_=io["b1"].rearrange("a p t -> (a p) t"),
               in_offset=bass.IndirectOffsetOnAxis(c["gix"][:, 4:5], 0))),
           p.dma2("gpsimd", lambda eng: eng.indirect_dma_start(
               out=smal[:, 82:90], out_offset=None,
               in_=io["b2"].rearrange("a p t -> (a p) t"),
               in_offset=bass.IndirectOffsetOnAxis(c["gix"][:, 4:5], 0)))]
    p.wait_dma("scalar", tkb)
    p.wait_dma("vector", tkb)
    p.on("vector", lambda v: v.tensor_scalar(
        stat[:, 40:48], smal[:, 47:55], 0.15, None, ALU.mult))
    vbo = p.counts["V"]

    wsl = wbuf.rearrange("p (b f) -> p b f", b=4)[:, 0:3, :]
    grp = [[ps_big[:, bass.ts(i, 512)] for i in range(4)],
           [ps_a[:, 0:512], ps_b[:, 0:512], ps_c[:, 0:512], ps_d[:, 0:512]]]
    harena = lambda mt: c["arena"][:, mt * 2048:(mt + 1) * 2048]
    slot_mark = [0, 0, 0]
    pgfree = [("V", p.counts["V"]), ("V", p.counts["V"])]
    p.wait("tensor", "S", p.counts["S"])
    w1flat = [c["wbrows"]("L3w1a", 2048), c["wbrows"]("L3w1b", 2048)]
    w2flat = [c["wbrows"]("L3w2a", 1024), c["wbrows"]("L3w2b", 1024)]

    def gate(q, buf):
        if slot_mark[buf] > 0:
            p.wait(q, "T", slot_mark[buf])

    def load_w1(w):
        buf = w % 3
        gate("gpsimd", buf)
        return [p.dma2("gpsimd",
                       lambda eng, kt=kt, buf=buf, w=w: eng.indirect_dma_start(
                           out=wsl[:, buf, kt * 256:(kt + 1) * 256],
                           out_offset=None, in_=w1flat[kt // 4][0],
                           element_offset=(w1flat[kt // 4][1]
                                           + (kt % 4) * 128 * 2048 + w * 256),
                           in_offset=bass.IndirectOffsetOnAxis(
                               c["gix"][:, 2:3], 0)))
                for kt in range(8)]

    def load_w2(mt):
        buf = (8 + mt) % 3
        gate("gpsimd", buf)
        return [p.dma2("gpsimd",
                       lambda eng, kt=kt, buf=buf, mt=mt:
                       eng.indirect_dma_start(
                           out=wsl[:, buf, kt * 128:(kt + 1) * 128],
                           out_offset=None, in_=w2flat[kt // 8][0],
                           element_offset=(w2flat[kt // 8][1]
                                           + (kt % 8) * 128 * 1024 + mt * 128),
                           in_offset=bass.IndirectOffsetOnAxis(
                               c["gix"][:, 3:4], 0)))
                for kt in range(16)]

    def load_wo(mt):
        buf = (16 + mt) % 3
        gate("sync", buf)
        return [p.dma("sync", wsl[:, buf, 0:1024],
                      c["wb"]("L3wot")[mt].rearrange("p a c -> p (a c)"))]

    tks = {w: load_w1(w) for w in range(3)}
    for w in range(8):
        buf = w % 3
        p.wait_dma("tensor", tks.pop(w))
        for m2 in range(2):
            mt = 2 * w + m2
            g = grp[mt % 2]
            eng_, cnt = pgfree[mt % 2]
            if cnt > 0:
                p.wait("tensor", eng_, cnt)
            for nch in range(4):
                for kt in range(8):
                    p.on("tensor", lambda t, buf=buf, kt=kt, m2=m2, nch=nch,
                         g=g: t.matmul(
                             g[nch], wsl[:, buf, kt * 256 + m2 * 128:
                                         kt * 256 + (m2 + 1) * 128],
                             ht[kt][:, bass.ts(nch, 512)],
                             start=(kt == 0), stop=(kt == 7)), inc="T")
            tmm = p.counts["T"]
            p.wait("scalar", "T", tmm)
            for nch in range(4):
                p.on("scalar", lambda s, mt=mt, nch=nch, g=g: s.activation(
                    harena(mt)[:, bass.ts(nch, 512)], g[nch],
                    AF.Gelu, bias=smal[:, 66 + mt:67 + mt]))
            pgfree[mt % 2] = ("S", p.counts["S"])
        slot_mark[buf] = p.counts["T"]
        if w + 3 < 8:
            tks[w + 3] = load_w1(w + 3)
    t_p1end = p.counts["T"]
    tks = {mt: load_w2(mt) for mt in range(3)}
    p.wait("tensor", "S", p.counts["S"])
    first_evac = True
    for mt in range(8):
        buf = (8 + mt) % 3
        p.wait_dma("tensor", tks.pop(mt))
        g = grp[mt % 2]
        eng_, cnt = pgfree[mt % 2]
        if cnt > 0:
            p.wait("tensor", eng_, cnt)
        for nch in range(4):
            for kt in range(16):
                p.on("tensor", lambda t, buf=buf, kt=kt, nch=nch, g=g:
                     t.matmul(
                         g[nch], wsl[:, buf, kt * 128:(kt + 1) * 128],
                         c["arena"][:, kt * 2048 + nch * 512:
                                    kt * 2048 + (nch + 1) * 512],
                         start=(kt == 0), stop=(kt == 15)), inc="T")
        tmm = p.counts["T"]
        p.wait("vector", "T", tmm)
        if first_evac:
            p.wait("vector", "T", t_p1end)
            first_evac = False
        for nch in range(4):
            p.on("vector", lambda v, mt=mt, nch=nch, g=g: v.tensor_scalar(
                ht[mt][:, bass.ts(nch, 512)], g[nch],
                smal[:, 82 + mt:83 + mt], None, ALU.add))
        pgfree[mt % 2] = ("V", p.counts["V"])
        slot_mark[buf] = p.counts["T"]
        if mt + 3 < 8:
            tks[mt + 3] = load_w2(mt + 3)
    vre = p.counts["V"]
    tks = {mt: load_wo(mt) for mt in range(3)}
    p.wait("tensor", "V", vre)
    for mt in range(8):
        buf = (16 + mt) % 3
        p.wait_dma("tensor", tks.pop(mt))
        g = grp[mt % 2]
        eng_, cnt = pgfree[mt % 2]
        if cnt > 0:
            p.wait("tensor", eng_, cnt)
        for nch in range(4):
            for kt in range(8):
                p.on("tensor", lambda t, buf=buf, kt=kt, nch=nch, g=g:
                     t.matmul(
                         g[nch], wsl[:, buf, kt * 128:(kt + 1) * 128],
                         ht[kt][:, bass.ts(nch, 512)],
                         start=(kt == 0), stop=(kt == 7)), inc="T")
        tmm = p.counts["T"]
        p.wait("vector", "T", tmm)
        for nch in range(4):
            p.on("vector", lambda v, mt=mt, nch=nch, g=g:
                 v.scalar_tensor_tensor(
                     xt[mt][:, bass.ts(nch, 512)], g[nch], 0.15,
                     xt[mt][:, bass.ts(nch, 512)], ALU.mult, ALU.add))
            p.on("vector", lambda v, mt=mt, nch=nch: v.tensor_scalar(
                xt[mt][:, bass.ts(nch, 512)], xt[mt][:, bass.ts(nch, 512)],
                stat[:, 40 + mt:41 + mt], None, ALU.add))
        pgfree[mt % 2] = ("V", p.counts["V"])
        slot_mark[buf] = p.counts["T"]
        if mt + 3 < 8:
            tks[mt + 3] = load_wo(mt + 3)
    p.flush("tensor")
    p.mark("vector")


def emit_l3(p, c, parts=("f1", "a", "f2")):
    if "f1" in parts:
        emit_l3_ffn(p, c)
    if "a" in parts:
        emit_l3_attn(p, c)
    if "f2" in parts:
        emit_l3_ffn(p, c)


def make_in_maps(inputs, n_cores=NCORES):
    x = np.ascontiguousarray(inputs["L0_attn_x"], dtype=np.float32)
    img = host_pack_wb(inputs)
    fvec = host_pack_fvec(inputs)
    in_maps = []
    for b in range(n_cores):
        m = {
            "x_in": np.ascontiguousarray(x[b].T),
            "conv_src": (img.reshape(-1) if n_cores == 1 else
                         np.concatenate([img[0, b * WB_HSH:(b + 1) * WB_HSH],
                                         img[1, b * WB_HSH:(b + 1) * WB_HSH]])),
            "fvec": fvec,
            "cid": np.array([b], dtype=np.int32),
            "gidx": np.stack([b * 8192 + np.arange(128, dtype=np.uint32),
                              np.arange(128, dtype=np.uint32) * 8 + b]),
            "b1": np.ascontiguousarray(
                np.asarray(inputs["L3_ffn_b1"], np.float32)
                .reshape(8, 16, 128).transpose(0, 2, 1)),
            "b2": np.ascontiguousarray(
                np.asarray(inputs["L3_ffn_b2"], np.float32)
                .reshape(8, 8, 128).transpose(0, 2, 1)),
        }
        in_maps.append(m)
    return in_maps


def kernel(**inputs):
    nc = build_nc()
    in_maps = make_in_maps(inputs)
    res = run_bass_kernel_spmd(nc, in_maps, core_ids=list(range(NCORES)))
    return np.stack([res.results[b]["out"].T for b in range(NCORES)], axis=0)


# version 10 (final)
# speedup vs baseline: 1.8859x; 1.0044x over previous
import math
import numpy as np

import concourse.bass as bass
import concourse.mybir as mybir
from concourse.bass_utils import run_bass_kernel_spmd

F32 = mybir.dt.float32
BF16 = mybir.dt.bfloat16
I32 = mybir.dt.int32
AF = mybir.ActivationFunctionType
ALU = mybir.AluOpType
AX = mybir.AxisListType

D = 1024
T = 2048
DT = D // 128
TT = T // 128
NCORES = 8
LN_EPS = 1e-5

ENGINES = ("tensor", "vector", "scalar", "gpsimd", "sync")



def _wb_table():
    t = []
    def add(key, shape):
        t.append((key, tuple(shape)))
    for e in range(7):
        add(("L1wg", e), (32, 128, 8, 128))
        add(("L1wu", e), (32, 128, 8, 128))
        add(("L1wd", e), (4096, D))
    add(("L1swg", 0), (32, 128, 8, 128))
    add(("L1swu", 0), (32, 128, 8, 128))
    add(("L1swd", 0), (4096, D))
    add(("L3w1a", 0), (8, 512, 2048))
    add(("L3w1b", 0), (8, 512, 2048))
    add(("L3w2a", 0), (8, 1024, D))
    add(("L3w2b", 0), (8, 1024, D))
    add(("L3wot", 0), (8, 128, 8, 128))
    add(("L3wp1", 0), (D, 128))
    add(("L3wp2", 0), (128, 8))
    add(("L0wrvkT", 0), (D, 3 * D))
    add(("L0woutT", 0), (D, D))
    add(("L0qprojT", 0), (D, 128))
    add(("L0routerT", 0), (128, 48))
    add(("L0auxkT", 0), (128, 4096))
    add(("L0auxv", 0), (4096, D))
    add(("L0uncw1T", 0), (D, 256))
    add(("L0maskrow", 0), (1, 4096))
    add(("L0ones", 0), (1, T))
    add(("L0uncw2T", 0), (256, 1))
    add(("L1wqkvT", 0), (D, 3 * D))
    add(("L1woutT", 0), (D, D))
    add(("L1routerT", 0), (D, 7))
    add(("L2wqkvT", 0), (D, 3 * D))
    add(("L2woutT", 0), (D, D))
    add(("L2wgate", 0), (D, 8))
    add(("L2wd", 0), (D, 512))
    add(("L2wu", 0), (512, D))
    add(("L2bu", 0), (8, D))
    add(("L3wq", 0), (D, D))
    add(("L3wk", 0), (D, D))
    add(("L3wv", 0), (D, D))
    nbins = 2 * NCORES
    fills = [0] * nbins
    binof = {}
    for key, shape in sorted(t, key=lambda kv: -int(np.prod(kv[1]))):
        n = (int(np.prod(shape)) + 31) // 32 * 32
        j = min(range(nbins), key=lambda i: (fills[i], i))
        binof[key] = (j, fills[j])
        fills[j] += n
    hsh = (max(fills) + 511) // 512 * 512
    offsets = {}
    for key, shape in t:
        j, off = binof[key]
        offsets[key] = (j % 2, (j // 2) * hsh + off)
    return dict(t), offsets, hsh

WB_SHAPES, WB_OFFSETS, WB_HSH = _wb_table()
WB_SHARD = 2 * WB_HSH
WB_HALF_TOTAL = NCORES * WB_HSH


def host_pack_wb(inp):
    img = np.zeros((2, WB_HALF_TOTAL), dtype=np.float32)
    def put(key, arr):
        a = np.ascontiguousarray(arr, dtype=np.float32)
        assert a.shape == WB_SHAPES[key], (key, a.shape, WB_SHAPES[key])
        h, o = WB_OFFSETS[key]
        img[h, o:o + a.size] = a.reshape(-1)
    def tile_gu(w):
        return w.reshape(8, 128, 32, 128).transpose(2, 1, 0, 3)
    for e in range(7):
        put(("L1wg", e), tile_gu(inp["L1_ffn_w_gate"][e]))
        put(("L1wu", e), tile_gu(inp["L1_ffn_w_up"][e]))
        put(("L1wd", e), inp["L1_ffn_w_down"][e])
    put(("L1swg", 0), tile_gu(inp["L1_ffn_sw_gate"][0]))
    put(("L1swu", 0), tile_gu(inp["L1_ffn_sw_up"][0]))
    put(("L1swd", 0), inp["L1_ffn_sw_down"][0])
    put(("L3w1a", 0), inp["L3_ffn_W1"][:, 0:512, :])
    put(("L3w1b", 0), inp["L3_ffn_W1"][:, 512:1024, :])
    put(("L3w2a", 0), inp["L3_ffn_W2"][:, 0:1024, :])
    put(("L3w2b", 0), inp["L3_ffn_W2"][:, 1024:2048, :])
    put(("L3wot", 0), inp["L3_ffn_Wo"].reshape(8, 128, 8, 128)
        .transpose(2, 1, 0, 3))
    put(("L3wp1", 0), inp["L3_ffn_Wp1"])
    put(("L3wp2", 0), inp["L3_ffn_Wp2"])
    wrvkT = np.ascontiguousarray(inp["L0_attn_W_rvk"], dtype=np.float32).T
    g = np.empty_like(wrvkT)
    for ct in range(8):
        g[:, ct * 384 + 0:ct * 384 + 128] = wrvkT[:, 0 * 1024 + ct * 128:0 * 1024 + (ct + 1) * 128]
        g[:, ct * 384 + 128:ct * 384 + 256] = wrvkT[:, 1 * 1024 + ct * 128:1 * 1024 + (ct + 1) * 128]
        g[:, ct * 384 + 256:ct * 384 + 384] = wrvkT[:, 2 * 1024 + ct * 128:2 * 1024 + (ct + 1) * 128]
    put(("L0wrvkT", 0), g)
    put(("L0woutT", 0), inp["L0_attn_W_out"].T)
    put(("L0qprojT", 0), inp["L0_ffn_q_proj_w"].T)
    put(("L0routerT", 0), inp["L0_ffn_router_w"].T)
    put(("L0auxkT", 0), inp["L0_ffn_aux_keys"].T)
    put(("L0auxv", 0), inp["L0_ffn_aux_values"])
    put(("L0uncw1T", 0), inp["L0_ffn_unc_w1"].T)
    put(("L0maskrow", 0), inp["L0_ffn_reliability_mask"].reshape(1, 4096))
    put(("L0ones", 0), np.ones((1, T), np.float32))
    put(("L0uncw2T", 0), inp["L0_ffn_unc_w2"].T)
    wqt = np.ascontiguousarray(inp["L1_attn_w_qkv"], np.float32).T
    halfA = np.concatenate([wqt[:, 0:512], wqt[:, 1024:1536], wqt[:, 2048:2560]], 1)
    halfB = np.concatenate([wqt[:, 512:1024], wqt[:, 1536:2048], wqt[:, 2560:3072]], 1)
    put(("L1wqkvT", 0), np.concatenate([halfA, halfB], 1))
    put(("L1woutT", 0), inp["L1_attn_w_out"].T)
    put(("L1routerT", 0), inp["L1_ffn_router_w"].T)
    w2t = np.ascontiguousarray(inp["L2_attn_w_qkv"], np.float32).T
    w2v = w2t.reshape(1024, 16, 3, 64)
    w2p = np.concatenate([w2v[:, :, 0], w2v[:, :, 1], w2v[:, :, 2]],
                         axis=1).reshape(1024, 3072)
    put(("L2wqkvT", 0), w2p)
    put(("L2woutT", 0), inp["L2_attn_w_out"].T)
    put(("L2wgate", 0), inp["L2_ffn_w_gate"])
    put(("L2wd", 0), inp["L2_ffn_Wd"].transpose(1, 0, 2).reshape(D, 512))
    put(("L2wu", 0), inp["L2_ffn_Wu"].reshape(512, D))
    put(("L2bu", 0), inp["L2_ffn_bu"])
    put(("L3wq", 0), inp["L3_attn_WQ"])
    put(("L3wk", 0), inp["L3_attn_WK"])
    put(("L3wv", 0), inp["L3_attn_WV"])
    return img


FV_COLS = 66

def host_pack_fvec(inp):
    f = np.zeros((128, FV_COLS), dtype=np.float32)
    def colmaj(v):
        return np.asarray(v, np.float32).reshape(-1, 128).T
    f[:, 0:8] = colmaj(inp["L0_attn_time_decay"])
    f[:, 8:16] = colmaj(inp["L0_attn_time_first"])
    f[:, 16:18] = colmaj(inp["L0_ffn_unc_b1"])
    f[:, 18:26] = colmaj(inp["L1_attn_b_out"])
    f[:, 26:34] = colmaj(inp["L2_attn_ln_w"])
    f[:, 34:42] = colmaj(inp["L2_attn_ln_b"])
    f[:, 42:46] = colmaj(inp["L2_ffn_bd"].reshape(-1))
    f[:, 46:47] = colmaj(inp["L3_ffn_bp1"])
    f[:, 47:55] = colmaj(inp["L3_ffn_bo"])
    f[0, 55] = np.float32(inp["L0_ffn_unc_b2"].reshape(-1)[0])
    f[0, 56] = np.float32(np.asarray(inp["L0_ffn_gate_w1"]).reshape(-1)[0])
    f[0, 57] = np.float32(np.asarray(inp["L0_ffn_gate_bias"]).reshape(-1)[0])
    f[0, 58:66] = np.asarray(inp["L3_ffn_bp2"], np.float32).reshape(-1)
    return f


class Prog:

    def __init__(self, nc):
        self.nc = nc
        self.ops = {e: [] for e in ENGINES}
        self.sem_handles = {}
        self.counts = {}
        self.dma_ix = {}

    def add_sem(self, name, handle):
        self.sem_handles[name] = handle
        self.counts[name] = 0

    AUTO_SERIAL = ("vector", "scalar", "gpsimd")

    def on(self, engine, fn, inc=None, inc_by=1, serial=None):
        if inc is None:
            inc = self.ESEM[engine]
        sem = self.sem_handles[inc]
        if serial is None:
            serial = engine in self.AUTO_SERIAL
        pre = None
        if serial:
            own = self.ESEM[engine]
            ov = self.counts[own]
            if ov > 0:
                pre = (self.sem_handles[own], ov)
        def wrapped(eng, fn=fn, sem=sem, n=inc_by, pre=pre):
            if pre is not None:
                eng.wait_ge(pre[0], pre[1])
            fn(eng).then_inc(sem, n)
        self.ops[engine].append(wrapped)
        self.counts[inc] += inc_by
        return self.counts[inc]

    def wait(self, engine, sem_name, val=None):
        if val is None:
            val = self.counts[sem_name]
        assert val <= self.counts[sem_name], (sem_name, val, self.counts[sem_name])
        if val <= 0:
            return
        sem = self.sem_handles[sem_name]
        self.ops[engine].append(lambda eng, s=sem, v=val: eng.wait_ge(s, v))

    ESEM = {"tensor": "T", "vector": "V", "scalar": "S", "gpsimd": "G", "sync": "Y"}

    def softsync(self, engines=("tensor", "vector", "scalar")):
        self.flush("tensor")
        cur = {e: self.counts[self.ESEM[e]] for e in engines}
        for e in engines:
            for e2 in engines:
                if e2 != e and cur[e2] > 0:
                    self.wait(e, self.ESEM[e2], cur[e2])

    def flush(self, engine):
        s = self.ESEM[engine]
        self.wait(engine, s, self.counts[s])

    def mark(self, engine):
        s = self.ESEM[engine]
        v = self.on(engine, lambda eng: eng.nop(), inc=s)
        return (s, v)

    def barrier(self, engines=("tensor", "vector", "scalar", "gpsimd", "sync")):
        marks = [(e, self.mark(e)) for e in engines]
        for e in engines:
            for e2, (s, v) in marks:
                if e2 != e:
                    self.wait(e, s, v)

    DMA_R = 4

    def dma(self, queue, out, in_, transpose=False, slow_ok=False):
        if queue not in self.dma_ix:
            self.dma_ix[queue] = 0
        k = self.dma_ix[queue]
        self.dma_ix[queue] = k + 1
        qc = {"sync": "y", "scalar": "c", "gpsimd": "g", "vector": "v"}[queue]
        s = f"d{qc}{k % self.DMA_R}"
        sem = self.sem_handles[s]
        prev = (k // self.DMA_R) * 16
        def fn(eng, out=out, in_=in_, sem=sem, prev=prev, transpose=transpose,
               slow_ok=slow_ok):
            if prev > 0:
                eng.wait_ge(sem, prev)
            if transpose:
                eng.dma_start(out=out, in_=in_, transpose=True).then_inc(sem, 16)
            elif slow_ok:
                eng.dma_start(out=out, in_=in_,
                              allow_slow_non_contiguous=True).then_inc(sem, 16)
            else:
                eng.dma_start(out=out, in_=in_).then_inc(sem, 16)
        self.ops[queue].append(fn)
        self.counts[s] += 16
        return (s, prev + 16)

    def dma2(self, queue, mkfn):
        if queue not in self.dma_ix:
            self.dma_ix[queue] = 0
        k = self.dma_ix[queue]
        self.dma_ix[queue] = k + 1
        qc = {"sync": "y", "scalar": "c", "gpsimd": "g", "vector": "v"}[queue]
        s = f"d{qc}{k % self.DMA_R}"
        sem = self.sem_handles[s]
        prev = (k // self.DMA_R) * 16
        def fn(eng, mkfn=mkfn, sem=sem, prev=prev):
            if prev > 0:
                eng.wait_ge(sem, prev)
            try:
                mkfn(eng).then_inc(sem, 16)
            except Exception:
                import traceback as _tb
                _tb.print_exc()
                raise
        self.ops[queue].append(fn)
        self.counts[s] += 16
        return (s, prev + 16)

    def wait_dma(self, engine, tokens):
        if isinstance(tokens, tuple):
            tokens = [tokens]
        best = {}
        for s, v in tokens:
            best[s] = max(best.get(s, 0), v)
        for s, v in best.items():
            self.wait(engine, s, v)



def emit_ln(p, c, src_tiles, dst_tiles, out_f32_tiles=None, want_var_row=False,
            sq_in_dst=False):
    ps_a, ps_b, ps_big = c["ps_a"], c["ps_b"], c["ps_big"]
    af, row = c["af"], c["row"]
    hftile = [af(dt) for dt in range(DT)]
    onescol = c["onesf"]
    onesrow = c["onesrow"]

    p.softsync()
    for ch in range(4):
        if ch >= 2:
            p.wait("tensor", "V", p.counts["V"] - (1 if ch == 2 else 0))
        for dt in range(DT):
            p.on("tensor", lambda t, ch=ch, dt=dt: t.matmul(
                ps_a[0:1, 0:512] if ch % 2 == 0 else ps_b[0:1, 0:512],
                onescol, src_tiles[dt][:, bass.ts(ch, 512)],
                start=(dt == 0), stop=(dt == DT - 1)), inc="T")
        tv = p.counts["T"]
        p.wait("vector", "T", tv)
        p.on("vector", lambda v, ch=ch: v.tensor_copy(
            row(0)[:, bass.ts(ch, 512)],
            ps_a[0:1, 0:512] if ch % 2 == 0 else ps_b[0:1, 0:512]), inc="V")
    p.flush("tensor")
    tsum1 = p.counts["T"]
    sqm = []
    sqt = dst_tiles if sq_in_dst else hftile
    for dt in range(DT):
        if dt == 0:
            p.wait("scalar", "T", tsum1)
        p.on("scalar", lambda s, dt=dt: s.activation(
            sqt[dt], src_tiles[dt], AF.Square), inc="S")
        sqm.append(p.counts["S"])
    p.on("vector", lambda v: v.tensor_scalar(
        row(0), row(0), -1.0 / D, None, ALU.mult), inc="V")
    p.on("vector", lambda v: v.tensor_tensor(
        row(1), row(0), row(0), ALU.mult), inc="V")
    vneg = p.counts["V"]
    for ch in range(4):
        for dt in range(DT):
            if ch == 0:
                p.wait("tensor", "S", sqm[dt])
            if dt == 0:
                p.wait("tensor", "V", vneg if ch < 2 else p.counts["V"])
            p.on("tensor", lambda t, ch=ch, dt=dt: t.matmul(
                ps_a[0:1, 0:512] if ch % 2 == 0 else ps_b[0:1, 0:512],
                c["onesb"] if sq_in_dst else onescol,
                sqt[dt][:, bass.ts(ch, 512)],
                start=(dt == 0), stop=(dt == DT - 1)), inc="T")
        tv = p.counts["T"]
        p.wait("vector", "T", tv)
        p.on("vector", lambda v, ch=ch: v.scalar_tensor_tensor(
            row(1)[:, bass.ts(ch, 512)],
            ps_a[0:1, 0:512] if ch % 2 == 0 else ps_b[0:1, 0:512],
            1.0 / D, row(1)[:, bass.ts(ch, 512)], ALU.mult, ALU.subtract),
            inc="V")
    if want_var_row:
        p.on("vector", lambda v: v.tensor_copy(c["var_row_dst"], row(1)), inc="V")
    vv = p.counts["V"]
    p.wait("scalar", "V", vv)
    p.on("scalar", lambda s: s.activation(
        row(1), row(1), AF.Sqrt, bias=c["epscol"][0:1, :]), inc="S")
    p.wait("vector", "S")
    p.on("vector", lambda v: v.reciprocal(row(1), row(1)), inc="V")
    vstat = p.counts["V"]

    p.wait("tensor", "V", vstat)
    for ch in range(4):
        p.on("tensor", lambda t, ch=ch: t.matmul(
            ps_big[:, bass.ts(ch, 512)], onesrow, row(0)[:, bass.ts(ch, 512)],
            start=True, stop=True), inc="T")
    trep = p.counts["T"]
    p.wait("vector", "T", trep)
    for dt in range(DT):
        p.on("vector", lambda v, dt=dt: v.tensor_tensor(
            hftile[dt], src_tiles[dt], ps_big[:, 0:T], ALU.add), inc="V")
    v1 = p.counts["V"]
    p.wait("tensor", "V", v1)
    for ch in range(4):
        p.on("tensor", lambda t, ch=ch: t.matmul(
            ps_big[:, bass.ts(ch, 512)], onesrow, row(1)[:, bass.ts(ch, 512)],
            start=True, stop=True), inc="T")
    trep2 = p.counts["T"]
    p.wait("vector", "T", trep2)
    for dt in range(DT):
        p.on("vector", lambda v, dt=dt: v.tensor_tensor(
            dst_tiles[dt], hftile[dt], ps_big[:, 0:T], ALU.mult), inc="V")
        if out_f32_tiles is not None:
            p.on("vector", lambda v, dt=dt: v.tensor_tensor(
                out_f32_tiles[dt], hftile[dt], ps_big[:, 0:T], ALU.mult),
                inc="V")
    p.mark("vector")


def build_nc(phases=("l0", "l1", "l2", "l3"), n_cores=NCORES):
    import contextlib
    nc = bass.Bass("TRN2", target_bir_lowering=False, debug=False)
    est = contextlib.ExitStack()

    io = {}
    io["x_in"] = nc.dram_tensor("x_in", [D, T], F32, kind="ExternalInput").ap()
    conv_n = WB_SHARD if n_cores > 1 else 2 * WB_HALF_TOTAL
    io["conv_src"] = nc.dram_tensor("conv_src", [conv_n], F32,
                                    kind="ExternalInput").ap()
    io["fvec"] = nc.dram_tensor("fvec", [128, FV_COLS], F32,
                                kind="ExternalInput").ap()
    io["cid"] = nc.dram_tensor("cid", [1], I32, kind="ExternalInput").ap()
    io["gidx"] = nc.dram_tensor("gidx", [2, 128], mybir.dt.uint32,
                                kind="ExternalInput").ap()
    io["b1"] = nc.dram_tensor("b1", [8, 128, 16], F32, kind="ExternalInput").ap()
    io["b2"] = nc.dram_tensor("b2", [8, 128, 8], F32, kind="ExternalInput").ap()
    io["out"] = nc.dram_tensor("out", [D, T], F32, kind="ExternalOutput").ap()

    wb_shard = nc.dram_tensor("wb_shard", [WB_SHARD], BF16).ap()
    wb_allA = nc.dram_tensor("wb_allA", [WB_HALF_TOTAL], BF16).ap()
    wb_allB = nc.dram_tensor("wb_allB", [WB_HALF_TOTAL], BF16).ap()
    ar_in = nc.dram_tensor("ar_in", [4160], F32).ap()
    grow = nc.dram_tensor("grow", [8, T], F32).ap()

    ar_out = nc.dram_tensor("ar_out", [4160], F32).ap()
    xg_in = nc.dram_tensor("xg_in", [D * T], F32).ap()
    xg_out = nc.dram_tensor("xg_out", [NCORES * D * T], F32).ap()

    def wb(key, e=0):
        k = key if isinstance(key, tuple) else (key, e)
        shape = WB_SHAPES[k]
        h, off = WB_OFFSETS[k]
        n = int(np.prod(shape))
        flat = (wb_allA if h == 0 else wb_allB)[off:off + n]
        names = " ".join(f"a{i}" for i in range(len(shape)))
        return flat.rearrange(f"({names}) -> {names}",
                              **{f"a{i}": s for i, s in enumerate(shape)})

    def sb(name, shape, dtype):
        return est.enter_context(nc.sbuf_tensor(name, shape, dtype))

    xs = sb("xs", [128, DT, T], F32)
    hs = sb("hs", [128, DT, T], BF16)
    arena = sb("arena", [128, 40960], BF16)
    wbuf = sb("wbuf", [128, 8192], BF16)
    stat = sb("stat", [128, 64], F32)
    cst = sb("cst", [128, 8], F32)
    cidt = sb("cidt", [1, 1], I32)
    idxt = sb("idxt", [1, 8], mybir.dt.uint32)
    gix = sb("gix", [128, 8], mybir.dt.uint32)
    cstb = sb("cstb", [128, 272], BF16)
    smal = sb("smal", [128, 1024], F32)
    idf = sb("idf", [128, 128], F32)
    arena_f32 = arena[:].bitcast(F32)

    def af(slot, n=1):
        return arena_f32[:, slot * T:(slot + n) * T]

    def ab(slot, n=1):
        return arena[:, slot * T:(slot + n) * T]

    def row(i):
        return arena_f32[0:1, (8 + i) * T:(9 + i) * T]
    hf = None

    ps_big = est.enter_context(nc.psum_tensor("ps_big", [128, 2048], F32))
    ps_a = est.enter_context(nc.psum_tensor("ps_a", [128, 512], F32))
    ps_b = est.enter_context(nc.psum_tensor("ps_b", [128, 512], F32))
    ps_c = est.enter_context(nc.psum_tensor("ps_c", [128, 512], F32))
    ps_d = est.enter_context(nc.psum_tensor("ps_d", [128, 512], F32))

    p = Prog(nc)
    semnames = ["T", "V", "S", "G", "Y", "cc"]
    for q in "gcvy":
        semnames += [f"d{q}{i}" for i in range(Prog.DMA_R)]
    for s in semnames:
        p.add_sem(s, est.enter_context(nc.semaphore(s)))

    def wbrows(key, width):
        h, off = WB_OFFSETS[key if isinstance(key, tuple) else (key, 0)]
        half = wb_allA if h == 0 else wb_allB
        n = (WB_HALF_TOTAL // width) * width
        return half[0:n].rearrange("(r w) -> r w", w=width), off

    c = dict(nc=nc, io=io, wb=wb, wbrows=wbrows,
             xs=xs, hs=hs, af=af, ab=ab, row=row,
             arena=arena, cidt=cidt, idxt=idxt, gix=gix,
             wbuf=wbuf, stat=stat, cst=cst, cstb=cstb, smal=smal,
             idf=idf, ps_big=ps_big, ps_a=ps_a, ps_b=ps_b, ps_c=ps_c, ps_d=ps_d,
             ar_in=ar_in, ar_out=ar_out, xg_in=xg_in, xg_out=xg_out, grow=grow,
             wb_shard=wb_shard, wb_allA=wb_allA, wb_allB=wb_allB,
             n_cores=n_cores)

    emit_prologue(p, c)

    xtiles = [xs[:, dt, :] for dt in range(DT)]
    c["xtiles"] = xtiles
    c["hstiles"] = [hs[:, dt, :] for dt in range(DT)]

    if "l0a" in phases:
        emit_l0(p, c, parts=("a",))
    elif "l0f" in phases:
        emit_l0(p, c, parts=("f1",))
    elif "l0" in phases:
        emit_l0(p, c)
    if "l1a" in phases:
        emit_l1(p, c, parts=("a",))
    elif "l1f" in phases:
        emit_l1(p, c, parts=("f1",))
    elif "l1" in phases:
        emit_l1(p, c)
    if "xag" in phases:
        emit_xallgather(p, c, to_seq=True)
        emit_xallgather(p, c, to_seq=False)
    elif "l2f" in phases:
        emit_l2(p, c, parts=("f1",))
    elif "l2at" in phases:
        emit_l2(p, c, parts=("a",))
    elif "l2" in phases:
        emit_l2(p, c)
    if "l3a" in phases:
        emit_l3(p, c, parts=("a",))
    elif "l3f" in phases:
        emit_l3(p, c, parts=("f1",))
    elif "l3" in phases:
        emit_l3(p, c)

    emit_epilogue(p, c)

    with nc.Block() as block:
        for ename in ENGINES:
            deco = getattr(block, ename)
            def make(ops):
                def _(eng):
                    for fn in ops:
                        fn(eng)
                return _
            deco(make(p.ops[ename]))
    est.close()
    return nc


def rowslice(c):
    return c["rows"][15:16, 0:128]


def emit_prologue(p, c):
    nc, io = c["nc"], c["io"]
    xs, cst, cstb, idf, smal = c["xs"], c["cst"], c["cstb"], c["idf"], c["smal"]
    af = c["af"]

    tk_x = p.dma("sync", xs[:], io["x_in"].rearrange("(a p) t -> p a t", p=128))
    tk_f = p.dma("sync", smal[:, 0:FV_COLS], io["fvec"])
    tk_c = p.dma("sync", c["cidt"][0:1, 0:1], io["cid"][None, :])
    tk_g = p.dma("gpsimd", c["gix"][:, 0:2], io["gidx"].rearrange("a p -> p a"),
                 slow_ok=True)
    t0, t1 = af(0), af(1)
    p.on("gpsimd", lambda g: g.iota(t0[:, 0:128], pattern=[[1, 128]],
                                    base=0, channel_multiplier=0,
                                    allow_small_or_imprecise_dtypes=True), inc="G")
    p.on("gpsimd", lambda g: g.iota(t1[:, 0:128], pattern=[[0, 128]],
                                    base=0, channel_multiplier=1,
                                    allow_small_or_imprecise_dtypes=True), inc="G")
    p.wait("vector", "G")
    p.on("vector", lambda v: v.tensor_tensor(idf[:], t0[:, 0:128],
                                             t1[:, 0:128], ALU.is_equal),
         inc="V")
    p.on("vector", lambda v: v.tensor_copy(cstb[:, 0:128], idf[:]), inc="V")
    p.on("vector", lambda v: v.memset(cst[:, 0:1], 1.0), inc="V")
    p.on("vector", lambda v: v.memset(cst[:, 1:2], LN_EPS), inc="V")
    p.on("vector", lambda v: v.memset(cst[:, 2:3], -1.0), inc="V")
    p.on("vector", lambda v: v.memset(cstb[:, 128:129], 1.0), inc="V")
    p.on("vector", lambda v: v.memset(cstb[0:1, 144:272], 1.0), inc="V")
    p.on("vector", lambda v: v.memset(c["smal"][0:1, 96:224], 1.0), inc="V")
    c["idb"] = cstb[:, 0:128]
    c["onesf"] = cst[:, 0:1]
    c["epscol"] = cst[:, 1:2]
    c["negone"] = cst[:, 2:3]
    c["onesb"] = cstb[:, 128:129]
    c["onesrowb"] = cstb[0:1, 144:272]
    c["onesrow"] = c["smal"][0:1, 96:224]
    c["idf"] = idf[:]

    CH = 1 << 20
    conv_tk = []
    if c["n_cores"] == 1:
        for h, dst in ((0, c["wb_allA"]), (1, c["wb_allB"])):
            n = WB_HALF_TOTAL
            for i in range((n + CH - 1) // CH):
                a, b = i * CH, min((i + 1) * CH, n)
                conv_tk.append(p.dma("gpsimd", dst[a:b],
                                     io["conv_src"][h * n + a:h * n + b]))
        p.wait_dma("gpsimd", conv_tk)
        m = p.mark("gpsimd")
        c["wb_ready_g"] = m
        _skip = True
    else:
        _skip = False
        nch = (WB_SHARD + CH - 1) // CH
        for i in range(nch):
            a, b = i * CH, min((i + 1) * CH, WB_SHARD)
            conv_tk.append(p.dma("gpsimd", c["wb_shard"][a:b],
                                 io["conv_src"][a:b]))
        p.wait_dma("gpsimd", conv_tk)
    if not _skip and c["n_cores"] > 1:
        p.on("gpsimd", lambda g: g.collective_compute(
            "AllGather", ALU.bypass,
            replica_groups=[list(range(c["n_cores"]))],
            ins=[c["wb_shard"][0:WB_HSH].opt()], outs=[c["wb_allA"].opt()]),
            inc="cc")
        p.on("gpsimd", lambda g: g.collective_compute(
            "AllGather", ALU.bypass,
            replica_groups=[list(range(c["n_cores"]))],
            ins=[c["wb_shard"][WB_HSH:].opt()], outs=[c["wb_allB"].opt()]),
            inc="cc")
        p.wait("gpsimd", "cc")
        c["wb_ready"] = ("cc", p.counts["cc"])
    if not _skip:
        m = p.mark("gpsimd")
        c["wb_ready_g"] = m
    p.wait_dma("sync", [tk_x, tk_f, tk_c, tk_g])
    sy, syv = p.mark("sync")
    vm = p.counts["V"]
    for e in ("tensor", "vector", "scalar"):
        p.wait(e, sy, syv)
        if e != "vector":
            p.wait(e, "V", vm)


def emit_epilogue(p, c):
    p.barrier()
    tk = p.dma("sync", c["io"]["out"].rearrange("(a p) t -> p a t", p=128),
               c["xs"][:])
    p.wait_dma("sync", tk)
    qeng = {"y": "sync", "c": "scalar", "g": "gpsimd", "v": "vector"}
    for s, v in sorted(p.counts.items()):
        if len(s) == 3 and s[0] == "d" and v > 0:
            p.wait(qeng[s[1]], s, v)


def emit_l0_attn(p, c):
    nc, wb, xs, hs = c["nc"], c["wb"], c["xs"], c["hs"]
    af, ab, row = c["af"], c["ab"], c["row"]
    wbuf, smal, stat = c["wbuf"], c["smal"], c["stat"]
    ps_a, ps_b = c["ps_a"], c["ps_b"]
    xt, ht = c["xtiles"], c["hstiles"]

    emit_ln(p, c, xt, ht)

    p.on("scalar", lambda s: s.activation(stat[:, 0:8], smal[:, 0:8], AF.Exp))
    p.on("scalar", lambda s: s.activation(stat[:, 0:8], stat[:, 0:8], AF.Exp,
                                          scale=c["negone"]))
    sdec = p.counts["S"]
    p.wait("vector", "S", sdec)

    wrvk = wb("L0wrvkT")
    wvb = wbuf[:, 0:6144].rearrange("p (b k f) -> p b k f", b=2, k=8)
    rsig = wbuf[:, 6144:8192]

    rwkv = [ab(i) for i in range(8)]
    fK, fV, fEK, fEKV, fEUK, fA, fB = 4, 5, 6, 7, 8, 9, 9

    p.wait("sync", c["wb_ready_g"][0], c["wb_ready_g"][1])
    p.flush("tensor")
    p.wait("sync", "T", p.counts["T"])
    last_mm_using = {}
    for ct in range(8):
        buf = ct % 2
        if ct >= 2:
            p.wait("sync", "T", last_mm_using[buf])
        tks = [p.dma("sync", wvb[:, buf, kt, :],
                     wrvk[bass.ts(kt, 128), bass.ts(ct, 384)])
               for kt in range(8)]
        p.wait_dma("tensor", tks)
        if ct == 0:
            p.wait("tensor", "V")
        for part, cols in (("v", 128), ("k", 256), ("r", 0)):
            for ch in range(4):
                pst = ps_a if ch % 2 == 0 else ps_b
                if (ct, part, ch) != (0, "v", 0):
                    p.wait("tensor", "V", p.counts["V"])
                    p.wait("tensor", "S", p.counts["S"])
                for kt in range(8):
                    p.on("tensor", lambda t, buf=buf, kt=kt, cols=cols, ch=ch,
                         pst=pst: t.matmul(
                             pst[:, 0:512],
                             wvb[:, buf, kt, cols:cols + 128],
                             ht[kt][:, bass.ts(ch, 512)],
                             start=(kt == 0), stop=(kt == 7)), inc="T")
                tv = p.counts["T"]
                if part == "v":
                    p.wait("vector", "T", tv)
                    p.on("vector", lambda v, ch=ch, pst=pst: v.tensor_copy(
                        af(5)[:, bass.ts(ch, 512)], pst[:, 0:512]))
                elif part == "k":
                    p.wait("vector", "T", tv)
                    p.on("vector", lambda v, ch=ch, pst=pst: v.tensor_copy(
                        af(4)[:, bass.ts(ch, 512)], pst[:, 0:512]))
                else:
                    p.wait("scalar", "T", tv)
                    p.on("scalar", lambda s, ch=ch, pst=pst: s.activation(
                        rsig[:, bass.ts(ch, 512)], pst[:, 0:512], AF.Sigmoid))
        last_mm_using[buf] = p.counts["T"]
        p.wait("scalar", "V")
        p.on("scalar", lambda s: s.activation(af(6), af(4), AF.Exp))
        p.on("scalar", lambda s, ct=ct: s.activation(
            af(8), af(4), AF.Exp, bias=smal[:, 8 + ct:9 + ct]))
        sk = p.counts["S"]
        p.wait("vector", "S", sk)
        p.on("vector", lambda v: v.tensor_tensor(af(7), af(6), af(5), ALU.mult))
        dbc = stat[:, ct:ct + 1].broadcast_to([128, T])
        p.on("vector", lambda v, dbc=dbc: v.tensor_tensor_scan(
            af(9), dbc, af(7), 0.0, ALU.mult, ALU.add))
        p.on("vector", lambda v, dbc=dbc: v.tensor_tensor_scan(
            af(4), dbc, af(6), 0.0, ALU.mult, ALU.add))
        p.on("vector", lambda v: v.tensor_tensor(af(5), af(8), af(5), ALU.mult))
        p.on("vector", lambda v: v.tensor_tensor(
            af(5)[:, 1:T], af(5)[:, 1:T], af(9)[:, 0:T - 1], ALU.add))
        p.on("vector", lambda v: v.tensor_copy(af(6)[:, 0:1], af(8)[:, 0:1]))
        p.on("vector", lambda v: v.tensor_tensor(
            af(6)[:, 1:T], af(8)[:, 1:T], af(4)[:, 0:T - 1], ALU.add))
        p.on("vector", lambda v: v.reciprocal(af(8), af(6)))
        p.on("vector", lambda v: v.tensor_tensor(af(5), af(5), af(8), ALU.mult))
        p.wait("vector", "S")
        p.on("vector", lambda v, ct=ct: v.tensor_tensor(
            rwkv[ct], af(5), rsig[:, 0:T], ALU.mult))
    vwkv = p.counts["V"]

    wout = wb("L0woutT")
    wob = wbuf[:].rearrange("p (b k f) -> p b k f", b=2, k=4)
    p.wait("sync", "T")
    p.wait("sync", "V")
    tks = [p.dma("sync", wob[:, half, kt, :],
                 wout[bass.ts(half * 4 + kt, 128), :])
           for half in range(2) for kt in range(4)]
    p.wait_dma("tensor", tks)
    p.wait("tensor", "V", vwkv)
    for mt in range(8):
        for ch in range(4):
            pst = ps_a if (mt * 4 + ch) % 2 == 0 else ps_b
            if (mt, ch) != (0, 0):
                p.wait("tensor", "V", p.counts["V"])
            for kt in range(8):
                p.on("tensor", lambda t, mt=mt, ch=ch, kt=kt, pst=pst: t.matmul(
                    pst[:, 0:512],
                    wob[:, kt // 4, kt % 4, bass.ts(mt, 128)],
                    ab(kt)[:, bass.ts(ch, 512)],
                    start=(kt == 0), stop=(kt == 7)), inc="T")
            tv = p.counts["T"]
            p.wait("vector", "T", tv)
            p.on("vector", lambda v, mt=mt, ch=ch, pst=pst: v.tensor_tensor(
                xt[mt][:, bass.ts(ch, 512)], xt[mt][:, bass.ts(ch, 512)],
                pst[:, 0:512], ALU.add))
    p.flush("tensor")
    p.mark("vector")


def emit_l0(p, c, parts=("f1", "a", "f2")):
    if "f1" in parts:
        emit_l0_ffn(p, c, 0)
    if "a" in parts:
        emit_l0_attn(p, c)
    if "f2" in parts:
        emit_l0_ffn(p, c, 1)


def emit_l0_ffn(p, c, slot):
    nc, wb, xs, hs = c["nc"], c["wb"], c["xs"], c["hs"]
    af, ab, row = c["af"], c["ab"], c["row"]
    wbuf, smal, stat, cst = c["wbuf"], c["smal"], c["stat"], c["cst"]
    ps_a, ps_b, ps_c, ps_d, ps_big = (c["ps_a"], c["ps_b"], c["ps_c"],
                                      c["ps_d"], c["ps_big"])
    xt, ht = c["xtiles"], c["hstiles"]
    onesb, onesf, onesrow = c["onesb"], c["onesf"], c["onesrow"]

    emit_ln(p, c, xt, ht)
    vln = p.counts["V"]

    for dt in range(DT):
        p.on("vector", lambda v, dt=dt: v.scalar_tensor_tensor(
            xt[dt], ht[dt], 0.5, xt[dt], ALU.mult, ALU.add))

    p.wait("scalar", "V", vln)
    for dt in range(DT):
        p.on("scalar", lambda s, dt=dt: s.activation(af(dt), ht[dt], AF.Square))
    ssq = p.counts["S"]
    p.wait("tensor", "V", vln)
    for tt in range(TT):
        for dt in range(DT):
            p.on("tensor", lambda t, tt=tt, dt=dt: t.matmul(
                ps_c[:, tt:tt + 1], ht[dt][:, bass.ts(tt, 128)], onesb,
                start=(tt == 0 and dt == 0), stop=(tt == TT - 1 and dt == DT - 1)),
                inc="T")
    p.wait("tensor", "S", ssq)
    for tt in range(TT):
        for dt in range(DT):
            p.on("tensor", lambda t, tt=tt, dt=dt: t.matmul(
                ps_d[:, tt:tt + 1], af(dt)[:, bass.ts(tt, 128)], onesf,
                start=(tt == 0 and dt == 0), stop=(tt == TT - 1 and dt == DT - 1)),
                inc="T")
    p.flush("tensor")
    tsum = p.counts["T"]
    p.wait("vector", "T", tsum)
    p.on("vector", lambda v: v.tensor_scalar(
        stat[:, 16:32], ps_c[:, 0:16], 1.0 / D, None, ALU.mult))
    p.on("vector", lambda v: v.tensor_tensor(
        stat[:, 48:64], stat[:, 16:32], stat[:, 16:32], ALU.mult))
    p.on("vector", lambda v: v.scalar_tensor_tensor(
        stat[:, 32:48], ps_d[:, 0:16], 1.0 / D, stat[:, 48:64], ALU.mult,
        ALU.subtract))
    vvar = p.counts["V"]
    p.wait("scalar", "V", vvar)
    p.on("scalar", lambda s: s.activation(
        stat[:, 48:64], stat[:, 32:48], AF.Ln, bias=onesf))
    slv = p.counts["S"]
    p.wait("vector", "S", slv)
    p.on("vector", lambda v: v.tensor_reduce(
        stat[:, 16:17], stat[:, 48:64], axis=AX.X, op=ALU.add))
    vred = p.counts["V"]
    p.wait("tensor", "V", vred)
    p.on("tensor", lambda t: t.matmul(ps_c[0:1, 32:33], stat[:, 16:17], onesf,
                                      start=True, stop=True), inc="T")
    p.flush("tensor")
    tps = p.counts["T"]
    p.wait("vector", "T", tps)
    p.on("vector", lambda v: v.tensor_copy(stat[0:1, 17:18], ps_c[0:1, 32:33]))
    vpart = p.counts["V"]
    p.wait("gpsimd", "V", vpart)
    ar_off = 4096 + slot
    tk = p.dma("gpsimd", c["ar_in"][ar_off:ar_off + 1], stat[0:1, 17:18])
    p.wait_dma("gpsimd", tk)
    if c["n_cores"] > 1:
        p.on("gpsimd", lambda g: g.collective_compute(
            "AllReduce", ALU.add, replica_groups=[list(range(c["n_cores"]))],
            ins=[c["ar_in"][ar_off:ar_off + 1].opt()],
            outs=[c["ar_out"][ar_off:ar_off + 1].opt()]), inc="cc")
        p.wait("gpsimd", "cc")
    else:
        tk = p.dma("gpsimd", c["ar_out"][ar_off:ar_off + 1],
                   c["ar_in"][ar_off:ar_off + 1])
        p.wait_dma("gpsimd", tk)
    tk_ar = p.dma("gpsimd", stat[0:1, 18:19], c["ar_out"][ar_off:ar_off + 1])
    garv = p.mark("gpsimd")

    p.wait("sync", c["wb_ready_g"][0], c["wb_ready_g"][1])
    p.flush("tensor")
    p.wait("sync", "T", p.counts["T"])
    w1t = wb("L0uncw1T")
    tk_w1 = [p.dma("sync", wbuf[:, bass.ts(kt, 256)], w1t[bass.ts(kt, 128), :])
             for kt in range(8)]
    w2t = wb("L0uncw2T")
    tk_w2 = [p.dma("sync", wbuf[:, 2048 + kt:2049 + kt],
                   w2t[bass.ts(kt, 128), :]) for kt in range(2)]
    qpt = wb("L0qprojT")
    tk_qp = [p.dma("sync", wbuf[:, 2304 + kt * 128:2304 + (kt + 1) * 128],
                   qpt[bass.ts(kt, 128), :]) for kt in range(8)]
    tk_rt = p.dma("sync", wbuf[:, 3328:3376], wb("L0routerT"))
    tk_ak = p.dma("sync", wbuf[:, 4096:8192], wb("L0auxkT"))
    p.wait("sync", "T", tsum)
    tk_on = p.dma("sync", ab(0)[48:49, :], wb("L0ones"))
    tk_mk = [p.dma("sync", ab(1 + i)[48:49, :],
                   wb("L0maskrow")[:, bass.ts(i, 2048)]) for i in range(2)]

    p.wait_dma("tensor", tk_w1)
    p.wait("tensor", "V", vvar)
    nch = 0
    for mt in range(2):
        for ch in range(4):
            pst = ps_a if nch % 2 == 0 else ps_b
            if nch >= 2:
                p.wait("tensor", "S", p.counts["S"])
            for kt in range(8):
                p.on("tensor", lambda t, mt=mt, ch=ch, kt=kt, pst=pst: t.matmul(
                    pst[:, 0:512],
                    wbuf[:, kt * 256 + mt * 128:kt * 256 + (mt + 1) * 128],
                    ht[kt][:, bass.ts(ch, 512)],
                    start=(kt == 0), stop=(kt == 7)), inc="T")
            p.flush("tensor")
            tv = p.counts["T"]
            p.wait("scalar", "T", tv)
            p.on("scalar", lambda s, mt=mt, ch=ch, pst=pst: s.activation(
                ab(4 + mt)[:, bass.ts(ch, 512)], pst[:, 0:512], AF.Gelu,
                bias=smal[:, 16 + mt:17 + mt]))
            nch += 1
    sh1 = p.counts["S"]
    p.wait("tensor", "S", sh1)
    p.wait_dma("tensor", tk_w2)
    for tt in range(TT):
        for kt in range(2):
            p.on("tensor", lambda t, tt=tt, kt=kt: t.matmul(
                ps_c[:, 16 + tt:17 + tt], ab(4 + kt)[:, bass.ts(tt, 128)],
                wbuf[:, 2048 + kt:2049 + kt],
                start=(tt == 0 and kt == 0), stop=(tt == TT - 1 and kt == 1)),
                inc="T")
    p.on("tensor", lambda t: t.matmul(ps_d[:, 0:1], onesrow,
                                      smal[0:1, 55:56], start=True, stop=True),
         inc="T")
    p.on("tensor", lambda t: t.matmul(ps_d[:, 1:2], onesrow,
                                      smal[0:1, 56:57], start=True, stop=True),
         inc="T")
    p.on("tensor", lambda t: t.matmul(ps_d[:, 2:3], onesrow,
                                      smal[0:1, 57:58], start=True, stop=True),
         inc="T")
    p.flush("tensor")
    tlearn = p.counts["T"]
    p.wait("vector", "T", tlearn)
    p.on("vector", lambda v: v.tensor_copy(stat[:, 44:47], ps_d[:, 0:3]))
    vb2 = p.counts["V"]
    p.wait("scalar", "V", vb2)
    p.on("scalar", lambda s: s.activation(
        stat[:, 0:16], ps_c[:, 16:32], AF.Sigmoid, bias=stat[:, 44:45]))
    ssig = p.counts["S"]

    p.wait_dma("tensor", tk_qp)
    p.wait("tensor", "S", ssig)
    for ch in range(4):
        pst = ps_a if ch % 2 == 0 else ps_b
        if ch >= 2:
            p.wait("tensor", "V", p.counts["V"])
        for kt in range(8):
            p.on("tensor", lambda t, ch=ch, kt=kt, pst=pst: t.matmul(
                pst[:, 0:512], wbuf[:, 2304 + kt * 128:2304 + (kt + 1) * 128],
                ht[kt][:, bass.ts(ch, 512)],
                start=(kt == 0), stop=(kt == 7)), inc="T")
        p.flush("tensor")
        tv = p.counts["T"]
        p.wait("vector", "T", tv)
        p.on("vector", lambda v, ch=ch, pst=pst: v.tensor_copy(
            ab(3)[:, bass.ts(ch, 512)], pst[:, 0:512]))
    vq = p.counts["V"]
    p.wait_dma("tensor", tk_rt)
    p.wait("tensor", "V", vq)
    for ch in range(4):
        pst = ps_a if ch % 2 == 0 else ps_b
        if ch >= 2:
            p.wait("tensor", "V", p.counts["V"])
        p.on("tensor", lambda t, ch=ch, pst=pst: t.matmul(
            pst[0:48, 0:512], wbuf[:, 3328:3376], ab(3)[:, bass.ts(ch, 512)],
            start=True, stop=True), inc="T")
        p.flush("tensor")
        tv = p.counts["T"]
        p.wait("vector", "T", tv)
        p.on("vector", lambda v, ch=ch, pst=pst: v.tensor_scalar(
            ab(0)[0:48, bass.ts(ch, 512)], pst[0:48, 0:512],
            0.14433756729740643, None, ALU.mult))
    p.wait_dma("tensor", tk_ak)
    for kb in range(8):
        pst = ps_a if kb % 2 == 0 else ps_b
        p.wait("tensor", "V", p.counts["V"])
        p.on("tensor", lambda t, kb=kb, pst=pst: t.matmul(
            pst[0:48, 0:512], wbuf[:, 3328:3376],
            wbuf[:, 4096 + kb * 512:4096 + (kb + 1) * 512],
            start=True, stop=True), inc="T")
        p.flush("tensor")
        tv = p.counts["T"]
        p.wait("vector", "T", tv)
        p.on("vector", lambda v, kb=kb, pst=pst: v.tensor_copy(
            ab(1)[0:48, bass.ts(kb, 512)] if kb < 4 else
            ab(2)[0:48, bass.ts(kb - 4, 512)], pst[0:48, 0:512]))
    vrk = p.counts["V"]
    t_wbuf_free = p.counts["T"]

    p.wait("vector", "G", garv[1])
    p.wait_dma("vector", tk_ar)
    p.on("vector", lambda v: v.tensor_scalar(
        stat[0:1, 18:19], stat[0:1, 18:19], 1.0 / 16384.0, 1e-6, ALU.mult,
        ALU.add))
    p.on("vector", lambda v: v.reciprocal(stat[0:1, 18:19], stat[0:1, 18:19]))
    vrg = p.counts["V"]
    p.wait("tensor", "V", vrg)
    p.on("tensor", lambda t: t.matmul(ps_d[:, 3:4], onesrow,
                                      stat[0:1, 18:19], start=True, stop=True),
         inc="T")
    p.flush("tensor")
    trg = p.counts["T"]
    p.wait("vector", "T", trg)
    p.on("vector", lambda v: v.tensor_copy(stat[:, 47:48], ps_d[:, 3:4]))
    p.on("vector", lambda v: v.scalar_tensor_tensor(
        stat[:, 16:32], stat[:, 48:64], 0.5,
        stat[:, 47:48].broadcast_to([128, 1, 16]).squeeze(1)
        if False else stat[:, 47:48].broadcast_to([128, 16]),
        ALU.mult, ALU.mult))
    p.on("vector", lambda v: v.scalar_tensor_tensor(
        stat[:, 16:32], stat[:, 0:16], 2.5, stat[:, 16:32], ALU.mult, ALU.add))
    p.on("vector", lambda v: v.tensor_scalar(
        stat[:, 16:32], stat[:, 16:32], 0.0, 5.0, ALU.max, ALU.min))
    vunc = p.counts["V"]
    p.wait("scalar", "V", vunc)
    p.on("scalar", lambda s: s.activation(
        stat[:, 0:16], stat[:, 16:32], AF.Sigmoid, bias=stat[:, 46:47],
        scale=stat[:, 45:46]))
    sgate = p.counts["S"]
    p.wait("vector", "S", sgate)

    p.wait_dma("tensor", [tk_on] + tk_mk)
    auxv = wb("L0auxv")
    wT = ab(12, 8).rearrange("p (k t) -> p k t", k=32)
    sc_mark = [p.counts["V"], p.counts["V"]]
    for g4 in range(4):
        for i4 in range(4):
            tt = g4 * 4 + i4
            ps_sc = [[ps_big[:, bass.ts(i, 512)] for i in range(4)],
                     [ps_a[:, 0:512], ps_b[:, 0:512], ps_c[:, 0:512],
                      ps_d[:, 0:512]]]
            for half in range(2):
                if sc_mark[half] > 0:
                    p.wait("tensor", "V", sc_mark[half])
                for kb4 in range(4):
                    kb = half * 4 + kb4
                    p.on("tensor", lambda t, tt=tt, kb=kb, kb4=kb4, half=half:
                         t.matmul(
                             ps_sc[half][kb4],
                             ab(0)[0:49, bass.ts(tt, 128)],
                             (ab(1) if kb < 4 else
                              ab(2))[0:49, bass.ts(kb % 4, 512)],
                             start=True, stop=True), inc="T")
                tv = p.counts["T"]
                p.wait("vector", "T", tv)
                for kb4 in range(4):
                    p.on("vector", lambda v, half=half, kb4=kb4: v.tensor_copy(
                        af(6 + half)[:, bass.ts(kb4, 512)], ps_sc[half][kb4]))
                sc_mark[half] = p.counts["V"]
            p.on("vector", lambda v: v.max(stat[:, 32:40], af(6, 2)))
            p.on("vector", lambda v: v.tensor_scalar(
                stat[:, 40:41], stat[:, 32:33], -1.0, None, ALU.mult))
            ve = p.counts["V"]
            p.wait("scalar", "V", ve)
            p.on("scalar", lambda s: s.activation(af(8), af(6), AF.Exp,
                                                  bias=stat[:, 40:41]))
            p.on("scalar", lambda s: s.activation(af(9), af(7), AF.Exp,
                                                  bias=stat[:, 40:41]))
            se = p.counts["S"]
            p.wait("vector", "S", se)
            p.on("vector", lambda v: v.tensor_scalar(
                af(6), af(6), stat[:, 39:40], None, ALU.is_ge))
            p.on("vector", lambda v: v.tensor_scalar(
                af(7), af(7), stat[:, 39:40], None, ALU.is_ge))
            p.on("vector", lambda v: v.tensor_tensor(af(8), af(8), af(6),
                                                     ALU.mult))
            p.on("vector", lambda v: v.tensor_tensor(af(9), af(9), af(7),
                                                     ALU.mult))
            p.on("vector", lambda v: v.tensor_reduce(
                stat[:, 41:42], af(8, 2), axis=AX.X, op=ALU.add))
            p.on("vector", lambda v: v.reciprocal(stat[:, 41:42],
                                                  stat[:, 41:42]))
            p.on("vector", lambda v, tt=tt: v.scalar_tensor_tensor(
                stat[:, 42:43], stat[:, tt:tt + 1], 0.5, stat[:, 41:42],
                ALU.mult, ALU.mult))
            p.on("vector", lambda v, i4=i4: v.tensor_scalar(
                ab(4 + 2 * i4), af(8), stat[:, 42:43], None, ALU.mult))
            p.on("vector", lambda v, i4=i4: v.tensor_scalar(
                ab(5 + 2 * i4), af(9), stat[:, 42:43], None, ALU.mult))
        vw = p.counts["V"]
        p.wait("tensor", "V", vw)
        pst4 = [ps_a, ps_b, ps_c, ps_d]
        vtm = [0, 0, 0, 0]
        w = 0
        for i4 in range(4):
            for kh in range(4):
                pst = pst4[w % 4][:].bitcast(BF16)
                if vtm[w % 4] > 0:
                    p.wait("tensor", "V", vtm[w % 4])
                for k8 in range(8):
                    kt = kh * 8 + k8
                    p.on("tensor", lambda t, i4=i4, kt=kt, k8=k8, pst=pst:
                         t.transpose(
                             pst[:, k8 * 128:(k8 + 1) * 128],
                             ab(4 + 2 * i4 + kt // 16)[:, bass.ts(kt % 16, 128)],
                             c["idb"]), inc="T")
                tv = p.counts["T"]
                p.wait("vector", "T", tv)
                p.on("vector", lambda v, i4=i4, kh=kh, pst=pst: v.tensor_copy(
                    wT[:, kh * 8:(kh + 1) * 8, bass.ts(i4, 128)],
                    pst.rearrange("p (a b) -> p a b", a=8)))
                vtm[w % 4] = p.counts["V"]
                w += 1
        vtr = p.counts["V"]
        p.wait("tensor", "V", vtr)
        psum_of = [ps_big[:, 0:512], ps_big[:, 512:1024], ps_big[:, 1024:1536],
                   ps_big[:, 1536:2048], ps_a[:, 0:512], ps_b[:, 0:512],
                   ps_c[:, 0:512], ps_d[:, 0:512]]
        def agg_load(hw):
            buf = hw % 2
            mk = c.get("l0agg_mark", {}).get(buf, 0)
            if mk > 0:
                p.wait("sync", "T", mk)
                p.wait("gpsimd", "T", mk)
            return [p.dma("sync" if kt < 2 else "gpsimd",
                          wbuf[:, buf * 4096 + kt * 1024:
                               buf * 4096 + (kt + 1) * 1024],
                          auxv[bass.ts(hw * 4 + kt, 128), :])
                    for kt in range(4)]
        if g4 == 0:
            c["l0agg_mark"] = {0: t_wbuf_free, 1: t_wbuf_free}
        tks = {0: agg_load(0), 1: agg_load(1)}
        for hw in range(8):
            buf = hw % 2
            p.wait_dma("tensor", tks.pop(hw))
            for dt in range(DT):
                for kt in range(4):
                    p.on("tensor", lambda t, dt=dt, kt=kt, hw=hw, buf=buf:
                         t.matmul(
                             psum_of[dt],
                             wbuf[:, buf * 4096 + kt * 1024 + dt * 128:
                                  buf * 4096 + kt * 1024 + (dt + 1) * 128],
                             wT[:, hw * 4 + kt, :],
                             start=(hw == 0 and kt == 0),
                             stop=(hw == 7 and kt == 3)), inc="T")
            c["l0agg_mark"][buf] = p.counts["T"]
            if hw + 2 < 8:
                tks[hw + 2] = agg_load(hw + 2)
        tagg = p.counts["T"]
        p.wait("vector", "T", tagg)
        for dt in range(DT):
            p.on("vector", lambda v, dt=dt, g4=g4: v.tensor_tensor(
                xt[dt][:, bass.ts(g4, 512)], xt[dt][:, bass.ts(g4, 512)],
                psum_of[dt], ALU.add))
        if g4 < 3:
            p.wait("tensor", "V", p.counts["V"])
    p.flush("tensor")
    p.mark("vector")


def emit_l1_ffn(p, c, slot):
    nc, wb, xs, hs = c["nc"], c["wb"], c["xs"], c["hs"]
    af, ab, row = c["af"], c["ab"], c["row"]
    wbuf, smal, stat, cst = c["wbuf"], c["smal"], c["stat"], c["cst"]
    ps_a, ps_b, ps_c, ps_d, ps_big = (c["ps_a"], c["ps_b"], c["ps_c"],
                                      c["ps_d"], c["ps_big"])
    xt, ht = c["xtiles"], c["hstiles"]
    onesb, onesf, onesrow = c["onesb"], c["onesf"], c["onesrow"]

    emit_ln(p, c, xt, ht)
    vln = p.counts["V"]

    p.wait("sync", c["wb_ready_g"][0], c["wb_ready_g"][1])
    p.flush("tensor")
    p.wait("sync", "T", p.counts["T"])
    tk_rt = p.dma("sync", wbuf[:, 0:56].rearrange("p (k f) -> p k f", k=8),
                  wb("L1routerT").rearrange("(k p) f -> p k f", p=128))
    p.on("vector", lambda v: v.memset(smal[:, 224:352], -1.0e38))
    vpad = p.counts["V"]
    p.wait_dma("tensor", tk_rt)
    p.wait("tensor", "V", vln)
    for tt in range(TT):
        for kt in range(8):
            p.on("tensor", lambda t, tt=tt, kt=kt: t.matmul(
                ps_c[:, tt * 7:(tt + 1) * 7], ht[kt][:, bass.ts(tt, 128)],
                wbuf[:, kt * 7:(kt + 1) * 7],
                start=(tt == 0 and kt == 0), stop=(tt == TT - 1 and kt == 7)),
                inc="T")
    p.flush("tensor")
    tlg = p.counts["T"]
    p.wait("vector", "T", tlg)
    p.wait("vector", "V", vpad)
    for tt in range(TT):
        p.on("vector", lambda v, tt=tt: v.tensor_copy(
            smal[:, 224 + tt * 8:224 + tt * 8 + 7],
            ps_c[:, tt * 7:(tt + 1) * 7]))
    for tt in range(TT):
        lg = smal[:, 224 + tt * 8:224 + tt * 8 + 8]
        p.on("vector", lambda v, lg=lg: v.max(stat[:, 0:8], lg))
        p.on("vector", lambda v: v.tensor_scalar(
            stat[:, 8:9], stat[:, 0:1], -1.0, None, ALU.mult))
        p.on("vector", lambda v, lg=lg: v.tensor_scalar(
            stat[:, 16:24], lg, stat[:, 1:2], None, ALU.is_ge))
        vge = p.counts["V"]
        p.wait("scalar", "V", vge)
        p.on("scalar", lambda s: s.activation(
            stat[:, 9:10], stat[:, 1:2], AF.Exp, bias=stat[:, 8:9]))
        p.on("scalar", lambda s, lg=lg: s.activation(
            stat[:, 24:32], lg, AF.Exp, bias=stat[:, 8:9]))
        sexp = p.counts["S"]
        p.wait("vector", "S", sexp)
        p.on("vector", lambda v: v.tensor_scalar(
            stat[:, 9:10], stat[:, 9:10], 1.0, None, ALU.add))
        p.on("vector", lambda v: v.reciprocal(stat[:, 9:10], stat[:, 9:10]))
        p.on("vector", lambda v: v.tensor_tensor(
            stat[:, 24:32], stat[:, 24:32], stat[:, 16:24], ALU.mult))
        p.on("vector", lambda v, lg=lg: v.tensor_scalar(
            lg, stat[:, 24:32], stat[:, 9:10], None, ALU.mult))
    vgates = p.counts["V"]
    p.wait("tensor", "V", vgates)
    prev_tk = None
    for e in range(7):
        gcol = smal[:, 224 + e:224 + e + 121:8].rearrange("p f -> p f")
        pst = ps_a if e % 2 == 0 else ps_b
        if e >= 2:
            p.wait("tensor", "V", p.counts["V"])
        p.on("tensor", lambda t, gcol=gcol, pst=pst: t.transpose(
            pst[0:16, 0:128], gcol, c["idf"]), inc="T")
        p.flush("tensor")
        tv = p.counts["T"]
        p.wait("vector", "T", tv)
        if prev_tk is not None:
            p.wait_dma("vector", prev_tk)
        p.on("vector", lambda v, pst=pst: v.tensor_copy(
            smal[0:16, 352:480], pst[0:16, 0:128]))
        vv = p.counts["V"]
        p.wait("gpsimd", "V", vv)
        prev_tk = p.dma("gpsimd", c["grow"][e:e + 1, :].rearrange(
            "a (b f) -> (a b) f", b=16), smal[0:16, 352:480])
    p.wait_dma("gpsimd", prev_tk)
    gmark = p.mark("gpsimd")

    wvgu = wbuf[:, 0:6144].rearrange("p (b f) -> p b f", b=3)
    wvd = wbuf[:, 6144:8192].rearrange("p (b f) -> p b f", b=4)
    hslot = lambda hq: c["arena"][:, hq * 512:(hq + 1) * 512]
    crep = af(8)[:, 0:512]
    gsus = ab(17)
    psum_mix = [ps_big[:, bass.ts(i, 512)] for i in range(4)]
    qname = ("sync", "gpsimd", "scalar")

    st = c.setdefault("l1state", {
        "gu": [0, 0, 0], "wd": [0, 0, 0, 0],
        "s": [0, 0], "v": [0, 0],
        "crep": 0, "vmix2": 0})

    def load_gu(j, wgt, wut):
        buf = j % 3
        if st["gu"][buf] > 0:
            p.wait("sync", "T", st["gu"][buf])
            p.wait("gpsimd", "T", st["gu"][buf])
        return [p.dma("sync", wvgu[:, buf, 0:1024],
                      wgt[j].rearrange("a b c -> a (b c)")),
                p.dma("gpsimd", wvgu[:, buf, 1024:2048],
                      wut[j].rearrange("a b c -> a (b c)"))]

    def load_d(j, wdt, half, q):
        buf = j % 4
        if st["wd"][buf] > 0:
            p.wait(q, "T", st["wd"][buf])
        return [p.dma(q, wvd[:, buf, 0:512],
                      wdt[bass.ts(j, 128), half * 512:(half + 1) * 512])]

    p.wait("sync", "G", gmark[1])
    for e in range(8):
        routed = e < 7
        wgt = wb("L1wg", e) if routed else wb("L1swg")
        wut = wb("L1wu", e) if routed else wb("L1swu")
        wdt = wb("L1wd", e) if routed else wb("L1swd")
        for c4 in range(4):
            if routed:
                if st["crep"] > 0:
                    p.wait("sync", "T", st["crep"])
                tk = p.dma("sync", smal[0:1, 512:1024],
                           c["grow"][e:e + 1, bass.ts(c4, 512)])
                p.wait_dma("tensor", tk)
                if st["v"][0] > 0:
                    p.wait("tensor", "V", st["v"][0])
                p.on("tensor", lambda t: t.matmul(
                    ps_d[:, 0:512], onesrow, smal[0:1, 512:1024],
                    start=True, stop=True), inc="T")
                st["crep"] = p.counts["T"]
                p.wait("vector", "T", st["crep"])
                p.on("vector", lambda v: v.tensor_copy(crep, ps_d[:, 0:512]))
                p.wait("tensor", "V", p.counts["V"])
            tks_gu = {j: load_gu(j, wgt, wut) for j in range(3)}
            tks_d = {j: load_d(j, wdt, 0, qname[2]) for j in range(4)}
            vh = {}
            for hq in range(32):
                buf, par = hq % 3, hq % 2
                pst_g = [ps_c, ps_a][par]
                pst_u = [ps_d, ps_b][par]
                p.wait_dma("tensor", tks_gu.pop(hq) + tks_d.pop(hq))
                if st["s"][par] > 0:
                    p.wait("tensor", "S", st["s"][par])
                if st["v"][par] > 0:
                    p.wait("tensor", "V", st["v"][par])
                for kt in range(8):
                    p.on("tensor", lambda t, buf=buf, kt=kt, c4=c4, pst_g=pst_g:
                         t.matmul(pst_g[:, 0:512],
                                  wvgu[:, buf, kt * 128:(kt + 1) * 128],
                                  ht[kt][:, bass.ts(c4, 512)],
                                  start=(kt == 0), stop=(kt == 7)), inc="T")
                tgv = p.counts["T"]
                for kt in range(8):
                    p.on("tensor", lambda t, buf=buf, kt=kt, c4=c4, pst_u=pst_u:
                         t.matmul(pst_u[:, 0:512],
                                  wvgu[:, buf, 1024 + kt * 128:
                                       1024 + (kt + 1) * 128],
                                  ht[kt][:, bass.ts(c4, 512)],
                                  start=(kt == 0), stop=(kt == 7)), inc="T")
                st["gu"][buf] = p.counts["T"]
                p.wait("scalar", "T", tgv)
                p.on("scalar", lambda s, par=par, pst_g=pst_g: s.activation(
                    gsus[:, par * 1024:par * 1024 + 512], pst_g[:, 0:512],
                    AF.Silu))
                st["s"][par] = p.counts["S"]
                p.wait("vector", "T", st["gu"][buf])
                if routed:
                    p.on("vector", lambda v, par=par, pst_u=pst_u:
                         v.tensor_tensor(
                             gsus[:, par * 1024 + 512:par * 1024 + 1024],
                             pst_u[:, 0:512], crep, ALU.mult))
                else:
                    p.on("vector", lambda v, par=par, pst_u=pst_u:
                         v.tensor_copy(
                             gsus[:, par * 1024 + 512:par * 1024 + 1024],
                             pst_u[:, 0:512]))
                st["v"][par] = p.counts["V"]
                p.wait("vector", "S", st["s"][par])
                p.on("vector", lambda v, hq=hq, par=par: v.tensor_tensor(
                    hslot(hq), gsus[:, par * 1024:par * 1024 + 512],
                    gsus[:, par * 1024 + 512:par * 1024 + 1024], ALU.mult))
                vh[hq] = p.counts["V"]
                if hq >= 1:
                    if hq == 1 and st["vmix2"] > 0:
                        p.wait("tensor", "V", st["vmix2"])
                    p.wait("tensor", "V", vh[hq - 1])
                    for dt in range(4):
                        p.on("tensor", lambda t, dt=dt, bufm=(hq - 1) % 4,
                             hq=hq: t.matmul(
                                 psum_mix[dt], wvd[:, bufm, dt * 128:
                                                  (dt + 1) * 128],
                                 hslot(hq - 1), start=(hq - 1 == 0),
                                 stop=False), inc="T")
                    st["wd"][(hq - 1) % 4] = p.counts["T"]
                if hq + 3 < 32:
                    tks_gu[hq + 3] = load_gu(hq + 3, wgt, wut)
                    if hq >= 1:
                        tks_d[hq + 3] = load_d(hq + 3, wdt, 0, qname[2])
            p.wait("tensor", "V", vh[31])
            for dt in range(4):
                p.on("tensor", lambda t, dt=dt: t.matmul(
                    psum_mix[dt], wvd[:, 31 % 4, dt * 128:(dt + 1) * 128],
                    hslot(31), start=False, stop=True), inc="T")
            st["wd"][31 % 4] = p.counts["T"]
            tmix = p.counts["T"]
            p.wait("vector", "T", tmix)
            for dt in range(4):
                p.on("vector", lambda v, dt=dt, c4=c4: v.scalar_tensor_tensor(
                    xt[dt][:, bass.ts(c4, 512)], psum_mix[dt], 0.5,
                    xt[dt][:, bass.ts(c4, 512)], ALU.mult, ALU.add))
            vmix = p.counts["V"]
            tks2 = {j: load_d(j, wdt, 1, qname[j % 3]) for j in range(4)}
            for hq in range(32):
                p.wait_dma("tensor", tks2.pop(hq))
                if hq == 0:
                    p.wait("tensor", "V", vmix)
                for dt in range(4):
                    p.on("tensor", lambda t, dt=dt, hq=hq: t.matmul(
                        psum_mix[dt], wvd[:, hq % 4, dt * 128:(dt + 1) * 128],
                        hslot(hq), start=(hq == 0), stop=(hq == 31)), inc="T")
                st["wd"][hq % 4] = p.counts["T"]
                if hq + 4 < 32:
                    tks2[hq + 4] = load_d(hq + 4, wdt, 1, qname[(hq + 4) % 3])
            tmix2 = p.counts["T"]
            p.wait("vector", "T", tmix2)
            for dt in range(4):
                p.on("vector", lambda v, dt=dt, c4=c4: v.scalar_tensor_tensor(
                    xt[4 + dt][:, bass.ts(c4, 512)], psum_mix[dt], 0.5,
                    xt[4 + dt][:, bass.ts(c4, 512)], ALU.mult, ALU.add))
            st["vmix2"] = p.counts["V"]
    p.wait("tensor", "V", p.counts["V"])

    p.flush("tensor")
    p.mark("vector")


def emit_l1(p, c, parts=("f1", "a", "f2")):
    if "f1" in parts:
        emit_l1_ffn(p, c, 0)
    if "a" in parts:
        emit_l1_attn(p, c)
    if "f2" in parts:
        emit_l1_ffn(p, c, 1)


def emit_l1_attn(p, c):
    nc, wb, xs, hs = c["nc"], c["wb"], c["xs"], c["hs"]
    af, ab, row = c["af"], c["ab"], c["row"]
    wbuf, smal, stat, cst = c["wbuf"], c["smal"], c["stat"], c["cst"]
    ps_a, ps_b, ps_c, ps_d, ps_big = (c["ps_a"], c["ps_b"], c["ps_c"],
                                      c["ps_d"], c["ps_big"])
    xt, ht = c["xtiles"], c["hstiles"]

    emit_ln(p, c, xt, ht)
    vln = p.counts["V"]

    p.wait("gpsimd", "V", vln)
    p.on("gpsimd", lambda g: g.iota(af(6)[:, 0:256], pattern=[[1, 256]],
                                    base=0, channel_multiplier=-1,
                                    allow_small_or_imprecise_dtypes=True))
    p.on("gpsimd", lambda g: g.iota(smal[:, 864:865], pattern=[[0, 1]],
                                    base=0, channel_multiplier=1,
                                    allow_small_or_imprecise_dtypes=True))
    gio = p.mark("gpsimd")
    p.wait("vector", "G", gio[1])
    p.on("vector", lambda v: v.tensor_scalar(
        smal[:, 224:480], af(6)[:, 0:256], 0.0, None, ALU.is_ge))
    p.on("vector", lambda v: v.tensor_scalar(
        af(6)[:, 256:512], af(6)[:, 0:256], 128.0, None, ALU.is_le))
    p.on("vector", lambda v: v.tensor_tensor(
        smal[:, 224:480], smal[:, 224:480], af(6)[:, 256:512], ALU.mult))
    p.on("vector", lambda v: v.tensor_scalar(
        smal[:, 480:736], af(6)[:, 0:256], 0.0, None, ALU.is_le))
    p.on("vector", lambda v: v.tensor_scalar(
        af(6)[:, 256:512], af(6)[:, 0:256], -128.0, None, ALU.is_ge))
    p.on("vector", lambda v: v.tensor_tensor(
        smal[:, 480:736], smal[:, 480:736], af(6)[:, 256:512], ALU.mult))
    p.on("vector", lambda v: v.scalar_tensor_tensor(
        smal[:, 864:865], smal[:, 864:865], -1.0, cst[:, 0:1], ALU.mult,
        ALU.bypass))
    p.on("vector", lambda v: v.tensor_scalar(
        smal[:, 864:865], smal[:, 864:865], 255.0, None, ALU.add))
    p.on("vector", lambda v: v.memset(smal[:, 865:866], 127.0))
    mask1 = smal[:, 224:480]
    mask0 = smal[:, 480:736]

    def b3(apv, n, w):
        return apv.rearrange("p (o j) -> p o j", o=1).broadcast_to([128, n, w])

    p.wait("sync", c["wb_ready_g"][0], c["wb_ready_g"][1])
    wq = wb("L1wqkvT")
    wo = wb("L1woutT")
    wvb = wbuf[:, 0:6144].rearrange("p (b k f) -> p b k f", b=2, k=8)
    sc3 = af(6).rearrange("p (h j) -> p h j", h=8)
    vgT = ab(14)[0:16, 0:512]
    sc = af(6)[:, 0:256]
    scg = af(6)[:, 256:272]
    eb = ab(13)[:, 0:256]
    ebT = ab(13)[:, 256:512].rearrange("p (a b) -> p a b", a=2)
    vT = ab(13)[:, 512:640].rearrange("p (a b) -> p a b", a=2)
    ag = ab(13)[:, 640:656]
    agT = ab(13)[0:16, 656:784]

    for half in range(2):
        p.wait("tensor", "V", vln if half == 0 else p.counts["V"])
        t0h = p.counts["T"]
        last_mm = {0: t0h, 1: t0h}
        vq_mark = [0, 0]
        for g3 in range(4):
            buf = g3 % 2
            if last_mm[buf] > 0:
                p.wait("sync", "T", last_mm[buf])
            tks = [p.dma("sync", wvb[:, buf, kt, :],
                         wq[bass.ts(kt, 128),
                            half * 1536 + g3 * 384:half * 1536 + (g3 + 1) * 384])
                   for kt in range(8)]
            p.wait_dma("tensor", tks)
            for m3 in range(3):
                mt = g3 * 3 + m3
                for ch in range(4):
                    par = (m3 * 4 + ch) % 2
                    pst = ps_a if par == 0 else ps_b
                    if vq_mark[par] > 0:
                        p.wait("tensor", "V", vq_mark[par])
                    for kt in range(8):
                        p.on("tensor", lambda t, buf=buf, kt=kt, m3=m3, ch=ch,
                             pst=pst: t.matmul(
                                 pst[:, 0:512],
                                 wvb[:, buf, kt, m3 * 128:(m3 + 1) * 128],
                                 ht[kt][:, bass.ts(ch, 512)],
                                 start=(kt == 0), stop=(kt == 7)), inc="T")
                    tv = p.counts["T"]
                    p.wait("vector", "T", tv)
                    p.on("vector", lambda v, mt=mt, ch=ch, pst=pst:
                         v.tensor_copy(ab(mt)[:, bass.ts(ch, 512)],
                                       pst[:, 0:512]))
                    vq_mark[par] = p.counts["V"]
            last_mm[buf] = p.counts["T"]
        vqkv = p.counts["V"]

        p.wait("tensor", "V", vqkv)
        for hh in range(8):
            po_ = 64 * (hh % 2)
            vslc = ab(8 + hh // 2)[po_:po_ + 64, 0:1921:128]
            idslc = c["idb"][po_:po_ + 64, po_:po_ + 64]
            pst = (ps_c if hh % 2 == 0 else ps_d)[:].bitcast(BF16)
            if hh >= 2:
                p.wait("tensor", "V", p.counts["V"])
            p.on("tensor", lambda t, vslc=vslc, pst=pst, idslc=idslc:
                 t.transpose(pst[0:16, 0:64], vslc, idslc), inc="T")
            p.flush("tensor")
            tv = p.counts["T"]
            p.wait("vector", "T", tv)
            p.on("vector", lambda v, hh=hh, pst=pst: v.tensor_copy(
                vgT[:, hh * 64:(hh + 1) * 64], pst[0:16, 0:64]))
        vvg = p.counts["V"]
        p.wait("tensor", "V", vvg)

        for it in range(TT):
            i0 = it * 128
            wstart = 0 if it == 0 else i0 - 128
            msk = mask0 if it == 0 else mask1
            for dh in range(4):
                for hh2 in range(2):
                    hh = 2 * dh + hh2
                    po = 64 * hh2
                    qslc = ab(dh)
                    kslc = ab(4 + dh)
                    vslc = ab(8 + dh)
                    qs = qslc[po:po + 64, :]
                    ks = kslc[po:po + 64, :]
                    vs = vslc[po:po + 64, :]
                    p.wait("tensor", "V", p.counts["V"])
                    p.wait("tensor", "S", p.counts["S"])
                    p.on("tensor", lambda t, qs=qs, ks=ks, i0=i0,
                         wstart=wstart: t.matmul(
                             ps_a[:, 0:256], qs[:, i0:i0 + 128],
                             ks[:, wstart:wstart + 256], start=True, stop=True),
                         inc="T")
                    p.on("tensor", lambda t, qs=qs, ks=ks, i0=i0: t.matmul(
                        ps_a[:, 256:272], qs[:, i0:i0 + 128],
                        ks[:, 0:1921:128], start=True, stop=True), inc="T")
                    p.flush("tensor")
                    tsc = p.counts["T"]
                    p.wait("vector", "T", tsc)
                    p.on("vector", lambda v, msk=msk: v.scalar_tensor_tensor(
                        sc, ps_a[:, 0:256], 0.125, msk, ALU.mult, ALU.mult))
                    p.on("vector", lambda v: v.tensor_scalar(
                        scg, ps_a[:, 256:272], 0.125, None, ALU.mult))
                    p.on("vector", lambda v: v.tensor_reduce(
                        stat[:, 0:1], sc, axis=AX.X, op=ALU.max))
                    p.on("vector", lambda v: v.tensor_scalar(
                        stat[:, 1:2], stat[:, 0:1], -1.0, None, ALU.mult))
                    p.on("vector", lambda v: v.tensor_reduce(
                        stat[:, 3:4], scg, axis=AX.X, op=ALU.max))
                    p.on("vector", lambda v: v.tensor_scalar(
                        stat[:, 4:5], stat[:, 3:4], -1.0, None, ALU.mult))
                    vm = p.counts["V"]
                    p.wait("scalar", "V", vm)
                    p.on("scalar", lambda s: s.activation(
                        sc, sc, AF.Exp, bias=stat[:, 1:2]))
                    p.on("scalar", lambda s: s.activation(
                        scg, scg, AF.Exp, bias=stat[:, 4:5]))
                    sel = p.counts["S"]
                    p.wait("vector", "S", sel)
                    p.on("vector", lambda v, msk=msk: v.tensor_tensor(
                        sc, sc, msk, ALU.mult))
                    p.on("vector", lambda v: v.tensor_reduce(
                        stat[:, 2:3], sc, axis=AX.X, op=ALU.add))
                    p.on("vector", lambda v: v.reciprocal(stat[:, 2:3],
                                                          stat[:, 2:3]))
                    p.on("vector", lambda v: v.tensor_scalar(
                        eb, sc, stat[:, 2:3], None, ALU.mult))
                    p.on("vector", lambda v: v.tensor_reduce(
                        stat[:, 5:6], scg, axis=AX.X, op=ALU.add))
                    p.on("vector", lambda v: v.reciprocal(stat[:, 5:6],
                                                          stat[:, 5:6]))
                    p.on("vector", lambda v: v.tensor_scalar(
                        ag, scg, stat[:, 5:6], None, ALU.mult))
                    vsm = p.counts["V"]
                    p.wait("tensor", "V", vsm)
                    for jt in range(2):
                        pst = (ps_b if jt == 0 else ps_c)[:].bitcast(BF16)
                        p.on("tensor", lambda t, jt=jt, pst=pst: t.transpose(
                            pst[0:128, 0:128], eb[:, bass.ts(jt, 128)],
                            c["idb"]), inc="T")
                        idslc = c["idb"][po:po + 64, po:po + 64]
                        p.on("tensor", lambda t, jt=jt, pst=pst, vs=vs,
                             wstart=wstart, idslc=idslc: t.transpose(
                                 pst[0:128, 128:192],
                                 vs[:, wstart + jt * 128:wstart + (jt + 1) * 128],
                                 idslc), inc="T")
                    p.on("tensor", lambda t: t.transpose(
                        ps_d[:, 0:512].bitcast(BF16)[0:16, 0:128], ag,
                        c["idb"]), inc="T")
                    p.flush("tensor")
                    ttr = p.counts["T"]
                    p.wait("vector", "T", ttr)
                    for jt in range(2):
                        pst = (ps_b if jt == 0 else ps_c)[:].bitcast(BF16)
                        p.on("vector", lambda v, jt=jt, pst=pst: v.tensor_copy(
                            ebT[:, jt, :], pst[0:128, 0:128]))
                        p.on("vector", lambda v, jt=jt, pst=pst: v.tensor_copy(
                            vT[:, jt, :], pst[0:128, 128:192]))
                    p.on("vector", lambda v: v.tensor_copy(
                        agT, ps_d[:, 0:512].bitcast(BF16)[0:16, 0:128]))
                    vtr = p.counts["V"]
                    p.wait("tensor", "V", vtr)
                    for jt in range(2):
                        p.on("tensor", lambda t, jt=jt, po=po: t.matmul(
                            ps_big[po:po + 64, 0:128], vT[:, jt, :],
                            ebT[:, jt, :], start=(jt == 0), stop=False),
                            inc="T")
                    p.on("tensor", lambda t, hh=hh, po=po: t.matmul(
                        ps_big[po:po + 64, 0:128],
                        vgT[:, hh * 64:(hh + 1) * 64], agT,
                        start=False, stop=True), inc="T")
                p.flush("tensor")
                tav = p.counts["T"]
                p.wait("vector", "T", tav)
                p.on("vector", lambda v, dh=dh, it=it: v.tensor_copy(
                    ab(16 + dh)[:, bass.ts(it, 128)], ps_big[:, 0:128]))
        vattn = p.counts["V"]

        p.wait("sync", "T", last_mm[0])
        p.wait("sync", "T", last_mm[1])
        tks = [p.dma("sync", wbuf[:, kt * 1024:(kt + 1) * 1024],
                     wo[bass.ts(half * 4 + kt, 128), :]) for kt in range(4)]
        p.wait_dma("tensor", tks)
        p.wait("tensor", "V", vattn)
        vo_mark = [0, 0]
        for mt in range(8):
            for ch in range(4):
                par = (mt * 4 + ch) % 2
                pst = ps_a if par == 0 else ps_b
                if vo_mark[par] > 0:
                    p.wait("tensor", "V", vo_mark[par])
                for kt in range(4):
                    p.on("tensor", lambda t, kt=kt, mt=mt, ch=ch, pst=pst:
                         t.matmul(
                             pst[:, 0:512],
                             wbuf[:, kt * 1024 + mt * 128:
                                  kt * 1024 + (mt + 1) * 128],
                             ab(16 + kt)[:, bass.ts(ch, 512)],
                             start=(kt == 0), stop=(kt == 3)), inc="T")
                tv = p.counts["T"]
                p.wait("vector", "T", tv)
                if half == 0:
                    p.on("vector", lambda v, mt=mt, ch=ch, pst=pst:
                         v.tensor_tensor(
                             xt[mt][:, bass.ts(ch, 512)],
                             xt[mt][:, bass.ts(ch, 512)],
                             pst[:, 0:512], ALU.add))
                else:
                    p.on("vector", lambda v, mt=mt, ch=ch, pst=pst:
                         v.scalar_tensor_tensor(
                             xt[mt][:, bass.ts(ch, 512)], pst[:, 0:512],
                             smal[:, 18 + mt:19 + mt],
                             xt[mt][:, bass.ts(ch, 512)], ALU.add, ALU.add))
                vo_mark[par] = p.counts["V"]
    p.flush("tensor")
    p.mark("vector")


def emit_xallgather(p, c, to_seq):
    xs, xg_in, xg_out = c["xs"], c["xg_in"], c["xg_out"]
    n = c["n_cores"]
    assert n == NCORES, "layer 2 requires the full 8-core mesh"
    p.barrier()
    tk = p.dma("gpsimd", xg_in.rearrange("(a p t) -> p a t", p=128, a=DT), xs[:])
    p.wait_dma("gpsimd", tk)
    p.on("gpsimd", lambda g: g.collective_compute(
        "AllGather", ALU.bypass, replica_groups=[list(range(n))],
        ins=[xg_in.opt()], outs=[xg_out.opt()]), inc="cc")
    p.wait("gpsimd", "cc")
    gm = p.mark("gpsimd")
    p.wait("sync", "G", gm[1])
    xg6 = xg_out.rearrange("(r a p c t) -> c r a p t", r=n, a=DT, p=128, c=8)
    gixcol = c["gix"][:, 1:2]
    xgrows = xg_out.rearrange("(n t) -> n t", t=256)
    tks = []
    for r in range(n):
        for a in range(DT):
            eo = (r * 8192 + a * 1024) * 256
            def mk(eng, r=r, a=a, eo=eo, gixcol=gixcol, xgrows=xgrows, xs=xs):
                return eng.indirect_dma_start(
                    out=xs[:, a, r * 256:(r + 1) * 256], out_offset=None,
                    in_=xgrows, element_offset=eo,
                    in_offset=bass.IndirectOffsetOnAxis(gixcol, 0))
            tks.append(p.dma2("gpsimd", mk))
    p.wait_dma("gpsimd", tks)
    gm2 = p.mark("gpsimd")
    for e in ("tensor", "vector", "scalar", "sync"):
        p.wait(e, "G", gm2[1])


def emit_l2_ffn(p, c):
    nc, wb, xs, hs = c["nc"], c["wb"], c["xs"], c["hs"]
    af, ab, row = c["af"], c["ab"], c["row"]
    wbuf, smal, stat = c["wbuf"], c["smal"], c["stat"]
    ps_a, ps_b, ps_c, ps_d, ps_big = (c["ps_a"], c["ps_b"], c["ps_c"],
                                      c["ps_d"], c["ps_big"])
    xt, ht = c["xtiles"], c["hstiles"]
    onesb, onesf, onesrow = c["onesb"], c["onesf"], c["onesrow"]

    emit_ln(p, c, xt, ht)
    vln = p.counts["V"]

    p.wait("sync", c["wb_ready_g"][0], c["wb_ready_g"][1])
    p.flush("tensor")
    p.wait("sync", "T", p.counts["T"])
    tk_g = p.dma("sync", wbuf[:, 0:64].rearrange("p (k f) -> p k f", k=8),
                 wb("L2wgate").rearrange("(k p) f -> p k f", p=128))
    p.wait_dma("tensor", tk_g)
    p.wait("tensor", "V", vln)
    for tt in range(TT):
        for kt in range(8):
            p.on("tensor", lambda t, tt=tt, kt=kt: t.matmul(
                ps_c[:, tt * 8:(tt + 1) * 8], ht[kt][:, bass.ts(tt, 128)],
                wbuf[:, kt * 8:(kt + 1) * 8],
                start=(tt == 0 and kt == 0), stop=(tt == TT - 1 and kt == 7)),
                inc="T")
    p.flush("tensor")
    tlg = p.counts["T"]
    p.wait("vector", "T", tlg)
    for tt in range(TT):
        p.on("vector", lambda v, tt=tt: v.tensor_copy(
            smal[:, 224 + tt * 8:232 + tt * 8], ps_c[:, tt * 8:(tt + 1) * 8]))
    for tt in range(TT):
        lg = smal[:, 224 + tt * 8:232 + tt * 8]
        p.on("vector", lambda v, lg=lg: v.max(stat[:, 0:8], lg))
        p.on("vector", lambda v: v.tensor_scalar(
            stat[:, 8:9], stat[:, 0:1], -1.0, None, ALU.mult))
        p.on("vector", lambda v, lg=lg: v.tensor_scalar(
            stat[:, 16:24], lg, stat[:, 1:2], None, ALU.is_ge))
        vge = p.counts["V"]
        p.wait("scalar", "V", vge)
        p.on("scalar", lambda s: s.activation(
            stat[:, 9:10], stat[:, 1:2], AF.Exp, bias=stat[:, 8:9]))
        p.on("scalar", lambda s, lg=lg: s.activation(
            stat[:, 24:32], lg, AF.Exp, bias=stat[:, 8:9]))
        sexp = p.counts["S"]
        p.wait("vector", "S", sexp)
        p.on("vector", lambda v: v.tensor_scalar(
            stat[:, 9:10], stat[:, 9:10], 1.0, None, ALU.add))
        p.on("vector", lambda v: v.reciprocal(stat[:, 9:10], stat[:, 9:10]))
        p.on("vector", lambda v: v.tensor_tensor(
            stat[:, 24:32], stat[:, 24:32], stat[:, 16:24], ALU.mult))
        p.on("vector", lambda v: v.tensor_scalar(
            stat[:, 24:32], stat[:, 24:32], stat[:, 9:10], None, ALU.mult))
        p.on("vector", lambda v, lg=lg: v.tensor_scalar(
            lg, stat[:, 24:32], 0.25, None, ALU.mult))
    vgates = p.counts["V"]
    p.wait("tensor", "V", vgates)
    prev_tk = None
    for e in range(8):
        gcol = smal[:, 224 + e:224 + e + 121:8]
        pst = ps_a if e % 2 == 0 else ps_b
        if e >= 2:
            p.wait("tensor", "V", p.counts["V"])
        p.on("tensor", lambda t, gcol=gcol, pst=pst: t.transpose(
            pst[0:16, 0:128], gcol, c["idf"]), inc="T")
        p.flush("tensor")
        tv = p.counts["T"]
        p.wait("vector", "T", tv)
        if prev_tk is not None:
            p.wait_dma("vector", prev_tk)
        p.on("vector", lambda v, pst=pst: v.tensor_copy(
            smal[0:16, 352:480], pst[0:16, 0:128]))
        vv = p.counts["V"]
        p.wait("gpsimd", "V", vv)
        prev_tk = p.dma("gpsimd", c["grow"][e:e + 1, :].rearrange(
            "a (b f) -> (a b) f", b=16), smal[0:16, 352:480])
    p.wait_dma("gpsimd", prev_tk)
    gmark = p.mark("gpsimd")
    p.wait("sync", "G", gmark[1])

    p.flush("tensor")
    p.wait("sync", "T", p.counts["T"])
    tks = [p.dma("sync", wbuf[:, kt * 512:(kt + 1) * 512],
                 wb("L2wd")[bass.ts(kt, 128), :]) for kt in range(8)]
    p.wait_dma("tensor", tks)
    for mt in range(4):
        for ch in range(4):
            pst = ps_a if (mt * 4 + ch) % 2 == 0 else ps_b
            p.wait("tensor", "S", p.counts["S"])
            for kt in range(8):
                p.on("tensor", lambda t, kt=kt, mt=mt, ch=ch, pst=pst: t.matmul(
                    pst[:, 0:512],
                    wbuf[:, kt * 512 + mt * 128:kt * 512 + (mt + 1) * 128],
                    ht[kt][:, bass.ts(ch, 512)],
                    start=(kt == 0), stop=(kt == 7)), inc="T")
            p.flush("tensor")
            tv = p.counts["T"]
            p.wait("scalar", "T", tv)
            p.on("scalar", lambda s, mt=mt, ch=ch, pst=pst: s.activation(
                ab(mt)[:, bass.ts(ch, 512)], pst[:, 0:512], AF.Relu,
                bias=smal[:, 42 + mt:43 + mt]))
    sh = p.counts["S"]
    p.wait("vector", "S", sh)
    for mt in range(4):
        for ch in range(4):
            p.flush("tensor")
            p.wait("sync", "T", p.counts["T"])
            tka = p.dma("sync", smal[0:1, 512:1024],
                        c["grow"][2 * mt:2 * mt + 1, bass.ts(ch, 512)])
            tkb = p.dma("sync", c["row"](0)[:, 0:512],
                        c["grow"][2 * mt + 1:2 * mt + 2, bass.ts(ch, 512)])
            p.wait_dma("tensor", [tka, tkb])
            p.wait("tensor", "V", p.counts["V"])
            p.on("tensor", lambda t: t.matmul(
                ps_c[0:64, 0:512], onesrow[:, 0:64], smal[0:1, 512:1024],
                start=True, stop=True), inc="T")
            p.on("tensor", lambda t: t.matmul(
                ps_c[64:128, 0:512], onesrow[:, 0:64], c["row"](0)[:, 0:512],
                start=True, stop=True), inc="T")
            p.flush("tensor")
            tv = p.counts["T"]
            p.wait("vector", "T", tv)
            p.on("vector", lambda v, mt=mt, ch=ch: v.tensor_tensor(
                ab(mt)[:, bass.ts(ch, 512)], ab(mt)[:, bass.ts(ch, 512)],
                ps_c[:, 0:512], ALU.mult))
    vhg = p.counts["V"]
    tk_gr = p.dma("gpsimd", ab(4)[0:8, 0:2048],
                  c["grow"][0:8, :])
    p.flush("tensor")
    p.wait("sync", "T", p.counts["T"])
    tks = [p.dma("sync", wbuf[:, 4096 + kt * 1024:4096 + (kt + 1) * 1024],
                 wb("L2wu")[bass.ts(kt, 128), :]) for kt in range(4)]
    tkbu = p.dma("sync", ab(5)[0:8, 0:1024], wb("L2bu"))
    p.wait_dma("tensor", tks + [tk_gr, tkbu])
    p.wait("tensor", "V", vhg)
    for mt in range(8):
        for ch in range(4):
            pst = ps_a if (mt * 4 + ch) % 2 == 0 else ps_b
            p.wait("tensor", "V", p.counts["V"])
            for kt in range(4):
                p.on("tensor", lambda t, kt=kt, mt=mt, ch=ch, pst=pst: t.matmul(
                    pst[:, 0:512],
                    wbuf[:, 4096 + kt * 1024 + mt * 128:4096 + kt * 1024 + (mt + 1) * 128],
                    ab(kt)[:, bass.ts(ch, 512)],
                    start=(kt == 0), stop=False), inc="T")
            p.on("tensor", lambda t, mt=mt, ch=ch, pst=pst: t.matmul(
                pst[:, 0:512], ab(5)[0:8, mt * 128:(mt + 1) * 128],
                ab(4)[0:8, bass.ts(ch, 512)],
                start=False, stop=True), inc="T")
            p.flush("tensor")
            tv = p.counts["T"]
            p.wait("vector", "T", tv)
            p.on("vector", lambda v, mt=mt, ch=ch, pst=pst: v.tensor_tensor(
                xt[mt][:, bass.ts(ch, 512)], xt[mt][:, bass.ts(ch, 512)],
                pst[:, 0:512], ALU.add))
    p.flush("tensor")
    p.mark("vector")


def emit_l2_attn(p, c):
    nc, wb, xs, hs = c["nc"], c["wb"], c["xs"], c["hs"]
    af, ab, row = c["af"], c["ab"], c["row"]
    wbuf, smal, stat = c["wbuf"], c["smal"], c["stat"]
    ps_a, ps_b, ps_c, ps_d, ps_big = (c["ps_a"], c["ps_b"], c["ps_c"],
                                      c["ps_d"], c["ps_big"])
    xt, ht = c["xtiles"], c["hstiles"]

    hf32 = [af(dt) for dt in range(DT)]
    emit_ln(p, c, xt, hf32)
    vln = p.counts["V"]
    for dt in range(DT):
        p.on("vector", lambda v, dt=dt: v.tensor_tensor(
            xt[dt], xt[dt], hf32[dt], ALU.add))
    emit_ln(p, c, hf32, ht, sq_in_dst=True)
    for dt in range(DT):
        p.on("vector", lambda v, dt=dt: v.tensor_scalar(
            ht[dt], ht[dt], smal[:, 26 + dt:27 + dt], smal[:, 34 + dt:35 + dt],
            ALU.mult, ALU.add))
    vxin = p.counts["V"]

    p.wait("sync", c["wb_ready_g"][0], c["wb_ready_g"][1])
    wq = wb("L2wqkvT")
    wo = wb("L2woutT")
    fmt = wbuf[:, 6144:8192]
    def tm(comp, t):
        return ab(2 * comp + t // 8)[:, (t % 8) * 256:(t % 8) * 256 + 256]
    def outf(t):
        return af(6 + t // 4)[:, (t % 4) * 256:(t % 4) * 256 + 256]

    for qtr in range(4):
        for comp in range(3):
            for ft in range(2):
                col0 = comp * 1024 + qtr * 256 + ft * 128
                p.flush("tensor")
                p.wait("sync", "T", p.counts["T"])
                tks = [p.dma("sync", wbuf[:, kt * 128:(kt + 1) * 128],
                             wq[bass.ts(kt, 128), col0:col0 + 128])
                       for kt in range(8)]
                p.wait_dma("tensor", tks)
                if (qtr, comp, ft) == (0, 0, 0):
                    p.wait("tensor", "V", vxin)
                for ch in range(4):
                    pst = ps_a if ch % 2 == 0 else ps_b
                    p.wait("tensor", "V", p.counts["V"])
                    for kt in range(8):
                        p.on("tensor", lambda t, kt=kt, ch=ch, pst=pst: t.matmul(
                            pst[:, 0:512], wbuf[:, kt * 128:(kt + 1) * 128],
                            ht[kt][:, bass.ts(ch, 512)],
                            start=(kt == 0), stop=(kt == 7)), inc="T")
                    p.flush("tensor")
                    tv = p.counts["T"]
                    p.wait("vector", "T", tv)
                    if comp == 0:
                        p.on("vector", lambda v, ch=ch, pst=pst: v.tensor_scalar(
                            fmt[:, bass.ts(ch, 512)], pst[:, 0:512],
                            0.03125, None, ALU.mult))
                    else:
                        p.on("vector", lambda v, ch=ch, pst=pst: v.tensor_copy(
                            fmt[:, bass.ts(ch, 512)], pst[:, 0:512]))
                vfm = p.counts["V"]
                p.wait("tensor", "V", vfm)
                for t in range(16):
                    pst = (ps_c if t % 2 == 0 else ps_d)[:].bitcast(BF16)
                    if t >= 2:
                        p.wait("tensor", "V", p.counts["V"])
                    p.on("tensor", lambda t_, t=t, pst=pst: t_.transpose(
                        pst[0:128, 0:128], fmt[:, bass.ts(t, 128)], c["idb"]),
                        inc="T")
                    p.flush("tensor")
                    tv = p.counts["T"]
                    p.wait("vector", "T", tv)
                    p.on("vector", lambda v, t=t, comp=comp, ft=ft, pst=pst:
                         v.tensor_copy(tm(comp, t)[:, ft * 128:(ft + 1) * 128],
                                       pst[0:128, 0:128]))
        vtm = p.counts["V"]

        for nc2 in range(2):
            for i in range(8):
                ti = 2 * i + nc2
                for j in range(8):
                    tj = 2 * j + nc2
                    p.on("vector", lambda v, ti=ti, tj=tj: v.tensor_tensor(
                        af(5)[:, 0:256], tm(0, ti), tm(1, tj), ALU.mult))
                    p.on("vector", lambda v, j=j: v.tensor_reduce(
                        stat[:, 0:32].rearrange("p (h j) -> p h j", j=8)
                        [:, :, j:j + 1],
                        af(5)[:, 0:256].rearrange("p (h d) -> p h d", d=64),
                        axis=AX.X, op=ALU.add))
                Sv = stat[:, 0:32].rearrange("p (h j) -> p h j", j=8)
                p.on("vector", lambda v, Sv=Sv: v.tensor_reduce(
                    stat[:, 32:36][:, :, None], Sv, axis=AX.X, op=ALU.max))
                p.on("vector", lambda v: v.tensor_scalar(
                    stat[:, 36:40], stat[:, 32:36], -1.0, None, ALU.mult))
                p.on("vector", lambda v, Sv=Sv: v.tensor_tensor(
                    Sv, Sv, stat[:, 36:40][:, :, None].broadcast_to(
                        [128, 4, 8]), ALU.add))
                vs = p.counts["V"]
                p.wait("scalar", "V", vs)
                p.on("scalar", lambda s: s.activation(
                    stat[:, 0:32], stat[:, 0:32], AF.Exp))
                se = p.counts["S"]
                p.wait("vector", "S", se)
                p.on("vector", lambda v, Sv=Sv: v.tensor_reduce(
                    stat[:, 32:36][:, :, None], Sv, axis=AX.X, op=ALU.add))
                p.on("vector", lambda v: v.reciprocal(stat[:, 32:36],
                                                      stat[:, 32:36]))
                p.on("vector", lambda v, Sv=Sv: v.tensor_tensor(
                    Sv, Sv, stat[:, 32:36][:, :, None].broadcast_to(
                        [128, 4, 8]), ALU.mult))
                for j in range(8):
                    tj = 2 * j + nc2
                    a_b = stat[:, 0:32].rearrange("p (h j) -> p h j", j=8)[
                        :, :, j:j + 1].broadcast_to([128, 4, 64])
                    if j == 0:
                        p.on("vector", lambda v, ti=ti, tj=tj, a_b=a_b:
                             v.tensor_tensor(
                                 outf(ti).rearrange("p (h d) -> p h d", d=64),
                                 tm(2, tj).rearrange("p (h d) -> p h d", d=64),
                                 a_b, ALU.mult))
                    else:
                        p.on("vector", lambda v, tj=tj, a_b=a_b: v.tensor_tensor(
                            af(5)[:, 0:256].rearrange("p (h d) -> p h d", d=64),
                            tm(2, tj).rearrange("p (h d) -> p h d", d=64),
                            a_b, ALU.mult))
                        p.on("vector", lambda v, ti=ti: v.tensor_tensor(
                            outf(ti), outf(ti), af(5)[:, 0:256], ALU.add))
        vout = p.counts["V"]

        p.wait("tensor", "V", vout)
        for ft in range(2):
            for t in range(16):
                pst = ps_c if t % 2 == 0 else ps_d
                p.wait("tensor", "V", p.counts["V"])
                p.on("tensor", lambda t_, t=t, ft=ft, pst=pst: t_.transpose(
                    pst[:, 0:128], outf(t)[:, ft * 128:(ft + 1) * 128],
                    c["idf"]), inc="T")
                p.flush("tensor")
                tv = p.counts["T"]
                p.wait("vector", "T", tv)
                p.on("vector", lambda v, t=t, pst=pst: v.tensor_copy(
                    fmt[:, bass.ts(t, 128)],
                    (ps_c if t % 2 == 0 else ps_d)[:, 0:128]))
            vfm2 = p.counts["V"]
            kt = qtr * 2 + ft
            p.flush("tensor")
            p.wait("sync", "T", p.counts["T"])
            tkw = p.dma("sync", wbuf[:, 2048:3072], wo[bass.ts(kt, 128), :])
            p.wait_dma("tensor", tkw)
            p.wait("tensor", "V", vfm2)
            for mt in range(8):
                for ch in range(4):
                    pst = ps_a if (mt * 4 + ch) % 2 == 0 else ps_b
                    p.wait("tensor", "V", p.counts["V"])
                    p.on("tensor", lambda t, mt=mt, ch=ch, pst=pst: t.matmul(
                        pst[:, 0:512],
                        wbuf[:, 2048 + mt * 128:2048 + (mt + 1) * 128],
                        fmt[:, bass.ts(ch, 512)],
                        start=True, stop=True), inc="T")
                    p.flush("tensor")
                    tv = p.counts["T"]
                    p.wait("vector", "T", tv)
                    p.on("vector", lambda v, mt=mt, ch=ch, pst=pst: v.tensor_tensor(
                        xt[mt][:, bass.ts(ch, 512)], xt[mt][:, bass.ts(ch, 512)],
                        pst[:, 0:512], ALU.add))
    p.flush("tensor")
    p.mark("vector")


def emit_l2(p, c, parts=("f1", "a", "f2")):
    emit_xallgather(p, c, to_seq=True)
    if "f1" in parts:
        emit_l2_ffn(p, c)
    if "a" in parts:
        emit_l2_attn(p, c)
    if "f2" in parts:
        emit_l2_ffn(p, c)
    emit_xallgather(p, c, to_seq=False)


def emit_l3_attn(p, c):
    nc, wb, xs, hs = c["nc"], c["wb"], c["xs"], c["hs"]
    af, ab, row = c["af"], c["ab"], c["row"]
    wbuf, smal, stat = c["wbuf"], c["smal"], c["stat"]
    ps_a, ps_b, ps_c, ps_d, ps_big = (c["ps_a"], c["ps_b"], c["ps_c"],
                                      c["ps_d"], c["ps_big"])
    xt, ht = c["xtiles"], c["hstiles"]
    arena = c["arena"]

    emit_ln(p, c, xt, ht)
    vln = p.counts["V"]

    def vtm(t):
        return arena[:, 16384 + t * 1024:16384 + (t + 1) * 1024]
    qb = lambda: arena[:, 32768:33792].rearrange("p (a b) -> p a b", a=8)
    PT = ab(17)[:, 0:640].rearrange("p (a b) -> p a b", a=5)
    Pb = ab(17)[:, 640:1280]
    m256 = ab(17)[:, 1280:1920]
    s_sb = af(9)[:, 0:640]
    av_sb = af(9)[:, 1024:2048]
    smal_bf = smal[:].bitcast(BF16)
    vrow = smal_bf[0:1, 1024:2048]

    p.wait("gpsimd", "V", vln)
    p.on("gpsimd", lambda g: g.iota(s_sb, pattern=[[1, 640]], base=0,
                                    channel_multiplier=-1,
                                    allow_small_or_imprecise_dtypes=True))
    gio = p.mark("gpsimd")
    p.wait("vector", "G", gio[1])
    def mkmask(dst, lo, hi, srcw):
        p.on("vector", lambda v: v.tensor_scalar(
            af(9)[:, 768:768 + srcw], s_sb[:, 0:srcw], float(lo), None,
            ALU.is_ge))
        p.on("vector", lambda v: v.tensor_scalar(
            af(9)[:, 1408:1408 + srcw], s_sb[:, 0:srcw], float(hi), None,
            ALU.is_le))
        p.on("vector", lambda v: v.tensor_tensor(
            dst, af(9)[:, 768:768 + srcw], af(9)[:, 1408:1408 + srcw],
            ALU.mult))
    mkmask(m256, 0, 511, 640)
    vmask = p.counts["V"]

    p.wait("sync", c["wb_ready_g"][0], c["wb_ready_g"][1])
    p.flush("tensor")
    p.wait("sync", "T", p.counts["T"])
    wv_ = wb("L3wv")
    tks = [p.dma("sync", wbuf[:, kt * 1024:(kt + 1) * 1024],
                 wv_[bass.ts(kt, 128), :]) for kt in range(8)]
    p.wait_dma("tensor", tks)
    p.wait("tensor", "V", vln)
    for t in range(16):
        for chh in range(2):
            pst = ps_a if (t * 2 + chh) % 2 == 0 else ps_b
            if (t, chh) != (0, 0):
                p.wait("tensor", "V", p.counts["V"])
            for ct in range(8):
                p.on("tensor", lambda t_, t=t, chh=chh, ct=ct, pst=pst:
                     t_.matmul(
                         pst[:, 0:512], ht[ct][:, bass.ts(t, 128)],
                         wbuf[:, ct * 1024 + chh * 512:ct * 1024 + (chh + 1) * 512],
                         start=(ct == 0), stop=(ct == 7)), inc="T")
            p.flush("tensor")
            tv = p.counts["T"]
            p.wait("vector", "T", tv)
            p.on("vector", lambda v, t=t, chh=chh, pst=pst: v.tensor_copy(
                vtm(t)[:, bass.ts(chh, 512)], pst[:, 0:512]))
    vvt = p.counts["V"]
    p.wait("tensor", "V", vvt)
    for chh in range(2):
        pst = ps_c if chh == 0 else ps_d
        for t in range(16):
            p.on("tensor", lambda t_, t=t, chh=chh, pst=pst: t_.matmul(
                pst[0:1, 0:512], c["onesb"], vtm(t)[:, bass.ts(chh, 512)],
                start=(t == 0), stop=(t == 15)), inc="T")
    p.flush("tensor")
    tvt = p.counts["T"]
    p.wait("vector", "T", tvt)
    p.on("vector", lambda v: v.tensor_copy(vrow[:, 0:512], ps_c[0:1, 0:512]))
    p.on("vector", lambda v: v.tensor_copy(vrow[:, 512:1024], ps_d[0:1, 0:512]))
    vvr = p.counts["V"]

    wk = wb("L3wk")
    wvb = wbuf[:, 0:6144].rearrange("p (b k f) -> p b k f", b=2, k=8)
    last_mm = {}
    p.flush("tensor")
    p.wait("sync", "T", p.counts["T"])
    groups = [(0, 3), (3, 3), (6, 2)]
    for gi, (m0, gn) in enumerate(groups):
        buf = gi % 2
        if gi >= 2:
            p.wait("sync", "T", last_mm[buf])
        tks = [p.dma("sync", wvb[:, buf, kt, 0:gn * 128],
                     wk[bass.ts(kt, 128), m0 * 128:(m0 + gn) * 128])
               for kt in range(8)]
        p.wait_dma("tensor", tks)
        for m3 in range(gn):
            mt = m0 + m3
            for ch in range(4):
                pst = ps_a if (m3 * 4 + ch) % 2 == 0 else ps_b
                p.wait("tensor", "V", p.counts["V"])
                for kt in range(8):
                    p.on("tensor", lambda t, buf=buf, kt=kt, m3=m3, ch=ch,
                         pst=pst: t.matmul(
                             pst[:, 0:512],
                             wvb[:, buf, kt, m3 * 128:(m3 + 1) * 128],
                             ht[kt][:, bass.ts(ch, 512)],
                             start=(kt == 0), stop=(kt == 7)), inc="T")
                p.flush("tensor")
                tv = p.counts["T"]
                p.wait("vector", "T", tv)
                p.on("vector", lambda v, mt=mt, ch=ch, pst=pst: v.tensor_copy(
                    ab(mt)[:, bass.ts(ch, 512)], pst[:, 0:512]))
        last_mm[buf] = p.counts["T"]
    vkf = p.counts["V"]
    p.wait("tensor", "V", vkf)

    wq_ = wb("L3wq")
    for it in range(TT):
        i0 = it * 128
        w0 = max(0, i0 - 256)
        w1 = min(2048, i0 + 384)
        w = w1 - w0
        p.flush("tensor")
        p.wait("sync", "T", p.counts["T"])
        tks = [p.dma("sync", wbuf[:, kt * 1024:(kt + 1) * 1024],
                     wq_[bass.ts(kt, 128), :]) for kt in range(8)]
        p.wait_dma("tensor", tks)
        p.wait("tensor", "V", p.counts["V"])
        p.wait("tensor", "S", p.counts["S"])
        for ft in range(8):
            pst = ps_c if ft % 2 == 0 else ps_d
            if ft >= 2:
                p.wait("tensor", "V", p.counts["V"])
            for ct in range(8):
                p.on("tensor", lambda t, ft=ft, ct=ct, i0=i0, pst=pst: t.matmul(
                    pst[:, 0:128],
                    wbuf[:, ct * 1024 + ft * 128:ct * 1024 + (ft + 1) * 128],
                    ht[ct][:, i0:i0 + 128],
                    start=(ct == 0), stop=(ct == 7)), inc="T")
            p.flush("tensor")
            tv = p.counts["T"]
            p.wait("vector", "T", tv)
            p.on("vector", lambda v, ft=ft, pst=pst: v.tensor_copy(
                qb()[:, ft, :], pst[:, 0:128]))
        vq = p.counts["V"]
        p.wait("tensor", "V", vq)
        nch = (w + 511) // 512
        for ct in range(8):
            for chw in range(nch):
                cw0 = chw * 512
                cw1 = min(w, cw0 + 512)
                p.on("tensor", lambda t, ct=ct, cw0=cw0, cw1=cw1, w0=w0: t.matmul(
                    ps_big[:, cw0:cw1], qb()[:, ct, :],
                    ab(ct)[:, w0 + cw0:w0 + cw1],
                    start=(ct == 0), stop=(ct == 7)), inc="T")
        p.flush("tensor")
        tsc = p.counts["T"]
        p.wait("vector", "T", tsc)
        if it <= 1:
            lo, hi, srcw = ((-256, 255, 384) if it == 0 else (-128, 383, 512))
            dst = ab(16)[:, 1024:1024 + srcw]
            p.wait("gpsimd", "V", p.counts["V"])
            p.on("gpsimd", lambda g, srcw=srcw: g.iota(
                af(9)[:, 768:768 + srcw], pattern=[[1, srcw]], base=0,
                channel_multiplier=-1,
                allow_small_or_imprecise_dtypes=True))
            gio2 = p.mark("gpsimd")
            p.wait("vector", "G", gio2[1])
            p.on("vector", lambda v, dst=dst, lo=lo, srcw=srcw: v.tensor_scalar(
                dst, af(9)[:, 768:768 + srcw], float(lo), None, ALU.is_ge))
            p.on("vector", lambda v, hi=hi, srcw=srcw: v.tensor_scalar(
                af(9)[:, 1536:1536 + srcw], af(9)[:, 768:768 + srcw],
                float(hi), None, ALU.is_le))
            p.on("vector", lambda v, dst=dst, srcw=srcw: v.tensor_tensor(
                dst, dst, af(9)[:, 1536:1536 + srcw], ALU.mult))
            msk = dst
        else:
            msk = m256[:, 0:w]
        p.on("vector", lambda v, w=w, msk=msk: v.scalar_tensor_tensor(
            s_sb[:, 0:w], ps_big[:, 0:w], 0.03125, msk, ALU.mult, ALU.mult))
        p.on("vector", lambda v, w=w: v.tensor_reduce(
            stat[:, 0:1], s_sb[:, 0:w], axis=AX.X, op=ALU.max))
        p.on("vector", lambda v: v.tensor_scalar(
            stat[:, 1:2], stat[:, 0:1], -1.0, None, ALU.mult))
        vs = p.counts["V"]
        p.wait("scalar", "V", vs)
        p.on("scalar", lambda s, w=w: s.activation(
            s_sb[:, 0:w], s_sb[:, 0:w], AF.Exp, bias=stat[:, 1:2]))
        p.on("scalar", lambda s: s.activation(
            stat[:, 2:3], stat[:, 1:2], AF.Exp))
        se = p.counts["S"]
        p.wait("vector", "S", se)
        p.on("vector", lambda v, w=w: v.tensor_scalar(
            s_sb[:, 0:w], s_sb[:, 0:w], stat[:, 2:3], None, ALU.subtract))
        p.on("vector", lambda v, w=w: v.tensor_reduce(
            stat[:, 3:4], s_sb[:, 0:w], axis=AX.X, op=ALU.add))
        p.on("vector", lambda v: v.scalar_tensor_tensor(
            stat[:, 3:4], stat[:, 2:3], 2048.0, stat[:, 3:4], ALU.mult,
            ALU.add))
        p.on("vector", lambda v: v.reciprocal(stat[:, 3:4], stat[:, 3:4]))
        p.on("vector", lambda v, w=w: v.tensor_scalar(
            Pb[:, 0:w], s_sb[:, 0:w], stat[:, 3:4], None, ALU.mult))
        p.on("vector", lambda v: v.tensor_tensor(
            stat[:, 4:5], stat[:, 2:3], stat[:, 3:4], ALU.mult))
        vP = p.counts["V"]
        p.wait("tensor", "V", vP)
        nwt = (w + 127) // 128
        for jt in range(nwt):
            pst = (ps_c if jt % 2 == 0 else ps_d)[:].bitcast(BF16)
            if jt >= 2:
                p.wait("tensor", "V", p.counts["V"])
            jw = min(128, w - jt * 128)
            p.on("tensor", lambda t, jt=jt, jw=jw, pst=pst: t.transpose(
                pst[0:jw, 0:128], Pb[:, jt * 128:jt * 128 + jw], c["idb"]),
                inc="T")
            p.flush("tensor")
            tv = p.counts["T"]
            p.wait("vector", "T", tv)
            p.on("vector", lambda v, jt=jt, jw=jw, pst=pst: v.tensor_copy(
                PT[:, jt, :][0:jw, :], pst[0:jw, 0:128]))
        p.wait("tensor", "V", p.counts["V"])
        p.on("tensor", lambda t: t.transpose(
            ps_c[0:1, 0:128], stat[:, 4:5], c["idf"]), inc="T")
        p.flush("tensor")
        tem = p.counts["T"]
        p.wait("vector", "T", tem)
        p.on("vector", lambda v: v.tensor_copy(
            smal_bf[0:1, 896:1024], ps_c[0:1, 0:128]))
        vtr = p.counts["V"]
        p.wait("tensor", "V", vtr)
        jt0 = w0 // 128
        for jt in range(nwt):
            jw = min(128, w - jt * 128)
            for chh in range(2):
                p.on("tensor", lambda t, jt=jt, jw=jw, chh=chh, jt0=jt0:
                     t.matmul(
                         ps_big[:, 1024 + chh * 512:1024 + (chh + 1) * 512],
                         PT[:, jt, :][0:jw, :],
                         vtm(jt0 + jt)[0:jw, bass.ts(chh, 512)],
                         start=(jt == 0), stop=False), inc="T")
        for chh in range(2):
            p.on("tensor", lambda t, chh=chh: t.matmul(
                ps_big[:, 1024 + chh * 512:1024 + (chh + 1) * 512],
                smal_bf[0:1, 896:1024], vrow[:, bass.ts(chh, 512)],
                start=False, stop=True), inc="T")
        p.flush("tensor")
        tav = p.counts["T"]
        p.wait("vector", "T", tav)
        p.on("vector", lambda v: v.tensor_copy(av_sb, ps_big[:, 1024:2048]))
        vav = p.counts["V"]
        p.wait("tensor", "V", vav)
        for dt in range(DT):
            pst = ps_c if dt % 2 == 0 else ps_d
            if dt >= 2:
                p.wait("tensor", "V", p.counts["V"])
            p.on("tensor", lambda t, dt=dt, pst=pst: t.transpose(
                pst[:, 0:128], av_sb[:, bass.ts(dt, 128)], c["idf"]), inc="T")
            p.flush("tensor")
            tv = p.counts["T"]
            p.wait("vector", "T", tv)
            p.on("vector", lambda v, dt=dt, it=it, pst=pst: v.tensor_tensor(
                xt[dt][:, bass.ts(it, 128)], xt[dt][:, bass.ts(it, 128)],
                pst[:, 0:128], ALU.add))
    p.flush("tensor")
    p.mark("vector")


def emit_l3_ffn(p, c):
    nc, wb, xs, hs = c["nc"], c["wb"], c["xs"], c["hs"]
    af, ab, row = c["af"], c["ab"], c["row"]
    wbuf, smal, stat, cstb = c["wbuf"], c["smal"], c["stat"], c["cstb"]
    ps_a, ps_b, ps_c, ps_d, ps_big = (c["ps_a"], c["ps_b"], c["ps_c"],
                                      c["ps_d"], c["ps_big"])
    xt, ht = c["xtiles"], c["hstiles"]
    io = c["io"]

    emit_ln(p, c, xt, ht)
    vln = p.counts["V"]
    for dt in range(DT):
        p.on("vector", lambda v, dt=dt: v.scalar_tensor_tensor(
            xt[dt], ht[dt], 0.5, xt[dt], ALU.mult, ALU.add))

    p.wait("sync", c["wb_ready_g"][0], c["wb_ready_g"][1])
    p.flush("tensor")
    p.wait("sync", "T", p.counts["T"])
    tk1 = [p.dma("sync", wbuf[:, kt * 128:(kt + 1) * 128],
                 wb("L3wp1")[bass.ts(kt, 128), :]) for kt in range(8)]
    tk2 = p.dma("sync", wbuf[:, 1024:1032], wb("L3wp2"))
    p.wait_dma("tensor", tk1)
    p.wait("tensor", "V", vln)
    p.wait("tensor", "S", p.counts["S"])
    for kt in range(8):
        p.on("tensor", lambda t, kt=kt: t.matmul(
            ps_c[:, 0:1], wbuf[:, kt * 128:(kt + 1) * 128],
            ht[kt][:, 2047:2048], start=(kt == 0), stop=(kt == 7)), inc="T")
    p.flush("tensor")
    th1 = p.counts["T"]
    p.wait("scalar", "T", th1)
    p.on("scalar", lambda s: s.activation(
        cstb[:, 130:131], ps_c[:, 0:1], AF.Gelu, bias=smal[:, 46:47]))
    sh1 = p.counts["S"]
    p.wait("tensor", "S", sh1)
    p.wait_dma("tensor", tk2)
    p.on("tensor", lambda t: t.matmul(
        ps_d[0:1, 0:8], cstb[:, 130:131], wbuf[:, 1024:1032],
        start=True, stop=True), inc="T")
    p.flush("tensor")
    tlg = p.counts["T"]
    p.wait("vector", "T", tlg)
    p.on("vector", lambda v: v.tensor_tensor(
        smal[0:1, 66:74], ps_d[0:1, 0:8], smal[0:1, 58:66], ALU.add))
    p.on("vector", lambda v: v.max(smal[0:1, 74:82], smal[0:1, 66:74]))
    p.on("vector", lambda v: v.max_index(c["idxt"][0:1, 0:8],
                                         smal[0:1, 74:82], smal[0:1, 66:74]))
    vsel = p.counts["V"]
    p.on("vector", lambda v: v.tensor_copy(smal[0:1, 90:91],
                                           c["idxt"][0:1, 0:1]))
    vself = p.counts["V"]
    p.wait("gpsimd", "V", vself)
    p.on("gpsimd", lambda g: g.iota(stat[:, 49:50], pattern=[[0, 1]], base=0,
                                    channel_multiplier=1,
                                    allow_small_or_imprecise_dtypes=True))
    giot = p.mark("gpsimd")
    p.wait("tensor", "V", vself)
    p.on("tensor", lambda t: t.matmul(ps_d[:, 4:5], c["onesrow"],
                                      smal[0:1, 90:91], start=True, stop=True),
         inc="T")
    p.flush("tensor")
    tsel = p.counts["T"]
    p.wait("vector", "T", tsel)
    p.wait("vector", "G", giot[1])
    p.on("vector", lambda v: v.tensor_copy(stat[:, 48:49], ps_d[:, 4:5]))
    for col, mul in ((2, 512.0), (3, 1024.0), (4, 128.0)):
        p.on("vector", lambda v, mul=mul: v.scalar_tensor_tensor(
            stat[:, 50:51], stat[:, 48:49], mul, stat[:, 49:50], ALU.mult,
            ALU.add))
        p.on("vector", lambda v, col=col: v.tensor_copy(
            c["gix"][:, col:col + 1], stat[:, 50:51]))
    vgix = p.counts["V"]
    p.wait("gpsimd", "V", vgix)
    tkb = [p.dma2("gpsimd", lambda eng: eng.indirect_dma_start(
               out=smal[:, 66:82], out_offset=None,
               in_=io["b1"].rearrange("a p t -> (a p) t"),
               in_offset=bass.IndirectOffsetOnAxis(c["gix"][:, 4:5], 0))),
           p.dma2("gpsimd", lambda eng: eng.indirect_dma_start(
               out=smal[:, 82:90], out_offset=None,
               in_=io["b2"].rearrange("a p t -> (a p) t"),
               in_offset=bass.IndirectOffsetOnAxis(c["gix"][:, 4:5], 0)))]
    p.wait_dma("scalar", tkb)
    p.wait_dma("vector", tkb)
    p.on("vector", lambda v: v.tensor_scalar(
        stat[:, 40:48], smal[:, 47:55], 0.15, None, ALU.mult))
    vbo = p.counts["V"]

    wsl = wbuf.rearrange("p (b f) -> p b f", b=4)[:, 0:3, :]
    grp = [[ps_big[:, bass.ts(i, 512)] for i in range(4)],
           [ps_a[:, 0:512], ps_b[:, 0:512], ps_c[:, 0:512], ps_d[:, 0:512]]]
    harena = lambda mt: c["arena"][:, mt * 2048:(mt + 1) * 2048]
    slot_mark = [0, 0, 0]
    pgfree = [("V", p.counts["V"]), ("V", p.counts["V"])]
    p.wait("tensor", "S", p.counts["S"])
    w1flat = [c["wbrows"]("L3w1a", 2048), c["wbrows"]("L3w1b", 2048)]
    w2flat = [c["wbrows"]("L3w2a", 1024), c["wbrows"]("L3w2b", 1024)]

    def gate(q, buf):
        if slot_mark[buf] > 0:
            p.wait(q, "T", slot_mark[buf])

    def load_w1(w):
        buf = w % 3
        gate("gpsimd", buf)
        return [p.dma2("gpsimd",
                       lambda eng, kt=kt, buf=buf, w=w: eng.indirect_dma_start(
                           out=wsl[:, buf, kt * 256:(kt + 1) * 256],
                           out_offset=None, in_=w1flat[kt // 4][0],
                           element_offset=(w1flat[kt // 4][1]
                                           + (kt % 4) * 128 * 2048 + w * 256),
                           in_offset=bass.IndirectOffsetOnAxis(
                               c["gix"][:, 2:3], 0)))
                for kt in range(8)]

    def load_w2(mt):
        buf = (8 + mt) % 3
        gate("gpsimd", buf)
        return [p.dma2("gpsimd",
                       lambda eng, kt=kt, buf=buf, mt=mt:
                       eng.indirect_dma_start(
                           out=wsl[:, buf, kt * 128:(kt + 1) * 128],
                           out_offset=None, in_=w2flat[kt // 8][0],
                           element_offset=(w2flat[kt // 8][1]
                                           + (kt % 8) * 128 * 1024 + mt * 128),
                           in_offset=bass.IndirectOffsetOnAxis(
                               c["gix"][:, 3:4], 0)))
                for kt in range(16)]

    def load_wo(mt):
        buf = (16 + mt) % 3
        gate("sync", buf)
        return [p.dma("sync", wsl[:, buf, 0:1024],
                      c["wb"]("L3wot")[mt].rearrange("p a c -> p (a c)"))]

    tks = {w: load_w1(w) for w in range(3)}
    for w in range(8):
        buf = w % 3
        p.wait_dma("tensor", tks.pop(w))
        for m2 in range(2):
            mt = 2 * w + m2
            g = grp[mt % 2]
            eng_, cnt = pgfree[mt % 2]
            if cnt > 0:
                p.wait("tensor", eng_, cnt)
            for nch in range(4):
                for kt in range(8):
                    p.on("tensor", lambda t, buf=buf, kt=kt, m2=m2, nch=nch,
                         g=g: t.matmul(
                             g[nch], wsl[:, buf, kt * 256 + m2 * 128:
                                         kt * 256 + (m2 + 1) * 128],
                             ht[kt][:, bass.ts(nch, 512)],
                             start=(kt == 0), stop=(kt == 7)), inc="T")
            tmm = p.counts["T"]
            p.wait("scalar", "T", tmm)
            for nch in range(4):
                p.on("scalar", lambda s, mt=mt, nch=nch, g=g: s.activation(
                    harena(mt)[:, bass.ts(nch, 512)], g[nch],
                    AF.Gelu, bias=smal[:, 66 + mt:67 + mt]))
            pgfree[mt % 2] = ("S", p.counts["S"])
        slot_mark[buf] = p.counts["T"]
        if w + 3 < 8:
            tks[w + 3] = load_w1(w + 3)
    t_p1end = p.counts["T"]
    tks = {mt: load_w2(mt) for mt in range(3)}
    p.wait("tensor", "S", p.counts["S"])
    first_evac = True
    for mt in range(8):
        buf = (8 + mt) % 3
        p.wait_dma("tensor", tks.pop(mt))
        g = grp[mt % 2]
        eng_, cnt = pgfree[mt % 2]
        if cnt > 0:
            p.wait("tensor", eng_, cnt)
        for nch in range(4):
            for kt in range(16):
                p.on("tensor", lambda t, buf=buf, kt=kt, nch=nch, g=g:
                     t.matmul(
                         g[nch], wsl[:, buf, kt * 128:(kt + 1) * 128],
                         c["arena"][:, kt * 2048 + nch * 512:
                                    kt * 2048 + (nch + 1) * 512],
                         start=(kt == 0), stop=(kt == 15)), inc="T")
        tmm = p.counts["T"]
        p.wait("vector", "T", tmm)
        if first_evac:
            p.wait("vector", "T", t_p1end)
            first_evac = False
        for nch in range(4):
            p.on("vector", lambda v, mt=mt, nch=nch, g=g: v.tensor_scalar(
                ht[mt][:, bass.ts(nch, 512)], g[nch],
                smal[:, 82 + mt:83 + mt], None, ALU.add))
        pgfree[mt % 2] = ("V", p.counts["V"])
        slot_mark[buf] = p.counts["T"]
        if mt + 3 < 8:
            tks[mt + 3] = load_w2(mt + 3)
    vre = p.counts["V"]
    tks = {mt: load_wo(mt) for mt in range(3)}
    p.wait("tensor", "V", vre)
    for mt in range(8):
        buf = (16 + mt) % 3
        p.wait_dma("tensor", tks.pop(mt))
        g = grp[mt % 2]
        eng_, cnt = pgfree[mt % 2]
        if cnt > 0:
            p.wait("tensor", eng_, cnt)
        for nch in range(4):
            for kt in range(8):
                p.on("tensor", lambda t, buf=buf, kt=kt, nch=nch, g=g:
                     t.matmul(
                         g[nch], wsl[:, buf, kt * 128:(kt + 1) * 128],
                         ht[kt][:, bass.ts(nch, 512)],
                         start=(kt == 0), stop=(kt == 7)), inc="T")
        tmm = p.counts["T"]
        p.wait("vector", "T", tmm)
        for nch in range(4):
            p.on("vector", lambda v, mt=mt, nch=nch, g=g:
                 v.scalar_tensor_tensor(
                     xt[mt][:, bass.ts(nch, 512)], g[nch], 0.15,
                     xt[mt][:, bass.ts(nch, 512)], ALU.mult, ALU.add))
            p.on("vector", lambda v, mt=mt, nch=nch: v.tensor_scalar(
                xt[mt][:, bass.ts(nch, 512)], xt[mt][:, bass.ts(nch, 512)],
                stat[:, 40 + mt:41 + mt], None, ALU.add))
        pgfree[mt % 2] = ("V", p.counts["V"])
        slot_mark[buf] = p.counts["T"]
        if mt + 3 < 8:
            tks[mt + 3] = load_wo(mt + 3)
    p.flush("tensor")
    p.mark("vector")


def emit_l3(p, c, parts=("f1", "a", "f2")):
    if "f1" in parts:
        emit_l3_ffn(p, c)
    if "a" in parts:
        emit_l3_attn(p, c)
    if "f2" in parts:
        emit_l3_ffn(p, c)


def make_in_maps(inputs, n_cores=NCORES):
    x = np.ascontiguousarray(inputs["L0_attn_x"], dtype=np.float32)
    img = host_pack_wb(inputs)
    fvec = host_pack_fvec(inputs)
    in_maps = []
    for b in range(n_cores):
        m = {
            "x_in": np.ascontiguousarray(x[b].T),
            "conv_src": (img.reshape(-1) if n_cores == 1 else
                         np.concatenate([img[0, b * WB_HSH:(b + 1) * WB_HSH],
                                         img[1, b * WB_HSH:(b + 1) * WB_HSH]])),
            "fvec": fvec,
            "cid": np.array([b], dtype=np.int32),
            "gidx": np.stack([b * 8192 + np.arange(128, dtype=np.uint32),
                              np.arange(128, dtype=np.uint32) * 8 + b]),
            "b1": np.ascontiguousarray(
                np.asarray(inputs["L3_ffn_b1"], np.float32)
                .reshape(8, 16, 128).transpose(0, 2, 1)),
            "b2": np.ascontiguousarray(
                np.asarray(inputs["L3_ffn_b2"], np.float32)
                .reshape(8, 8, 128).transpose(0, 2, 1)),
        }
        in_maps.append(m)
    return in_maps


def kernel(**inputs):
    nc = build_nc()
    in_maps = make_in_maps(inputs)
    res = run_bass_kernel_spmd(nc, in_maps, core_ids=list(range(NCORES)))
    return np.stack([res.results[b]["out"].T for b in range(NCORES)], axis=0)
